# version 01
# baseline (speedup 1.0000x reference)
import numpy as np
import ml_dtypes
import concourse.bass as bass
import concourse.mybir as mybir
from concourse.bass_utils import run_bass_kernel_spmd

F32 = mybir.dt.float32
BF16 = mybir.dt.bfloat16
AF = mybir.ActivationFunctionType
ALU = mybir.AluOpType
AX = mybir.AxisListType.X
BF = ml_dtypes.bfloat16

S, D, DI, CONVD, PROJ = 2048, 4096, 8192, 10240, 18560
NH, PH, NG, NN, CH = 128, 64, 8, 128, 128
NCH, REP = S // CH, NH // NG
EPS = 1e-5
NOB, NKC, NIC = PROJ // 128, D // 128, DI // 128
NMB = D // 128
OB_HBC0 = DI // 128
ENGS = ("tensor", "vector", "scalar", "gpsimd", "sync")


class Rec:

    def __init__(self, nc):
        self.nc = nc
        self.ev = {e: [] for e in ENGS}
        self.cnt = {}
        self.handles = {}

    ALIAS = {}

    @classmethod
    def canon(cls, name):
        if name == "f5xld":
            return name
        for pre, sub in (("f5", "f1"), ("fw2", "fw1")):
            if name.startswith(pre):
                return sub + name[len(pre):]
        return name

    def _h(self, name):
        name = self.canon(name)
        if name not in self.cnt:
            self.cnt[name] = 0
            self.handles[name] = self.nc.alloc_semaphore(name)
        return self.handles[name]

    def val(self, name):
        return self.cnt.get(self.canon(name), 0)

    def op(self, eng, fn, waits=(), inc=None, inc_n=1):
        wl = []
        for w in waits:
            name, v = w if isinstance(w, tuple) else (w, self.val(w))
            if v > 0:
                wl.append((self._h(name), v))
        ih = None
        if inc is not None:
            inc = self.canon(inc)
            self._h(inc)
            self.cnt[inc] += inc_n
            ih = (self.handles[inc], inc_n)

        def emit(e, fn=fn, wl=wl, ih=ih):
            for h, v in wl:
                e.wait_ge(h, v)
            ins = fn(e)
            if ih is not None:
                ins.then_inc(ih[0], ih[1])

        self.ev[eng].append(emit)
        return self.cnt[inc] if inc else None

    STRICT = {"wpc", "w2c", "wtp", "wblk", "w2blk", "p2bwl", "p2bst",
              "p4yl", "p4wl", "p4rl", "p4st", "f1ost"}

    def dma(self, eng, out, in_, waits=(), inc=None, **kw):
        waits = list(waits)
        if inc is not None and self.canon(inc) in self.STRICT:
            waits.append((self.canon(inc), self.val(inc)))
        return self.op(eng, lambda e: e.dma_start(out, in_, **kw), waits, inc, 16)

    def barrier_wait(self, eng, waits):
        wl = [(w if isinstance(w, tuple) else (w, self.val(w))) for w in waits]
        wl = [(self._h(n), v) for n, v in wl if v > 0]

        def emit(e, wl=wl):
            for h, v in wl:
                e.wait_ge(h, v)
        self.ev[eng].append(emit)


def v3(ap2d, k, r):
    return ap2d.rearrange("p (k r) -> p k r", r=r)


def colb(col_ap, k, r):
    return col_ap.broadcast_to(list(col_ap.shape) + [r])


def rowb(row_ap, k, r):
    p0, rr = row_ap.shape
    return row_ap.unsqueeze(1).broadcast_to([p0, k, rr])


class Carve:

    def __init__(self, t, total):
        self.t = t
        self.total = total
        self.off = 0

    def take(self, n):
        assert self.off + n <= self.total, (self.off, n, self.total)
        ap = self.t[:, self.off:self.off + n]
        self.off += n
        return ap

    def reset(self, off=0):
        self.off = off


def build_program(dbg=None, phases=("p2b", "p3", "p35", "p4", "p5"), detect_races=True):
    PHASES = set(phases)
    dbg = dbg or {}
    nc = bass.Bass("TRN2", target_bir_lowering=False,
                   detect_race_conditions=detect_races)

    xT = nc.dram_tensor("xT", [D, S], F32, kind="ExternalInput")
    wp = nc.dram_tensor("wp", [PROJ, D], F32, kind="ExternalInput")
    w2 = nc.dram_tensor("w2", [D, DI], F32, kind="ExternalInput")
    ffa = nc.dram_tensor("ffa", [128, D], F32, kind="ExternalInput")
    ffb = nc.dram_tensor("ffb", [NG, D, 16], F32, kind="ExternalInput")
    wrr = nc.dram_tensor("wrr", [NG, D], F32, kind="ExternalInput")
    brow = nc.dram_tensor("brow", [1, NG], F32, kind="ExternalInput")
    normw = nc.dram_tensor("normw", [NKC, 128], F32, kind="ExternalInput")
    convw = nc.dram_tensor("convw", [CONVD, 4], F32, kind="ExternalInput")
    convb = nc.dram_tensor("convb", [CONVD // 128, 128], F32, kind="ExternalInput")
    dtb = nc.dram_tensor("dtb", [1, NH], F32, kind="ExternalInput")
    alog = nc.dram_tensor("alog", [1, NH], F32, kind="ExternalInput")
    dpr = nc.dram_tensor("dpr", [1, NH], F32, kind="ExternalInput")
    gnw = nc.dram_tensor("gnw", [NIC, 128], F32, kind="ExternalInput")
    out_d = nc.dram_tensor("out", [D, S], F32, kind="ExternalOutput")

    dbg_t = {}
    for k, shape, dt_ in (("proj", [PROJ, S], BF16), ("h", [D, S], BF16),
                          ("x1", [D, S], BF16), ("yt", [S, DI], BF16),
                          ("ygn", [S, DI], BF16), ("r2", [D, S], BF16),
                          ("x2", [D, S], BF16)):
        if dbg.get(k):
            dbg_t[k] = nc.dram_tensor("dbg_" + k, shape, dt_, kind="ExternalOutput")
    if dbg.get("p3c0"):
        dbg_t["p3c0"] = nc.dram_tensor("dbg_p3c0", [128, 36096], BF16, kind="ExternalOutput")
        dbg_t["p3f"] = nc.dram_tensor("dbg_p3f", [128, 512], F32, kind="ExternalOutput")

    wp_bf = nc.dram_tensor("wp_bf", [PROJ, D], BF16)
    w2_bf = nc.dram_tensor("w2_bf", [D, DI], BF16)
    wpt_blk = nc.dram_tensor("wpt_blk", [NOB, 128, NKC, 128], BF16)
    w2t_blk = nc.dram_tensor("w2t_blk", [NMB, 128, NIC, 128], BF16)
    a_bfd = nc.dram_tensor("a_bfd", [128, D], BF16)
    b_bfd = nc.dram_tensor("b_bfd", [D, 128], BF16)
    wr_bfd = nc.dram_tensor("wr_bfd", [NG, D], BF16)
    proj_d = nc.dram_tensor("proj_d", [PROJ, S], BF16)
    r2_d = nc.dram_tensor("r2_d", [D, S], BF16)
    yt_d = nc.dram_tensor("yt_d", [S, DI], BF16)
    ygn_d = nc.dram_tensor("ygn_d", [S, DI], BF16)
    x2_d = nc.dram_tensor("x2_d", [D, S], BF16)

    c_id_bf = nc.inline_tensor(np.eye(128, dtype=BF), "c_id_bf")
    c_id_f32 = nc.inline_tensor(np.eye(128, dtype=np.float32), "c_id_f32")
    c_ones1_bf = nc.inline_tensor(np.ones((1, 128), dtype=BF), "c_ones1_bf")
    c_ones1_f = nc.inline_tensor(np.ones((1, 128), dtype=np.float32), "c_ones1_f")
    c_onescol_bf = nc.inline_tensor(np.ones((128, 1), dtype=BF), "c_onescol_bf")
    c_ones8_bf = nc.inline_tensor(np.ones((8, 1), dtype=BF), "c_ones8_bf")
    c_ones8_f = nc.inline_tensor(np.ones((8, 1), dtype=np.float32), "c_ones8_f")
    c_l1_f32 = nc.inline_tensor(np.triu(np.ones((128, 128), dtype=np.float32)), "c_l1_f32")
    c_u_bf = nc.inline_tensor(np.triu(np.ones((128, 128))).astype(BF), "c_u_bf")
    c_irep_bf = nc.inline_tensor(np.tile(np.eye(128), (1, 16)).astype(BF), "c_irep_bf")
    e8 = np.zeros((8, 128))
    for hh in range(8):
        e8[hh, hh * 16:(hh + 1) * 16] = 1
    c_e8_bf = nc.inline_tensor(e8.astype(BF), "c_e8_bf")
    e127 = np.zeros((128, 1), dtype=np.float32)
    e127[127, 0] = 1.0
    c_e127_f = nc.inline_tensor(e127, "c_e127_f")


    SBA = nc.alloc_sbuf_tensor
    slab1 = SBA("slab1", [128, 65536], BF16)
    slab2 = SBA("slab2", [128, 16384], BF16)
    slab3 = SBA("slab3", [128, 8448], BF16)
    wtmp0 = SBA("wtmp0", [128, 2048], BF16)
    wtmp1 = SBA("wtmp1", [128, 2048], BF16)
    wbuf0 = SBA("wbuf0", [128, 4096], BF16)
    r_irep = SBA("r_irep", [128, 2048], BF16)
    r_u = SBA("r_u", [128, 128], BF16)
    r_idbf = SBA("r_idbf", [128, 128], BF16)
    r_idf = SBA("r_idf", [128, 128], F32)
    r_e8 = SBA("r_e8", [8, 128], BF16)
    r_ones1b = SBA("r_ones1b", [1, 128], BF16)
    r_ones1f = SBA("r_ones1f", [1, 128], F32)
    r_onescolb = SBA("r_onescolb", [128, 1], BF16)
    r_ones8b = SBA("r_ones8b", [8, 1], BF16)
    r_ones8f = SBA("r_ones8f", [8, 1], F32)
    r_brcol = SBA("r_brcol", [8, 1], F32)
    r_e127 = SBA("r_e127", [128, 1], F32)
    r_l1 = SBA("r_l1", [128, 128], F32)
    r_convw = SBA("r_convw", [128, 320], F32)
    r_convb = SBA("r_convb", [128, 80], F32)
    r_normwT = SBA("r_normwT", [128, 32], F32)
    r_gnwT = SBA("r_gnwT", [128, 64], F32)
    r_abc = SBA("r_abc", [128, 128], F32)
    r_biasbc = SBA("r_biasbc", [128, 128], F32)
    r_dpbc = SBA("r_dpbc", [128, 128], BF16)
    f32s = SBA("f32s", [128, 1024], F32)
    f32b = SBA("f32b", [128, 256], F32)
    rem = nc.sbuf_bytes_remaining
    assert rem >= 0, rem

    wbuf1 = slab2[:, 10368:14464]
    ps = nc.alloc_psum_tensor("ps", [128, 4096], F32)

    def psbf(c0, n):
        return ps[:, c0:c0 + n // 2].bitcast(BF16)

    p = Rec(nc)
    dodbg = {k: (dbg_t[k] if k in dbg_t else None) for k in
             ("proj", "h", "x1", "yt", "ygn", "r2", "x2")}

    for dst, src in ((r_irep, c_irep_bf), (r_u, c_u_bf), (r_idbf, c_id_bf),
                     (r_idf, c_id_f32), (r_e8, c_e8_bf), (r_ones1b, c_ones1_bf),
                     (r_ones1f, c_ones1_f), (r_onescolb, c_onescol_bf),
                     (r_ones8b, c_ones8_bf), (r_ones8f, c_ones8_f), (r_e127, c_e127_f), (r_l1, c_l1_f32)):
        p.dma("gpsimd", dst[:, :], src.ap()[:, :], inc="cld")
    for i in range(8):
        src = convw.ap().rearrange("(c p) k -> p c k", p=128)[:, i * 10:(i + 1) * 10, :]
        p.dma("gpsimd", v3(r_convw[:, i * 40:(i + 1) * 40], 10, 4), src, inc="cld")
    p.dma("gpsimd", f32s[0:1, 0:128], dtb.ap()[:, :], inc="cld")
    p.dma("gpsimd", f32s[0:1, 128:256], alog.ap()[:, :], inc="cld")
    p.dma("gpsimd", f32s[0:1, 256:384], dpr.ap()[:, :], inc="cld")
    p.dma("gpsimd", f32s[0:1, 384:392], brow.ap()[:, :], inc="cld")
    p.dma("gpsimd", f32s[0:32, 392:520], normw.ap()[:, :], inc="cld")
    p.dma("gpsimd", f32s[0:64, 520:648], gnw.ap()[:, :], inc="cld")
    p.dma("gpsimd", f32s[0:80, 648:776], convb.ap()[:, :], inc="cld")
    p.dma("gpsimd", a_bfd.ap()[:, :], ffa.ap()[:, :], inc="ffnbf")
    for hh in range(NG):
        p.dma("gpsimd", b_bfd.ap()[:, hh * 16:(hh + 1) * 16], ffb.ap()[hh], inc="ffnbf")
    p.dma("gpsimd", wr_bfd.ap()[:, :], wrr.ap()[:, :], inc="ffnbf")
    CLD = p.val("cld")
    FFNBF = p.val("ffnbf")

    p.op("scalar", lambda s: s.activation(f32s[0:1, 776:904], f32s[0:1, 128:256], AF.Exp),
         waits=[("cld", CLD)], inc="p0a")
    p.op("vector", lambda v: v.tensor_scalar_mul(f32s[0:1, 776:904], f32s[0:1, 776:904], -1.0),
         waits=[("p0a", p.val("p0a"))], inc="p0a")
    P0A = p.val("p0a")

    def small_tr(src_ap, dst_ap, ncol):
        p.op("tensor", lambda t: t.transpose(ps[:, 0:ncol], src_ap, r_idf[0:ncol, 0:ncol]),
             waits=[("cld", CLD), ("p0ev", p.val("p0ev"))], inc="p0tr")
        p.op("vector", lambda v: v.tensor_copy(dst_ap, ps[:, 0:ncol]),
             waits=[("p0tr", p.val("p0tr"))], inc="p0ev")

    small_tr(f32s[0:32, 392:520], r_normwT[:, :], 32)
    small_tr(f32s[0:64, 520:648], r_gnwT[:, :], 64)
    small_tr(f32s[0:80, 648:776], r_convb[:, :], 80)

    def row_bcast(row_ap, dst_ap, n, extra=()):
        p.op("tensor", lambda t: t.matmul(ps[:, 512:512 + n], r_ones1f[0:1, 0:128],
                                          row_ap, start=True, stop=True),
             waits=[("cld", CLD), ("p0ev", p.val("p0ev"))] + list(extra), inc="p0tr")
        p.op("vector", lambda v: v.tensor_copy(dst_ap, ps[:, 512:512 + n]),
             waits=[("p0tr", p.val("p0tr"))], inc="p0ev")

    row_bcast(f32s[0:1, 0:128], r_biasbc[:, :], 128)
    row_bcast(f32s[0:1, 776:904], r_abc[:, :], 128, extra=[("p0a", P0A)])
    row_bcast(f32s[0:1, 256:384], r_dpbc[:, :], 128)
    p.op("tensor", lambda t: t.transpose(ps[0:8, 0:1], f32s[0:1, 384:392], r_idf[0:1, 0:1]),
         waits=[("cld", CLD), ("p0ev", p.val("p0ev"))], inc="p0tr")
    p.op("vector", lambda v: v.tensor_copy(r_brcol[:, :], ps[0:8, 0:1]),
         waits=[("p0tr", p.val("p0tr"))], inc="p0ev")

    at_t = slab3[:, 0:4096]
    bt_t = slab3[:, 4096:8192]
    wrt_t = slab3[:, 8192:8448]

    def build_ffn_tiles(tag):
        p.dma("sync", wtmp0[:, 0:2048], a_bfd.ap()[:, 0:2048], waits=[("ffnbf", FFNBF)], inc=tag + "ld")
        p.dma("sync", wtmp1[:, 0:2048], a_bfd.ap()[:, 2048:4096], waits=[("ffnbf", FFNBF)], inc=tag + "ld")
        p.dma("sync", v3(wbuf1[:, 0:4096], 32, 128),
              b_bfd.ap().rearrange("(c p) q -> p c q", p=128), waits=[("ffnbf", FFNBF)], inc=tag + "ld")
        ldv = p.val(tag + "ld")
        for kc in range(32):
            src = (wtmp0 if kc < 16 else wtmp1)[:, (kc % 16) * 128:(kc % 16) * 128 + 128]
            bank = 1024 + (kc % 2) * 512
            p.op("tensor", lambda t, src=src, bank=bank: t.transpose(psbf(bank, 256)[:, 0:128], src, r_idbf[:, :]),
                 waits=[(tag + "ld", ldv), (tag + "ev", max(0, p.val(tag + "ev") - 1))], inc=tag + "tr")
            p.op("vector", lambda v, kc=kc, bank=bank: v.tensor_copy(
                at_t[:, kc * 128:(kc + 1) * 128], psbf(bank, 256)[:, 0:128]),
                waits=[(tag + "tr", p.val(tag + "tr"))], inc=tag + "ev")
        for oc in range(32):
            src = wbuf1[:, oc * 128:(oc + 1) * 128]
            bank = 1024 + (oc % 2) * 512
            p.op("tensor", lambda t, src=src, bank=bank: t.transpose(psbf(bank, 256)[:, 0:128], src, r_idbf[:, :]),
                 waits=[(tag + "ld", ldv), (tag + "ev", max(0, p.val(tag + "ev") - 1))], inc=tag + "tr")
            p.op("vector", lambda v, oc=oc, bank=bank: v.tensor_copy(
                bt_t[:, oc * 128:(oc + 1) * 128], psbf(bank, 256)[:, 0:128]),
                waits=[(tag + "tr", p.val(tag + "tr"))], inc=tag + "ev")
        p.dma("sync", wtmp0[0:8, 0:2048], wr_bfd.ap()[:, 0:2048],
              waits=[(tag + "ev", p.val(tag + "ev"))], inc=tag + "ld")
        p.dma("sync", wtmp1[0:8, 0:2048], wr_bfd.ap()[:, 2048:4096],
              waits=[(tag + "ev", p.val(tag + "ev"))], inc=tag + "ld")
        ldv = p.val(tag + "ld")
        for kc in range(32):
            src = (wtmp0 if kc < 16 else wtmp1)[0:8, (kc % 16) * 128:(kc % 16) * 128 + 128]
            bank = 1024 + (kc % 2) * 512
            p.op("tensor", lambda t, src=src, bank=bank: t.transpose(psbf(bank, 16)[:, 0:8], src, r_idbf[0:8, 0:8]),
                 waits=[(tag + "ld", ldv), (tag + "ev", max(0, p.val(tag + "ev") - 1))], inc=tag + "tr")
            p.op("vector", lambda v, kc=kc, bank=bank: v.tensor_copy(
                wrt_t[:, kc * 8:(kc + 1) * 8], psbf(bank, 16)[:, 0:8]),
                waits=[(tag + "tr", p.val(tag + "tr"))], inc=tag + "ev")
        return p.val(tag + "ev")

    FFNW1 = build_ffn_tiles("fw1")

    wp_spans = [(i * 2048, 2048) for i in range(9)] + [(18432, 128)]
    wp_cast_at = {}
    for si, (o0, olen) in enumerate(wp_spans):
        p.dma("gpsimd", wp_bf.ap()[o0:o0 + olen, :], wp.ap()[o0:o0 + olen, :], inc="wpc")
        wp_cast_at[si] = p.val("wpc")
    for i, m0 in enumerate((0, 2048)):
        p.dma("gpsimd", w2_bf.ap()[m0:m0 + 2048, :], w2.ap()[m0:m0 + 2048, :], inc="w2c")

    blk_ready = {}
    span_order = [4, 5, 6, 7, 8, 9, 0, 1, 2, 3]
    wtp_cnt = 0
    for si in span_order:
        o0, olen = wp_spans[si]
        nob_span = olen // 128
        for kc in range(NKC):
            buf = (wtmp0 if wtp_cnt % 2 == 0 else wtmp1)[:, 0:olen]
            wtp_cnt += 1
            p.dma("sync", buf, wp_bf.ap()[o0:o0 + olen, kc * 128:(kc + 1) * 128],
                  waits=[("wpc", wp_cast_at[si]), ("wblk", p.val("wblk")),
                         ("fw1ev", FFNW1)],
                  inc="wtp", transpose=True)
            tpv = p.val("wtp")
            dst = wpt_blk.ap()[o0 // 128:o0 // 128 + nob_span, :, kc, :].rearrange("b d o -> d b o")
            p.dma("sync", dst, v3(buf, nob_span, 128), waits=[("wtp", tpv)], inc="wblk")
        for ob in range(o0 // 128, o0 // 128 + nob_span):
            blk_ready[ob] = p.val("wblk")
    w2blk_ready = {}
    for si, m0 in enumerate((0, 2048)):
        for kc in range(NIC):
            buf = wtmp0[:, 0:2048] if wtp_cnt % 2 == 0 else wtmp1[:, 0:2048]
            wtp_cnt += 1
            p.dma("sync", buf, w2_bf.ap()[m0:m0 + 2048, kc * 128:(kc + 1) * 128],
                  waits=[("w2c", 16 * (si + 1)), ("w2gn", max(0, p.val("w2gn") - 1)),
                         ("wblk", p.val("wblk")), ("w2blk", p.val("w2blk"))],
                  inc="wtp", transpose=True)
            tpv = p.val("wtp")
            p.op("vector", lambda v, buf=buf, kc=kc: v.tensor_scalar_mul(buf, buf, r_gnwT[:, kc:kc + 1]),
                 waits=[("wtp", tpv), ("p0ev", p.val("p0ev"))], inc="w2gn")
            gv = p.val("w2gn")
            dst = w2t_blk.ap()[m0 // 128:m0 // 128 + 16, :, kc, :].rearrange("b d o -> d b o")
            p.dma("sync", dst, v3(buf, 16, 128), waits=[("w2gn", gv)], inc="w2blk")
        for mb in range(m0 // 128, m0 // 128 + 16):
            w2blk_ready[mb] = p.val("w2blk")

    h_t = slab1[:, :]
    TT = 256
    NT1 = S // TT
    t_x = slab2[:, 0:8192]
    t_ln = slab2[:, 8192:16384]

    R_SUM, R_SQ, R_M, R_RS, R_T1, R_S2 = 0, 256, 512, 768, 904, 904

    def ln_stats(tag, src_tile, xv, guard):
        p.op("scalar", lambda s: s.activation(t_ln[:, :], src_tile, AF.Square),
             waits=[(tag + "xld", xv)] + list(guard), inc=tag + "sq")
        sqv = p.val(tag + "sq")
        for kc in range(NKC):
            p.op("tensor", lambda t, kc=kc: t.matmul(
                ps[0:1, 2048:2048 + TT], r_onescolb[:, :], src_tile[:, kc * TT:(kc + 1) * TT],
                start=(kc == 0), stop=(kc == NKC - 1)),
                waits=([(tag + "xld", xv)] + list(guard)) if kc == 0 else (),
                inc=None if kc < NKC - 1 else tag + "st")
        for kc in range(NKC):
            p.op("tensor", lambda t, kc=kc: t.matmul(
                ps[0:1, 2560:2560 + TT], r_onescolb[:, :], t_ln[:, kc * TT:(kc + 1) * TT],
                start=(kc == 0), stop=(kc == NKC - 1)),
                waits=[(tag + "sq", sqv)] if kc == 0 else (),
                inc=None if kc < NKC - 1 else tag + "st")
        stv = p.val(tag + "st")
        r = f32s[0:1, :]
        p.op("vector", lambda v: v.tensor_scalar_mul(r[:, R_M:R_M + TT], ps[0:1, 2048:2048 + TT], 1.0 / D),
             waits=[(tag + "st", stv)], inc=tag + "r")
        p.op("vector", lambda v: v.tensor_scalar_mul(r[:, R_SQ:R_SQ + TT], ps[0:1, 2560:2560 + TT], 1.0 / D),
             inc=tag + "r")
        p.op("vector", lambda v: v.tensor_tensor(r[:, R_SUM:R_SUM + TT], r[:, R_M:R_M + TT], r[:, R_M:R_M + TT], ALU.mult),
             waits=[(tag + "r", p.val(tag + "r"))], inc=tag + "r")
        p.op("vector", lambda v: v.tensor_tensor(r[:, R_SQ:R_SQ + TT], r[:, R_SQ:R_SQ + TT], r[:, R_SUM:R_SUM + TT], ALU.subtract),
             waits=[(tag + "r", p.val(tag + "r"))], inc=tag + "r")
        p.op("vector", lambda v: v.tensor_scalar_add(r[:, R_RS:R_RS + TT], r[:, R_SQ:R_SQ + TT], EPS),
             waits=[(tag + "r", p.val(tag + "r"))], inc=tag + "r")
        p.op("scalar", lambda s: s.activation(r[:, R_SM:R_SM + TT], r[:, R_RS:R_RS + TT], AF.Ln),
             waits=[(tag + "r", p.val(tag + "r"))], inc=tag + "r")
        p.op("scalar", lambda s: s.activation(r[:, R_RS:R_RS + TT], r[:, R_SM:R_SM + TT], AF.Exp, scale=-0.5),
             inc=tag + "r")
        return p.val(tag + "r")

    def bc_row(row_ap, dst_c, n, waits, tag):
        p.op("tensor", lambda t: t.matmul(ps[:, dst_c:dst_c + n], r_ones1f[0:1, 0:1],
                                          row_ap, start=True, stop=True),
             waits=waits, inc=tag)
        return p.val(tag)

    h_t = slab1[:, :]
    TT = 256
    NT1 = S // TT
    t_x = slab2[:, 0:8192]
    t_ln = slab2[:, 8192:16384]
    R_SM, R_SQ, R_M, R_RS = 0, 256, 512, 768
    P0EV = p.val("p0ev")
    P0TR = p.val("p0tr")

    def ln_stats(tag, src_tile, xwaits):
        p.op("scalar", lambda s: s.activation(t_ln[:, :], src_tile, AF.Square),
             waits=list(xwaits) + [(tag + "r", p.val(tag + "r"))], inc=tag + "sq")
        sqv = p.val(tag + "sq")
        for kc in range(NKC):
            p.op("tensor", lambda t, kc=kc: t.matmul(
                ps[0:1, 2048:2048 + TT], r_onescolb[:, :], src_tile[:, kc * TT:(kc + 1) * TT],
                start=(kc == 0), stop=(kc == NKC - 1)),
                waits=(list(xwaits) + [(tag + "r", p.val(tag + "r"))]) if kc == 0 else (),
                inc=None if kc < NKC - 1 else tag + "st")
        for kc in range(NKC):
            p.op("tensor", lambda t, kc=kc: t.matmul(
                ps[0:1, 2560:2560 + TT], r_onescolb[:, :], t_ln[:, kc * TT:(kc + 1) * TT],
                start=(kc == 0), stop=(kc == NKC - 1)),
                waits=[(tag + "sq", sqv)] if kc == 0 else (),
                inc=None if kc < NKC - 1 else tag + "st")
        stv = p.val(tag + "st")
        r = f32s[0:1, :]
        p.op("vector", lambda v: v.tensor_scalar_mul(r[:, R_M:R_M + TT], ps[0:1, 2048:2048 + TT], 1.0 / D),
             waits=[(tag + "st", stv)], inc=tag + "r")
        p.op("vector", lambda v: v.tensor_scalar_mul(r[:, R_SQ:R_SQ + TT], ps[0:1, 2560:2560 + TT], 1.0 / D),
             inc=tag + "r")
        p.op("vector", lambda v: v.tensor_tensor(r[:, R_SM:R_SM + TT], r[:, R_M:R_M + TT], r[:, R_M:R_M + TT], ALU.mult),
             waits=[(tag + "r", p.val(tag + "r"))], inc=tag + "r")
        p.op("vector", lambda v: v.tensor_tensor(r[:, R_SQ:R_SQ + TT], r[:, R_SQ:R_SQ + TT], r[:, R_SM:R_SM + TT], ALU.subtract),
             waits=[(tag + "r", p.val(tag + "r"))], inc=tag + "r")
        p.op("vector", lambda v: v.tensor_scalar_add(r[:, R_RS:R_RS + TT], r[:, R_SQ:R_SQ + TT], EPS),
             waits=[(tag + "r", p.val(tag + "r"))], inc=tag + "r")
        p.op("scalar", lambda s: s.activation(r[:, R_SM:R_SM + TT], r[:, R_RS:R_RS + TT], AF.Ln),
             waits=[(tag + "r", p.val(tag + "r"))], inc=tag + "r")
        p.op("scalar", lambda s: s.activation(r[:, R_RS:R_RS + TT], r[:, R_SM:R_SM + TT], AF.Exp, scale=-0.5),
             inc=tag + "r")
        return p.val(tag + "r")

    SMF = slab2[:, 14464:15488].bitcast(F32)
    SMB = slab2

    def ffn_tile(tag, ti, lnv, x_tile, out_writer, extra_first=()):
        for kc in range(NKC):
            p.op("tensor", lambda t, kc=kc: t.matmul(
                ps[:, 0:TT], at_t[:, kc * 128:(kc + 1) * 128], t_ln[:, kc * TT:(kc + 1) * TT],
                start=(kc == 0), stop=(kc == NKC - 1)),
                waits=([(tag + "ln", lnv), (tag + "loev", p.val(tag + "loev"))] + list(extra_first)) if kc == 0 else (),
                inc=None if kc < NKC - 1 else tag + "lo")
        for kc in range(NKC):
            p.op("tensor", lambda t, kc=kc: t.matmul(
                ps[0:8, 512:512 + TT], wrt_t[:, kc * 8:(kc + 1) * 8], t_ln[:, kc * TT:(kc + 1) * TT],
                start=(kc == 0), stop=(kc == NKC - 1)),
                waits=[(tag + "ls", p.val(tag + "ls"))] if kc == 0 else (),
                inc=None if kc < NKC - 1 else tag + "lo")
        lov = p.val(tag + "lo")
        p.op("scalar", lambda s: s.activation(SMF[0:8, 0:TT], ps[0:8, 512:512 + TT], AF.Copy),
             waits=[(tag + "lo", lov), (tag + "sb", p.val(tag + "sb"))], inc=tag + "sm")
        p.op("vector", lambda v: v.tensor_scalar_add(SMF[0:8, 0:TT], SMF[0:8, 0:TT], r_brcol[:, :]),
             waits=[(tag + "sm", p.val(tag + "sm"))], inc=tag + "sm")
        p.op("tensor", lambda t: t.matmul(ps[0:1, 2304:2304 + TT], r_ones8f[:, :], SMF[0:8, 0:TT],
                                          start=True, stop=True),
             waits=[(tag + "sm", p.val(tag + "sm"))], inc=tag + "smm")
        p.op("vector", lambda v: v.tensor_copy(f32s[0:1, R_SM:R_SM + TT], ps[0:1, 2304:2304 + TT]),
             waits=[(tag + "smm", p.val(tag + "smm"))], inc=tag + "sm")
        p.op("tensor", lambda t: t.matmul(ps[0:8, 2304:2304 + TT], r_ones1f[0:1, 0:8], f32s[0:1, R_SM:R_SM + TT],
                                          start=True, stop=True),
             waits=[(tag + "sm", p.val(tag + "sm"))], inc=tag + "smm")
        p.op("vector", lambda v: v.scalar_tensor_tensor(
            SMF[0:8, 256:256 + TT], ps[0:8, 2304:2304 + TT], -0.125, SMF[0:8, 0:TT],
            ALU.mult, ALU.add),
            waits=[(tag + "smm", p.val(tag + "smm"))], inc=tag + "sm")
        p.op("scalar", lambda s: s.activation(SMF[0:8, 0:TT], SMF[0:8, 256:256 + TT], AF.Exp),
             waits=[(tag + "sm", p.val(tag + "sm"))], inc=tag + "sm")
        p.op("tensor", lambda t: t.matmul(ps[0:1, 2304:2304 + TT], r_ones8f[:, :], SMF[0:8, 0:TT],
                                          start=True, stop=True),
             waits=[(tag + "sm", p.val(tag + "sm"))], inc=tag + "smm")
        p.op("vector", lambda v: v.tensor_copy(f32s[0:1, R_SM:R_SM + TT], ps[0:1, 2304:2304 + TT]),
             waits=[(tag + "smm", p.val(tag + "smm"))], inc=tag + "sm")
        p.op("vector", lambda v: v.reciprocal(f32s[0:1, R_SM:R_SM + TT], f32s[0:1, R_SM:R_SM + TT]),
             inc=tag + "sm")
        p.op("tensor", lambda t: t.matmul(ps[0:8, 2304:2304 + TT], r_ones1f[0:1, 0:8], f32s[0:1, R_SM:R_SM + TT],
                                          start=True, stop=True),
             waits=[(tag + "sm", p.val(tag + "sm"))], inc=tag + "smm")
        sc_bf = SMB[0:8, 14976:14976 + TT]
        p.op("vector", lambda v: v.scalar_tensor_tensor(
            sc_bf, ps[0:8, 2304:2304 + TT], 8.0, SMF[0:8, 0:TT], ALU.mult, ALU.mult),
            waits=[(tag + "smm", p.val(tag + "smm"))], inc=tag + "sm")
        smv = p.val(tag + "sm")
        p.op("tensor", lambda t: t.matmul(ps[:, 512:512 + TT], r_e8[:, :], sc_bf, start=True, stop=True),
             waits=[(tag + "sm", smv)], inc=tag + "sb")
        sbv = p.val(tag + "sb")
        low_sb = t_ln[:, 0:TT]
        low_s = t_ln[:, TT:2 * TT]
        p.op("scalar", lambda s: s.activation(low_sb, ps[:, 0:TT], AF.Copy),
             waits=[(tag + "lo", lov)], inc=tag + "loev")
        p.op("vector", lambda v: v.tensor_tensor(low_s, low_sb, ps[:, 512:512 + TT], ALU.mult),
             waits=[(tag + "sb", sbv), (tag + "loev", p.val(tag + "loev"))], inc=tag + "ls")
        lsv = p.val(tag + "ls")
        for kc in range(NKC):
            bank = 1024 + (kc % 2) * 512
            p.op("tensor", lambda t, kc=kc, bank=bank: t.matmul(
                ps[:, bank:bank + TT], bt_t[:, kc * 128:(kc + 1) * 128], low_s,
                start=True, stop=True),
                waits=[(tag + "ls", lsv)] + ([(tag + "oev", p.val(tag + "oev") - 1)] if kc >= 2 else []),
                inc=tag + "o")
            out_writer(kc, ps[:, bank:bank + TT], [(tag + "o", p.val(tag + "o"))])
        return p.val(tag + "oev")

    hdump = []
    for ti in range(NT1):
        xw = [("p2ar2", p.val("p2ar2"))]
        if ti == 0:
            xw += [("p0ev", P0EV), ("fw1ev", FFNW1)]
        for pc in range(4):
            p.dma("gpsimd",
                  v3(t_x[:, pc * 2048:(pc + 1) * 2048], 8, TT),
                  xT.ap().rearrange("(kc q) t -> q kc t", q=128)[:, pc * 8:(pc + 1) * 8, ti * TT:(ti + 1) * TT],
                  waits=xw, inc="f1xld")
        xv = p.val("f1xld")
        rowv = ln_stats("f1", t_x[:, :], [("f1xld", xv), ("p2ast2", p.val("p2ast2"))])
        p.op("tensor", lambda t: t.matmul(ps[:, 3072:3072 + TT], r_ones1f[0:1, 0:128],
                                          f32s[0:1, R_M:R_M + TT], start=True, stop=True),
             waits=[("f1r", rowv), ("f1ln", p.val("f1ln")), ("p2aln2", p.val("p2aln2"))], inc="f1b")
        p.op("tensor", lambda t: t.matmul(ps[:, 3328:3328 + TT], r_ones1f[0:1, 0:128],
                                          f32s[0:1, R_RS:R_RS + TT], start=True, stop=True),
             inc="f1b")
        bv = p.val("f1b")
        p.op("vector", lambda v: v.tensor_tensor(
            v3(t_ln[:, :], 32, TT), v3(t_x[:, :], 32, TT),
            rowb(ps[:, 3072:3072 + TT], 32, TT), ALU.subtract),
            waits=[("f1b", bv), ("f1sq", p.val("f1sq"))], inc="f1ln")
        p.op("vector", lambda v: v.tensor_tensor(
            v3(t_ln[:, :], 32, TT), v3(t_ln[:, :], 32, TT),
            rowb(ps[:, 3328:3328 + TT], 32, TT), ALU.mult),
            waits=[("f1ln", p.val("f1ln"))], inc="f1ln")
        lnv = p.val("f1ln")

        def x1_writer(kc, ps_ap, waits):
            p.op("vector", lambda v, kc=kc, ps_ap=ps_ap: v.tensor_tensor(
                t_x[:, kc * TT:(kc + 1) * TT], ps_ap, t_x[:, kc * TT:(kc + 1) * TT], ALU.add),
                waits=waits, inc="f1oev")

        f1ov = ffn_tile("f1", ti, lnv, t_x, x1_writer)
        if dodbg["x1"]:
            for pc in range(4):
                p.dma("sync", dodbg["x1"].ap().rearrange("(kc q) t -> q kc t", q=128)[:, pc * 8:(pc + 1) * 8, ti * TT:(ti + 1) * TT],
                      v3(t_x[:, pc * 2048:(pc + 1) * 2048], 8, TT),
                      waits=[("f1oev", f1ov)], inc="dbgst")
        rowv = ln_stats("p2a", t_x[:, :], [("f1oev", f1ov)])
        r = f32s[0:1, :]
        SCRQ = f32b[0:1, 0:TT]
        p.op("vector", lambda v: v.tensor_scalar_add(SCRQ, r[:, R_SQ:R_SQ + TT], EPS),
             waits=[("p2ar", rowv), ("p3dt", p.val("p3dt"))], inc="p2ar")
        p.op("vector", lambda v: v.reciprocal(r[:, R_SM:R_SM + TT], SCRQ),
             inc="p2ar")
        p.op("vector", lambda v: v.tensor_tensor(SCRQ, r[:, R_SM:R_SM + TT], r[:, R_SQ:R_SQ + TT], ALU.mult),
             waits=[("p2ar", p.val("p2ar"))], inc="p2ar")
        p.op("vector", lambda v: v.tensor_scalar_add(r[:, R_SM:R_SM + TT], SCRQ, EPS),
             waits=[("p2ar", p.val("p2ar"))], inc="p2ar")
        p.op("scalar", lambda s: s.activation(SCRQ, r[:, R_SM:R_SM + TT], AF.Ln),
             waits=[("p2ar", p.val("p2ar"))], inc="p2ar")
        p.op("scalar", lambda s: s.activation(r[:, R_SM:R_SM + TT], SCRQ, AF.Exp, scale=-0.5),
             inc="p2ar")
        p.op("vector", lambda v: v.tensor_tensor(r[:, R_SM:R_SM + TT], r[:, R_SM:R_SM + TT], r[:, R_RS:R_RS + TT], ALU.mult),
             waits=[("p2ar", p.val("p2ar"))], inc="p2ar")
        rowv = p.val("p2ar")
        p.op("tensor", lambda t: t.matmul(ps[:, 3072:3072 + TT], r_ones1f[0:1, 0:128],
                                          r[:, R_M:R_M + TT], start=True, stop=True),
             waits=[("p2ar", rowv), ("p2au", p.val("p2au")), ("p2aln2", p.val("p2aln2"))],
             inc="p2ab")
        p.op("tensor", lambda t: t.matmul(ps[:, 3328:3328 + TT], r_ones1f[0:1, 0:128],
                                          r[:, R_RS:R_RS + TT], start=True, stop=True),
             inc="p2ab")
        p.op("tensor", lambda t: t.matmul(ps[:, 3584:3584 + TT], r_ones1f[0:1, 0:128],
                                          r[:, R_SM:R_SM + TT], start=True, stop=True),
             waits=[("p2ah", p.val("p2ah"))], inc="p2ab")
        bv = p.val("p2ab")
        p.op("vector", lambda v: v.tensor_tensor(
            v3(t_ln[:, :], 32, TT), v3(t_x[:, :], 32, TT),
            rowb(ps[:, 3072:3072 + TT], 32, TT), ALU.subtract),
            waits=[("p2ab", bv), ("p2asq", p.val("p2asq"))], inc="p2au")
        uv = p.val("p2au")
        hview = v3(h_t, 32, 2048)[:, :, ti * TT:(ti + 1) * TT]
        p.op("vector", lambda v, hview=hview: v.tensor_tensor(
            hview, v3(t_ln[:, :], 32, TT), rowb(ps[:, 3584:3584 + TT], 32, TT), ALU.mult),
            waits=[("p2au", uv)], inc="p2ah")
        for kc in range(NKC):
            p.op("vector", lambda v, kc=kc, ti=ti: v.tensor_scalar_mul(
                h_t[:, kc * 2048 + ti * TT: kc * 2048 + (ti + 1) * TT],
                h_t[:, kc * 2048 + ti * TT: kc * 2048 + (ti + 1) * TT],
                r_normwT[:, kc:kc + 1]),
                waits=[("p2ah", p.val("p2ah"))] if kc == 0 else (), inc="p2ahn")
        hnv = p.val("p2ahn")
        p.op("vector", lambda v: v.tensor_tensor(
            v3(t_ln[:, :], 32, TT), v3(t_ln[:, :], 32, TT),
            rowb(ps[:, 3328:3328 + TT], 32, TT), ALU.mult),
            waits=[("p2ah", p.val("p2ah"))], inc="p2aln2")
        p.op("vector", lambda v: v.tensor_tensor(
            v3(t_ln[:, :], 32, TT), v3(t_ln[:, :], 32, TT), v3(t_x[:, :], 32, TT), ALU.add),
            waits=[("p2aln2", p.val("p2aln2"))], inc="p2ar2")
        r2v = p.val("p2ar2")
        for pc in range(4):
            p.dma("sync", r2_d.ap().rearrange("(kc q) t -> q kc t", q=128)[:, pc * 8:(pc + 1) * 8, ti * TT:(ti + 1) * TT],
                  v3(t_ln[:, pc * 2048:(pc + 1) * 2048], 8, TT),
                  waits=[("p2ar2", r2v)], inc="p2ast2")
        if dodbg["r2"]:
            for pc in range(4):
                p.dma("sync", dodbg["r2"].ap().rearrange("(kc q) t -> q kc t", q=128)[:, pc * 8:(pc + 1) * 8, ti * TT:(ti + 1) * TT],
                      v3(t_ln[:, pc * 2048:(pc + 1) * 2048], 8, TT),
                      waits=[("p2ar2", r2v)], inc="dbgst")
    HREADY = p.val("p2ahn")
    if dodbg["h"]:
        p.dma("sync", dodbg["h"].ap().rearrange("(kc q) t -> q kc t", q=128),
              v3(h_t[:, :], 32, 2048), waits=[("p2ahn", HREADY)], inc="dbgst")

    if "p2b" in PHASES:
        ob_order = list(range(OB_HBC0, NOB)) + list(range(0, OB_HBC0))
        t_pre0 = slab2[:, 0:2051]
        t_pre1 = slab2[:, 2051:4102]
        t_acc = slab2[:, 4102:6150]
        t_st0 = slab2[:, 6150:8198]
        t_st1 = slab2[:, 8198:10246]
        wbufs = (wbuf0, wbuf1)
        p2b_store_done = {}
        p.op("vector", lambda v: v.memset(t_pre0[:, 0:3], 0.0), waits=[("p2ast2", p.val("p2ast2"))], inc="cpad")
        p.op("vector", lambda v: v.memset(t_pre1[:, 0:3], 0.0), inc="cpad")
        CPAD = p.val("cpad")
        for i, ob in enumerate(ob_order):
            par = i % 2
            wb = wbufs[par]
            is_hbc = OB_HBC0 <= ob < OB_HBC0 + CONVD // 128
            is_dt = ob == NOB - 1
            p.dma("sync", v3(wb, 32, 128), wpt_blk.ap()[ob],
                  waits=[("wblk", blk_ready[ob]), ("p2bmm", max(0, p.val("p2bmm") - 1))],
                  inc="p2bwl")
            wlv = p.val("p2bwl")
            for k in range(NKC):
                for t4 in range(4):
                    w8 = []
                    if k == 0 and t4 == 0:
                        w8 = [("p2bwl", wlv), ("p2ahn", HREADY),
                              ("p2bev", max(0, p.val("p2bev") - 4))]
                    p.op("tensor", lambda t, k=k, t4=t4, par=par, wb=wb: t.matmul(
                        ps[:, par * 2048 + t4 * 512: par * 2048 + t4 * 512 + 512],
                        wb[:, k * 128:(k + 1) * 128],
                        h_t[:, k * 2048 + t4 * 512: k * 2048 + (t4 + 1) * 512],
                        start=(k == 0), stop=(k == NKC - 1)),
                        waits=w8,
                        inc=("p2bmm" if (k == NKC - 1 and t4 == 3) else None))
            mmv = p.val("p2bmm")
            dest = (t_pre0 if par == 0 else t_pre1) if is_hbc else (t_st0 if par == 0 else t_st1)
            off = 3 if is_hbc else 0
            guard = ("p2bcv", p.val("p2bcv")) if is_hbc else ("p2bst", max(0, p.val("p2bst") - 16))
            for t4 in range(4):
                p.op("scalar", lambda s, t4=t4, par=par, dest=dest, off=off: s.activation(
                    dest[:, off + t4 * 512: off + (t4 + 1) * 512],
                    ps[:, par * 2048 + t4 * 512: par * 2048 + t4 * 512 + 512], AF.Copy),
                    waits=[("p2bmm", mmv), guard, ("cpad", CPAD)] if t4 == 0 else [("p2bmm", mmv)],
                    inc="p2bev")
            evv = p.val("p2bev")
            if is_hbc:
                cb = ob - OB_HBC0
                pre = t_pre0 if par == 0 else t_pre1
                st = t_st0 if par == 0 else t_st1
                p.op("vector", lambda v, pre=pre, cb=cb: v.tensor_scalar_mul(
                    t_acc[:, 0:2048], pre[:, 0:2048], r_convw[:, 4 * cb:4 * cb + 1]),
                    waits=[("p2bev", evv), ("p2bsl", p.val("p2bsl"))], inc="p2bcv")
                for tap in range(1, 4):
                    p.op("vector", lambda v, pre=pre, cb=cb, tap=tap: v.scalar_tensor_tensor(
                        t_acc[:, 0:2048], pre[:, tap:2048 + tap], r_convw[:, 4 * cb + tap:4 * cb + tap + 1],
                        t_acc[:, 0:2048], ALU.mult, ALU.add),
                        waits=[("p2bcv", p.val("p2bcv"))], inc="p2bcv")
                cvv = p.val("p2bcv")
                p.op("scalar", lambda s, st=st, cb=cb: s.activation(
                    st[:, 0:2048], t_acc[:, 0:2048], AF.Silu, bias=r_convb[:, cb:cb + 1]),
                    waits=[("p2bcv", cvv), ("p2bst", max(0, p.val("p2bst") - 16))], inc="p2bsl")
                slv = p.val("p2bsl")
                p.dma("sync", proj_d.ap()[ob * 128:(ob + 1) * 128, :], st[:, 0:2048],
                      waits=[("p2bsl", slv)], inc="p2bst")
            else:
                st = t_st0 if par == 0 else t_st1
                p.dma("sync", proj_d.ap()[ob * 128:(ob + 1) * 128, :], st[:, 0:2048],
                      waits=[("p2bev", evv)], inc="p2bst")
            p2b_store_done[ob] = p.val("p2bst")
        PST_HBC = p2b_store_done[NOB - 1]
        PST_ALL = p.val("p2bst")
        if dodbg["proj"]:
            p.dma("sync", dodbg["proj"].ap()[:, :], proj_d.ap()[:, :],
                  waits=[("p2bst", PST_ALL)], inc="dbgst")

    if "p3" in PHASES:
        P2BMM = p.val("p2bmm")
        tileT = (slab1[:, 0:10368], slab1[:, 10368:20736])
        pstate = slab1[:, 20736:28928]
        hc_t = slab1[:, 28928:37120]
        hd2_t = slab1[:, 37120:45312]
        ytc_t = slab1[:, 45312:53504]
        ytmpA = slab1[:, 53504:54528]
        ytmpB = slab1[:, 54528:55552]
        ptmp_t = slab1[:, 55552:56576]
        ghat_t = slab1[:, 56576:57600]
        mhat_t = slab2[:, 0:16384]
        d0sb = (slab3[:, 0:2048], slab3[:, 2048:4096])
        cgf_t = slab3[:, 4096:5120]
        row_t = slab3[:, 5120:7168]
        sm_dtsp = slab3[:, 7168:7296]
        sm_w2 = slab3[:, 7296:7424]
        sm_decay = slab3[:, 7424:7552]
        sm_eacF = slab3[:, 7552:7680]
        sm_lam = slab3[:, 7680:7808]
        sm_eacs = slab3[:, 7808:7936]
        bgf_t = slab3[:, 7936:8192]

        p.op("vector", lambda v: v.memset(pstate[:, :], 0.0),
             waits=[("p2bmm", P2BMM)], inc="p3pupd", inc_n=8)
        for c in range(NCH):
            tl = tileT[c % 2]
            tlw = [("p2bst", PST_HBC), ("p2bmm", P2BMM),
                   ("p3yad", max(0, (c - 1) * 8)), ("p3mmS", max(0, (c - 1) * 8)),
                   ("p3dt", max(0, p.val("p3dt") - 7))]
            for piece in range(6):
                r0 = 8192 + piece * 1728
                p.dma("sync", tl[:, piece * 1728:(piece + 1) * 1728],
                      proj_d.ap()[r0:r0 + 1728, c * 128:(c + 1) * 128],
                      waits=tlw, inc="p3tl", transpose=True)
            tlv = p.val("p3tl")
            dtT = tl[:, 10240:10368]
            s1 = f32b[:, 0:128]
            scr = f32b[:, 128:256]
            acs_sb = f32s[:, 256:384]
            mx = f32s[:, 128:256]
            p.op("vector", lambda v, dtT=dtT: v.tensor_tensor(s1, dtT, r_biasbc[:, :], ALU.add),
                 waits=[("p3tl", tlv), ("p3acW", p.val("p3acW"))], inc="p3dt")
            p.op("scalar", lambda s: s.activation(scr, s1, AF.Abs),
                 waits=[("p3dt", p.val("p3dt")), ("p3mmT", p.val("p3mmT")),
                        ("p3mmA", p.val("p3mmA"))], inc="p3dta")
            p.op("scalar", lambda s: s.activation(scr, scr, AF.Exp, scale=-1.0), inc="p3dta")
            p.op("scalar", lambda s: s.activation(scr, scr, AF.Ln, bias=1.0), inc="p3dta")
            p.op("vector", lambda v: v.tensor_scalar_max(mx, s1, 0.0),
                 waits=[("p3dt", p.val("p3dt"))], inc="p3dt")
            p.op("vector", lambda v: v.tensor_tensor(s1, scr, mx, ALU.add),
                 waits=[("p3dta", p.val("p3dta")), ("p3dt", p.val("p3dt"))], inc="p3dt")
            p.op("vector", lambda v: v.tensor_copy(sm_dtsp[:, :], s1),
                 waits=[("p3dt", p.val("p3dt")), ("p3hc", p.val("p3hc")),
                        ("p3w2", p.val("p3w2"))], inc="p3dt")
            p.op("vector", lambda v: v.tensor_tensor(scr, s1, r_abc[:, :], ALU.mult),
                 waits=[("p3dt", p.val("p3dt"))], inc="p3acW")
            acv = p.val("p3acW")
            p.op("tensor", lambda t: t.matmul(ps[:, 2048:2176], r_l1[:, :], scr, start=True, stop=True),
                 waits=[("p3acW", acv), ("p3acs", p.val("p3acs")),
                        ("p3bc", p.val("p3bc"))] +
                       ([("p2bev", p.val("p2bev"))] if c == 0 else []),
                 inc="p3mmA")
            mmav = p.val("p3mmA")
            p.op("scalar", lambda s: s.activation(sm_eacs[:, :], ps[:, 2048:2176], AF.Exp),
                 waits=[("p3mmA", mmav), ("p3ev1", p.val("p3ev1"))], inc="p3acs")
            p.op("scalar", lambda s: s.activation(acs_sb, ps[:, 2048:2176], AF.Copy),
                 waits=[("p3mmA", mmav), ("p3mmL", p.val("p3mmL"))], inc="p3acs")
            acsv = p.val("p3acs")
            p.op("tensor", lambda t: t.matmul(ps[0:1, 3584:3712], r_e127[:, :], acs_sb, start=True, stop=True),
                 waits=[("p3acs", acsv), ("p3lam", p.val("p3lam"))], inc="p3mmL")
            p.op("scalar", lambda s: s.activation(f32s[0:1, 0:128], ps[0:1, 3584:3712], AF.Exp),
                 waits=[("p3mmL", p.val("p3mmL"))], inc="p3lam")
            p.op("tensor", lambda t: t.matmul(ps[:, 3584:3712], r_ones1f[0:1, 0:128], f32s[0:1, 0:128],
                                              start=True, stop=True),
                 waits=[("p3lam", p.val("p3lam"))], inc="p3mmL")
            p.op("vector", lambda v: v.tensor_copy(sm_lam[:, :], ps[:, 3584:3712]),
                 waits=[("p3mmL", p.val("p3mmL")), ("p3pt", p.val("p3pt"))], inc="p3lam")
            lamv = p.val("p3lam")
            p.op("tensor", lambda t: t.transpose(ps[:, 2560:2688], scr, r_idf[:, :]),
                 waits=[("p3acW", acv), ("p3eac", p.val("p3eac")),
                        ("p3bc", p.val("p3bc"))], inc="p3mmT")
            p.op("scalar", lambda s: s.activation(sm_eacF[:, :], ps[:, 2560:2688], AF.Exp),
                 waits=[("p3mmT", p.val("p3mmT")), ("p3row", p.val("p3row"))], inc="p3eac")
            p.op("vector", lambda v: v.memset(sm_eacF[:, 0:1], 0.0),
                 waits=[("p3eac", p.val("p3eac"))], inc="p3eac")
            eacv = p.val("p3eac")
            for g in range(NG):
                bank = 2304 if g % 2 == 0 else 2816
                p.op("tensor", lambda t, g=g, bank=bank, tl=tl: t.transpose(
                    psbf(bank, 256)[:, 0:128], tl[:, 8192 + g * 128:8192 + (g + 1) * 128], r_idbf[:, :]),
                    waits=[("p3tl", tlv), ("p3bc", p.val("p3bc")),
                           ("p3acs", acsv), ("p3eac", eacv)], inc="p3mmT")
                p.op("vector", lambda v, g=g, bank=bank: v.tensor_copy(
                    bgf_t[:, (g % 2) * 128:(g % 2) * 128 + 128], psbf(bank, 256)[:, 0:128]),
                    waits=[("p3mmT", p.val("p3mmT"))], inc="p3bc")
                p.op("tensor", lambda t, g=g, bank=bank, tl=tl: t.transpose(
                    psbf(bank + 128, 256)[:, 0:128], tl[:, 9216 + g * 128:9216 + (g + 1) * 128], r_idbf[:, :]),
                    waits=[("p3bc", p.val("p3bc"))], inc="p3mmT")
                p.op("vector", lambda v, g=g, bank=bank: v.tensor_copy(
                    cgf_t[:, g * 128:(g + 1) * 128], psbf(bank + 128, 256)[:, 0:128]),
                    waits=[("p3mmT", p.val("p3mmT"))], inc="p3bc")
                p.op("tensor", lambda t, g=g: t.matmul(
                    ps[:, 3072:3200], bgf_t[:, (g % 2) * 128:(g % 2) * 128 + 128],
                    cgf_t[:, g * 128:(g + 1) * 128], start=True, stop=True),
                    waits=[("p3bc", p.val("p3bc")), ("p3gh", p.val("p3gh"))], inc="p3mmG")
                p.op("vector", lambda v, g=g: v.tensor_tensor(
                    ghat_t[:, g * 128:(g + 1) * 128], ps[:, 3072:3200], r_u[:, :], ALU.mult),
                    waits=[("p3mmG", p.val("p3mmG")), ("p3mh", max(0, 8 * c))], inc="p3gh")
            ghv = p.val("p3gh")
            p.op("gpsimd", lambda g, tl=tl: g.tensor_tensor(
                v3(hc_t, NH, PH), v3(tl[:, 0:8192], NH, PH), colb(sm_dtsp[:, :], NH, PH), ALU.mult),
                waits=[("p3dt", p.val("p3dt")), ("p3mmYd", p.val("p3mmYd"))], inc="p3hc")
            for e in range(8):
                p.dma("sync", row_t[0:1, :].rearrange("q (a b) -> q a b", a=16),
                      sm_eacF[e * 16:(e + 1) * 16, :],
                      waits=[("p3eac", eacv), ("p3mmD", p.val("p3mmD"))], inc="p3row")
                rv = p.val("p3row")
                for q in range(4):
                    w8 = [("p3row", rv)]
                    if e == 0 and q == 0:
                        w8 += [("p3pupd", p.val("p3pupd")), ("p3ev2", p.val("p3ev2"))]
                    if q == 0:
                        w8 += [("p3d0", p.val("p3d0"))]
                    p.op("tensor", lambda t, q=q: t.matmul(
                        ps[:, q * 512:(q + 1) * 512], r_ones1b[:, :], row_t[0:1, q * 512:(q + 1) * 512],
                        start=True, stop=True),
                        waits=w8, inc=None if q < 3 else "p3mmD")
                dmv = p.val("p3mmD")
                p.op("scalar", lambda s, e=e: s.activation(d0sb[e % 2][:, :], ps[:, 0:2048], AF.Copy),
                     waits=[("p3mmD", dmv), ("p3scan", max(0, p.val("p3scan") - 1))], inc="p3d0")
                d0v = p.val("p3d0")
                p.op("vector", lambda v, e=e: v.tensor_tensor_scan(
                    mhat_t[:, e * 2048:(e + 1) * 2048], d0sb[e % 2][:, :], r_irep[:, :],
                    0.0, ALU.mult, ALU.add),
                    waits=[("p3d0", d0v), ("p3mmYd", p.val("p3mmYd"))], inc="p3scan")
                scv = p.val("p3scan")
                p.op("vector", lambda v, e=e: v.tensor_copy(
                    sm_decay[:, e * 16:(e + 1) * 16],
                    v3(mhat_t[:, e * 2048:(e + 1) * 2048], 16, 128)[:, :, 127]),
                    waits=[("p3scan", scv)], inc="p3dec")
                p.op("gpsimd", lambda g, e=e: g.tensor_tensor(
                    v3(mhat_t[:, e * 2048:(e + 1) * 2048], 16, 128),
                    v3(mhat_t[:, e * 2048:(e + 1) * 2048], 16, 128),
                    rowb(ghat_t[:, e * 128:(e + 1) * 128], 16, 128), ALU.mult),
                    waits=[("p3scan", scv), ("p3dec", p.val("p3dec")), ("p3gh", ghv)], inc="p3mh")
            p.op("vector", lambda v: v.tensor_tensor(sm_w2[:, :], sm_dtsp[:, :], sm_decay[:, :], ALU.mult),
                 waits=[("p3dec", p.val("p3dec")), ("p3hd", p.val("p3hd"))], inc="p3w2")
            p.op("gpsimd", lambda g, tl=tl: g.tensor_tensor(
                v3(hd2_t, NH, PH), v3(tl[:, 0:8192], NH, PH), colb(sm_w2[:, :], NH, PH), ALU.mult),
                waits=[("p3w2", p.val("p3w2")), ("p3mmS", p.val("p3mmS"))], inc="p3hd")
            hdv = p.val("p3hd")
            for e in range(8):
                ytslice = ytc_t[:, e * 1024:(e + 1) * 1024]
                for q in range(2):
                    w8 = [("p3d0", p.val("p3d0"))] if (e == 0 and q == 0) else []
                    if q == 0:
                        w8 += [("p3bc", p.val("p3bc")), ("p3pupd", p.val("p3pupd"))]
                    p.op("tensor", lambda t, e=e, q=q: t.matmul(
                        ps[:, q * 512:(q + 1) * 512], cgf_t[:, e * 128:(e + 1) * 128],
                        pstate[:, e * 1024 + q * 512: e * 1024 + (q + 1) * 512],
                        start=True, stop=True),
                        waits=w8, inc=None if q == 0 else "p3mmY")
                ymv = p.val("p3mmY")
                p.op("vector", lambda v, e=e: v.tensor_tensor(
                    v3(ytmpA, 16, 64), v3(ps[:, 0:1024], 16, 64),
                    colb(sm_eacs[:, e * 16:(e + 1) * 16], 16, 64), ALU.mult),
                    waits=[("p3mmY", ymv), ("p3acs", acsv), ("p3yad", max(0, p.val("p3yad") - 1))],
                    inc="p3ev1")
                e1v = p.val("p3ev1")
                for hr in range(16):
                    w8 = []
                    if hr == 0:
                        w8 = [("p3mh", p.val("p3mh")), ("p3hc", p.val("p3hc")),
                              ("p3ev2", p.val("p3ev2"))]
                    p.op("tensor", lambda t, e=e, hr=hr: t.matmul(
                        ps[:, 1024 + hr * 64:1024 + (hr + 1) * 64],
                        mhat_t[:, (e * 16 + hr) * 128:(e * 16 + hr + 1) * 128],
                        hc_t[:, (e * 16 + hr) * 64:(e * 16 + hr + 1) * 64],
                        start=True, stop=True),
                        waits=w8, inc=None if hr < 15 else "p3mmYd")
                ydv = p.val("p3mmYd")
                p.op("gpsimd", lambda g, e=e, tl=tl: g.tensor_tensor(
                    v3(ytmpB, 16, 64), v3(tl[:, e * 1024:(e + 1) * 1024], 16, 64),
                    colb(r_dpbc[:, e * 16:(e + 1) * 16], 16, 64), ALU.mult),
                    waits=[("p3tl", tlv), ("p3yad", max(0, p.val("p3yad") - 1))], inc="p3dres")
                drv = p.val("p3dres")
                p.op("vector", lambda v, e=e, ytslice=ytslice: v.tensor_tensor(
                    ytslice, ps[:, 1024:2048], ytmpA, ALU.add),
                    waits=[("p3mmYd", ydv), ("p3ev1", e1v), ("p3yst", p.val("p3yst"))],
                    inc="p3ev2")
                e2v = p.val("p3ev2")
                p.op("gpsimd", lambda g, ytslice=ytslice: g.tensor_tensor(
                    ytslice, ytslice, ytmpB, ALU.add),
                    waits=[("p3ev2", e2v), ("p3dres", drv)], inc="p3yad")
                for q in range(2):
                    w8 = [("p3ev1", e1v), ("p3hd", hdv),
                          ("p3pupd", p.val("p3pupd"))] if q == 0 else []
                    p.op("tensor", lambda t, e=e, q=q, tl=tl: t.matmul(
                        ps[:, q * 512:(q + 1) * 512], tl[:, 8192 + e * 128:8192 + (e + 1) * 128],
                        hd2_t[:, e * 1024 + q * 512: e * 1024 + (q + 1) * 512],
                        start=True, stop=True),
                        waits=w8, inc=None if q == 0 else "p3mmS")
                smv = p.val("p3mmS")
                p.op("vector", lambda v, e=e: v.tensor_tensor(
                    v3(ptmp_t, 16, 64), v3(pstate[:, e * 1024:(e + 1) * 1024], 16, 64),
                    colb(sm_lam[:, e * 16:(e + 1) * 16], 16, 64), ALU.mult),
                    waits=[("p3lam", lamv), ("p3mmY", ymv)], inc="p3pt")
                ptv = p.val("p3pt")
                p.op("vector", lambda v, e=e: v.tensor_tensor(
                    pstate[:, e * 1024:(e + 1) * 1024], ptmp_t[:, 0:1024],
                    ps[:, 0:1024], ALU.add),
                    waits=[("p3mmS", smv), ("p3pt", ptv)], inc="p3pupd")
            p.dma("sync", yt_d.ap()[c * 128:(c + 1) * 128, :], ytc_t[:, :],
                  waits=[("p3yad", p.val("p3yad"))], inc="p3yst")
            if c == 0 and "p3c0" in dbg_t:
                dbgw = [("p3yad", p.val("p3yad")), ("p3pupd", p.val("p3pupd")),
                        ("p3mh", p.val("p3mh")), ("p3hd", p.val("p3hd"))]
                dd = dbg_t["p3c0"].ap()
                p.dma("sync", dd[:, 0:16384], mhat_t[:, :], waits=dbgw, inc="dbgst")
                p.dma("sync", dd[:, 16384:24576], hc_t[:, :], waits=dbgw, inc="dbgst")
                p.dma("sync", dd[:, 24576:24704], sm_dtsp[:, :], waits=dbgw, inc="dbgst")
                p.dma("sync", dd[:, 24704:24832], sm_eacF[:, :], waits=dbgw, inc="dbgst")
                p.dma("sync", dd[:, 24832:24960], sm_decay[:, :], waits=dbgw, inc="dbgst")
                p.dma("sync", dd[:, 24960:25088], sm_lam[:, :], waits=dbgw, inc="dbgst")
                p.dma("sync", dd[:, 25088:25216], sm_eacs[:, :], waits=dbgw, inc="dbgst")
                p.dma("sync", dd[:, 25216:26240], ghat_t[:, :], waits=dbgw, inc="dbgst")
                p.dma("sync", dd[:, 26240:34432], pstate[:, :], waits=dbgw, inc="dbgst")
                p.dma("sync", dbg_t["p3f"].ap()[:, 0:256], f32b[:, 0:256], waits=dbgw, inc="dbgst")
                p.barrier_wait("vector", [("dbgst", p.val("dbgst"))])
        P3YST = p.val("p3yst")
        if dodbg["yt"]:
            p.dma("sync", dodbg["yt"].ap()[:, :], yt_d.ap()[:, :],
                  waits=[("p3yst", P3YST)], inc="dbgst")

    if "p35" in PHASES:
        ytin = (slab1[:, 0:8192], slab1[:, 8192:16384])
        gt = (slab1[:, 16384:24576], slab1[:, 24576:32768])
        ygn_b = (slab1[:, 32768:40960], slab1[:, 40960:49152])
        yg_t = slab1[:, 49152:57344]
        P3DONE = [("p3yst", P3YST), ("p3pupd", p.val("p3pupd")),
                  ("p3mmS", p.val("p3mmS")), ("p3yad", p.val("p3yad")),
                  ("p3mh", p.val("p3mh")), ("p3hd", p.val("p3hd")),
                  ("p3dres", p.val("p3dres")), ("p3hc", p.val("p3hc"))]
        for c in range(NCH):
            par = c % 2
            ldw = P3DONE + [("p35yg", max(0, p.val("p35yg") - 1)), ("p2bst", PST_ALL)]
            p.dma("sync", ytin[par][:, :], yt_d.ap()[c * 128:(c + 1) * 128, :],
                  waits=ldw, inc="p35ld")
            for piece in range(4):
                p.dma("sync", gt[par][:, piece * 2048:(piece + 1) * 2048],
                      proj_d.ap()[piece * 2048:(piece + 1) * 2048, c * 128:(c + 1) * 128],
                      waits=ldw + [("p35sg", max(0, p.val("p35sg") - 1))],
                      inc="p35ld", transpose=True)
            ldv = p.val("p35ld")
            p.op("scalar", lambda s, par=par: s.activation(gt[par][:, :], gt[par][:, :], AF.Silu),
                 waits=[("p35ld", ldv)], inc="p35sg")
            sgv = p.val("p35sg")
            p.op("vector", lambda v, par=par: v.tensor_tensor(yg_t[:, :], ytin[par][:, :], gt[par][:, :], ALU.mult),
                 waits=[("p35sg", sgv), ("p35ld", ldv), ("p35gn", p.val("p35gn"))], inc="p35yg")
            ygv = p.val("p35yg")
            p.op("vector", lambda v, par=par: v.tensor_tensor(ytin[par][:, :], yg_t[:, :], yg_t[:, :], ALU.mult),
                 waits=[("p35yg", ygv)], inc="p35sq")
            sqv = p.val("p35sq")
            for gidx in range(8):
                p.op("vector", lambda v, par=par, gidx=gidx: v.reduce_sum(
                    f32b[:, gidx:gidx + 1], ytin[par][:, gidx * 1024:(gidx + 1) * 1024], axis=AX),
                    waits=[("p35sq", sqv)] if gidx == 0 else (), inc="p35rs")
            rsv = p.val("p35rs")
            p.op("vector", lambda v: v.tensor_scalar(f32b[:, 8:16], f32b[:, 0:8], 1.0 / 1024.0, EPS, ALU.mult, ALU.add),
                 waits=[("p35rs", rsv)], inc="p35r")
            p.op("scalar", lambda s: s.activation(f32b[:, 8:16], f32b[:, 8:16], AF.Ln),
                 waits=[("p35r", p.val("p35r"))], inc="p35r")
            p.op("scalar", lambda s: s.activation(f32b[:, 8:16], f32b[:, 8:16], AF.Exp, scale=-0.5),
                 inc="p35r")
            p.op("vector", lambda v: v.tensor_copy(f32b[:, 16:20].bitcast(BF16)[:, 0:8], f32b[:, 8:16]),
                 waits=[("p35r", p.val("p35r"))], inc="p35r")
            rv8 = p.val("p35r")
            rstd_bf = f32b[:, 16:20].bitcast(BF16)[:, 0:8]
            p.op("vector", lambda v, par=par: v.tensor_tensor(
                v3(ygn_b[par][:, :], 8, 1024), v3(yg_t[:, :], 8, 1024),
                colb(rstd_bf, 8, 1024), ALU.mult),
                waits=[("p35r", rv8), ("p35st", max(0, p.val("p35st") - 16))], inc="p35gn")
            gnv = p.val("p35gn")
            p.dma("sync", ygn_d.ap()[c * 128:(c + 1) * 128, :], ygn_b[par][:, :],
                  waits=[("p35gn", gnv)], inc="p35st")
        P35ST = p.val("p35st")
        if dodbg["ygn"]:
            p.dma("sync", dodbg["ygn"].ap()[:, :], ygn_d.ap()[:, :],
                  waits=[("p35st", P35ST)], inc="dbgst")

    FFNW2 = build_ffn_tiles("fw2")

    if "p4" in PHASES:
        ygnF = slab1[:, 0:32768]
        w2b = (slab1[:, 32768:40960], slab1[:, 40960:49152])
        r2t = (slab1[:, 49152:49664], slab1[:, 49664:50176])
        x2t = (slab1[:, 50176:50688], slab1[:, 50688:51200])
        for t4 in range(4):
            for kc in range(NIC):
                p.dma("sync", ygnF[:, kc * 512:(kc + 1) * 512],
                      ygn_d.ap()[t4 * 512:(t4 + 1) * 512, kc * 128:(kc + 1) * 128],
                      waits=[("p35st", P35ST), ("p4mm", max(0, p.val("p4mm") - 1)),
                             ("p35gn", p.val("p35gn"))],
                      inc="p4yl", transpose=True)
            ylv = p.val("p4yl")
            for mb in range(NMB):
                par = mb % 2
                p.dma("sync", v3(w2b[par], NIC, 128), w2t_blk.ap()[mb],
                      waits=[("w2blk", w2blk_ready[mb]), ("p4mm", max(0, p.val("p4mm") - 1))],
                      inc="p4wl")
                wlv = p.val("p4wl")
                p.dma("sync", r2t[par], r2_d.ap()[mb * 128:(mb + 1) * 128, t4 * 512:(t4 + 1) * 512],
                      waits=[("p2ast2", p.val("p2ast2")), ("p4ev", max(0, p.val("p4ev") - 1))],
                      inc="p4rl")
                rlv = p.val("p4rl")
                for kc in range(NIC):
                    w8 = []
                    if kc == 0:
                        w8 = [("p4yl", ylv), ("p4wl", wlv),
                              ("p4ev", max(0, p.val("p4ev") - 1))]
                        if t4 == 0 and mb == 0:
                            w8 += [("p3pupd", p.val("p3pupd")),
                                   ("p3ev2", p.val("p3ev2")),
                                   ("p3ev1", p.val("p3ev1"))]
                    p.op("tensor", lambda t, kc=kc, par=par: t.matmul(
                        ps[:, par * 512:(par + 1) * 512], w2b[par][:, kc * 128:(kc + 1) * 128],
                        ygnF[:, kc * 512:(kc + 1) * 512],
                        start=(kc == 0), stop=(kc == NIC - 1)),
                        waits=w8, inc=None if kc < NIC - 1 else "p4mm")
                mmv = p.val("p4mm")
                p.op("vector", lambda v, par=par: v.tensor_tensor(
                    x2t[par], ps[:, par * 512:(par + 1) * 512], r2t[par], ALU.add),
                    waits=[("p4mm", mmv), ("p4rl", rlv), ("p4st", max(0, p.val("p4st") - 16))],
                    inc="p4ev")
                evv = p.val("p4ev")
                p.dma("sync", x2_d.ap()[mb * 128:(mb + 1) * 128, t4 * 512:(t4 + 1) * 512],
                      x2t[par], waits=[("p4ev", evv)], inc="p4st")
        P4ST = p.val("p4st")
        if dodbg["x2"]:
            p.dma("sync", dodbg["x2"].ap()[:, :], x2_d.ap()[:, :],
                  waits=[("p4st", P4ST)], inc="dbgst")

    if "p5" in PHASES:
        for ti in range(NT1):
            xw = [("p4st", P4ST), ("fw2ev", FFNW2), ("f5oev", p.val("f5oev")),
                  ("p3mmYd", p.val("p3mmYd")), ("p35gn", p.val("p35gn"))]
            for pc in range(4):
                p.dma("sync",
                      v3(t_x[:, pc * 2048:(pc + 1) * 2048], 8, TT),
                      x2_d.ap().rearrange("(kc q) t -> q kc t", q=128)[:, pc * 8:(pc + 1) * 8, ti * TT:(ti + 1) * TT],
                      waits=xw, inc="f5xld")
            xv = p.val("f5xld")
            rowv = ln_stats("f5", t_x[:, :], [("f5xld", xv), ("f5ost", max(0, p.val("f5ost") - 32))])
            p.op("tensor", lambda t: t.matmul(ps[:, 3072:3072 + TT], r_ones1f[0:1, 0:128],
                                              f32s[0:1, R_M:R_M + TT], start=True, stop=True),
                 waits=[("f5r", rowv), ("f5ln", p.val("f5ln"))], inc="f5b")
            p.op("tensor", lambda t: t.matmul(ps[:, 3328:3328 + TT], r_ones1f[0:1, 0:128],
                                              f32s[0:1, R_RS:R_RS + TT], start=True, stop=True),
                 inc="f5b")
            bv = p.val("f5b")
            p.op("vector", lambda v: v.tensor_tensor(
                v3(t_ln[:, :], 32, TT), v3(t_x[:, :], 32, TT),
                rowb(ps[:, 3072:3072 + TT], 32, TT), ALU.subtract),
                waits=[("f5b", bv), ("f5sq", p.val("f5sq"))], inc="f5ln")
            p.op("vector", lambda v: v.tensor_tensor(
                v3(t_ln[:, :], 32, TT), v3(t_ln[:, :], 32, TT),
                rowb(ps[:, 3328:3328 + TT], 32, TT), ALU.mult),
                waits=[("f5ln", p.val("f5ln"))], inc="f5ln")
            lnv = p.val("f5ln")

            def out_writer(kc, ps_ap, waits, ti=ti):
                fpar = kc % 2
                fb = wbuf1.bitcast(F32)[:, fpar * 256:fpar * 256 + TT]
                p.op("vector", lambda v, kc=kc, ps_ap=ps_ap, fb=fb: v.tensor_tensor(
                    fb, ps_ap, t_x[:, kc * TT:(kc + 1) * TT], ALU.add),
                    waits=list(waits) + [("f5ost", max(0, p.val("f5ost") - 16))],
                    inc="f5oev")
                p.dma("sync", out_d.ap().rearrange("(kc q) t -> q kc t", q=128)[:, kc, ti * TT:(ti + 1) * TT],
                      fb, waits=[("f5oev", p.val("f5oev"))], inc="f5ost")

            ffn_tile("f5", ti, lnv, t_x, out_writer)

    fin = [("f5ost", p.val("f5ost"))] if p.val("f5ost") else [("p2ast2", p.val("p2ast2"))]
    if p.val("dbgst"):
        fin.append(("dbgst", p.val("dbgst")))
    p.barrier_wait("sync", fin)

    with nc.Block() as block:
        @block.tensor
        def _(e):
            for f in p.ev["tensor"]:
                f(e)

        @block.vector
        def _(e):
            for f in p.ev["vector"]:
                f(e)

        @block.scalar
        def _(e):
            for f in p.ev["scalar"]:
                f(e)

        @block.gpsimd
        def _(e):
            for f in p.ev["gpsimd"]:
                f(e)

        @block.sync
        def _(e):
            for f in p.ev["sync"]:
                f(e)

    return nc


def _prep_inputs(inp):
    w = {}
    w["wp"] = np.ascontiguousarray(inp["L0_attn_in_proj_w"])
    w["w2"] = np.ascontiguousarray(inp["L0_attn_out_proj_w"])
    w["ffa"] = np.ascontiguousarray(inp["L0_ffn_A"].reshape(128, D))
    w["ffb"] = np.ascontiguousarray(inp["L0_ffn_B"])
    w["wrr"] = np.ascontiguousarray(inp["L0_ffn_Wr"])
    w["brow"] = np.ascontiguousarray(inp["L0_ffn_br"].reshape(1, NG))
    w["normw"] = np.ascontiguousarray(inp["L0_attn_norm_w"].reshape(NKC, 128))
    w["convw"] = np.ascontiguousarray(inp["L0_attn_conv_w"])
    w["convb"] = np.ascontiguousarray(inp["L0_attn_conv_b"].reshape(CONVD // 128, 128))
    w["dtb"] = np.ascontiguousarray(inp["L0_attn_dt_bias"].reshape(1, NH))
    w["alog"] = np.ascontiguousarray(inp["L0_attn_A_log"].reshape(1, NH))
    w["dpr"] = np.ascontiguousarray(inp["L0_attn_Dp"].reshape(1, NH))
    w["gnw"] = np.ascontiguousarray(inp["L0_attn_gnorm_w"].reshape(NIC, 128))
    return w


def kernel(**inp):
    w = _prep_inputs(inp)
    x = inp["L0_attn_x"]
    nc = build_program()
    in_maps = []
    for b in range(8):
        m = dict(w)
        m["xT"] = np.ascontiguousarray(x[b].T)
        in_maps.append(m)
    res = run_bass_kernel_spmd(nc, in_maps, core_ids=list(range(8)))
    out = np.stack([r["out"].T for r in res.results], axis=0)
    return np.ascontiguousarray(out.astype(np.float32))


if __name__ == "__main__":
    nc = build_program()
    n = nc.n_instructions()
    print("instructions:", n)


# version 02
# speedup vs baseline: 1.5626x; 1.5626x over previous
import numpy as np
import ml_dtypes
import concourse.bass as bass
import concourse.mybir as mybir
from concourse.bass_utils import run_bass_kernel_spmd

F32 = mybir.dt.float32
BF16 = mybir.dt.bfloat16
AF = mybir.ActivationFunctionType
ALU = mybir.AluOpType
AX = mybir.AxisListType.X
BF = ml_dtypes.bfloat16

S, D, DI, CONVD, PROJ = 2048, 4096, 8192, 10240, 18560
NH, PH, NG, NN, CH = 128, 64, 8, 128, 128
NCH, REP = S // CH, NH // NG
EPS = 1e-5
NOB, NKC, NIC = PROJ // 128, D // 128, DI // 128
NMB = D // 128
OB_HBC0 = DI // 128
ENGS = ("tensor", "vector", "scalar", "gpsimd", "sync")


class Rec:

    def __init__(self, nc):
        self.nc = nc
        self.ev = {e: [] for e in ENGS}
        self.cnt = {}
        self.handles = {}

    ALIAS = {}

    @classmethod
    def canon(cls, name):
        if name == "f5xld":
            return name
        for pre, sub in (("f5", "f1"), ("fw2", "fw1")):
            if name.startswith(pre):
                return sub + name[len(pre):]
        return name

    def _h(self, name):
        name = self.canon(name)
        if name not in self.cnt:
            self.cnt[name] = 0
            self.handles[name] = self.nc.alloc_semaphore(name)
        return self.handles[name]

    def val(self, name):
        return self.cnt.get(self.canon(name), 0)

    def op(self, eng, fn, waits=(), inc=None, inc_n=1):
        wl = []
        for w in waits:
            name, v = w if isinstance(w, tuple) else (w, self.val(w))
            if v > 0:
                wl.append((self._h(name), v))
        ih = None
        if inc is not None:
            inc = self.canon(inc)
            self._h(inc)
            self.cnt[inc] += inc_n
            ih = (self.handles[inc], inc_n)

        def emit(e, fn=fn, wl=wl, ih=ih):
            for h, v in wl:
                e.wait_ge(h, v)
            ins = fn(e)
            if ih is not None:
                ins.then_inc(ih[0], ih[1])

        self.ev[eng].append(emit)
        return self.cnt[inc] if inc else None

    STRICT = {"wpc", "w2c", "wtp", "wblk", "w2blk", "p2bst",
              "p4wl", "p4rl", "p4st", "f1ost"}

    def dma(self, eng, out, in_, waits=(), inc=None, **kw):
        waits = list(waits)
        if inc is not None and self.canon(inc) in self.STRICT:
            waits.append((self.canon(inc), self.val(inc)))
        return self.op(eng, lambda e: e.dma_start(out, in_, **kw), waits, inc, 16)

    def barrier_wait(self, eng, waits):
        wl = [(w if isinstance(w, tuple) else (w, self.val(w))) for w in waits]
        wl = [(self._h(n), v) for n, v in wl if v > 0]

        def emit(e, wl=wl):
            for h, v in wl:
                e.wait_ge(h, v)
        self.ev[eng].append(emit)


def v3(ap2d, k, r):
    return ap2d.rearrange("p (k r) -> p k r", r=r)


def colb(col_ap, k, r):
    return col_ap.broadcast_to(list(col_ap.shape) + [r])


def rowb(row_ap, k, r):
    p0, rr = row_ap.shape
    return row_ap.unsqueeze(1).broadcast_to([p0, k, rr])


class Carve:

    def __init__(self, t, total):
        self.t = t
        self.total = total
        self.off = 0

    def take(self, n):
        assert self.off + n <= self.total, (self.off, n, self.total)
        ap = self.t[:, self.off:self.off + n]
        self.off += n
        return ap

    def reset(self, off=0):
        self.off = off


def build_program(dbg=None, phases=("p2b", "p3", "p35", "p4", "p5"), detect_races=True):
    PHASES = set(phases)
    dbg = dbg or {}
    nc = bass.Bass("TRN2", target_bir_lowering=False,
                   detect_race_conditions=detect_races)

    xT = nc.dram_tensor("xT", [D, S], F32, kind="ExternalInput")
    wp = nc.dram_tensor("wp", [PROJ, D], F32, kind="ExternalInput")
    w2 = nc.dram_tensor("w2", [D, DI], F32, kind="ExternalInput")
    ffa = nc.dram_tensor("ffa", [128, D], F32, kind="ExternalInput")
    ffb = nc.dram_tensor("ffb", [NG, D, 16], F32, kind="ExternalInput")
    wrr = nc.dram_tensor("wrr", [NG, D], F32, kind="ExternalInput")
    brow = nc.dram_tensor("brow", [1, NG], F32, kind="ExternalInput")
    normw = nc.dram_tensor("normw", [NKC, 128], F32, kind="ExternalInput")
    convw = nc.dram_tensor("convw", [CONVD, 4], F32, kind="ExternalInput")
    convb = nc.dram_tensor("convb", [CONVD // 128, 128], F32, kind="ExternalInput")
    dtb = nc.dram_tensor("dtb", [1, NH], F32, kind="ExternalInput")
    alog = nc.dram_tensor("alog", [1, NH], F32, kind="ExternalInput")
    dpr = nc.dram_tensor("dpr", [1, NH], F32, kind="ExternalInput")
    gnw = nc.dram_tensor("gnw", [NIC, 128], F32, kind="ExternalInput")
    out_d = nc.dram_tensor("out", [D, S], F32, kind="ExternalOutput")

    dbg_t = {}
    for k, shape, dt_ in (("proj", [PROJ, S], BF16), ("h", [D, S], BF16),
                          ("x1", [D, S], BF16), ("yt", [S, DI], BF16),
                          ("ygn", [S, DI], BF16), ("r2", [D, S], BF16),
                          ("x2", [D, S], BF16)):
        if dbg.get(k):
            dbg_t[k] = nc.dram_tensor("dbg_" + k, shape, dt_, kind="ExternalOutput")
    if dbg.get("p3c0"):
        dbg_t["p3c0"] = nc.dram_tensor("dbg_p3c0", [128, 36096], BF16, kind="ExternalOutput")
        dbg_t["p3f"] = nc.dram_tensor("dbg_p3f", [128, 512], F32, kind="ExternalOutput")

    wp_bf = nc.dram_tensor("wp_bf", [PROJ, D], BF16)
    w2_bf = nc.dram_tensor("w2_bf", [D, DI], BF16)
    wpt_blk = nc.dram_tensor("wpt_blk", [NOB, 128, NKC, 128], BF16)
    w2t_blk = nc.dram_tensor("w2t_blk", [NMB, 128, NIC, 128], BF16)
    a_bfd = nc.dram_tensor("a_bfd", [128, D], BF16)
    b_bfd = nc.dram_tensor("b_bfd", [D, 128], BF16)
    wr_bfd = nc.dram_tensor("wr_bfd", [NG, D], BF16)
    proj_d = nc.dram_tensor("proj_d", [PROJ, S], BF16)
    r2_d = nc.dram_tensor("r2_d", [D, S], BF16)
    yt_d = nc.dram_tensor("yt_d", [S, DI], BF16)
    ygn_d = nc.dram_tensor("ygn_d", [S, DI], BF16)
    x2_d = nc.dram_tensor("x2_d", [D, S], BF16)

    c_id_bf = nc.inline_tensor(np.eye(128, dtype=BF), "c_id_bf")
    c_id_f32 = nc.inline_tensor(np.eye(128, dtype=np.float32), "c_id_f32")
    c_ones1_bf = nc.inline_tensor(np.ones((1, 128), dtype=BF), "c_ones1_bf")
    c_ones1_f = nc.inline_tensor(np.ones((1, 128), dtype=np.float32), "c_ones1_f")
    c_onescol_bf = nc.inline_tensor(np.ones((128, 1), dtype=BF), "c_onescol_bf")
    c_ones8_bf = nc.inline_tensor(np.ones((8, 1), dtype=BF), "c_ones8_bf")
    c_ones8_f = nc.inline_tensor(np.ones((8, 1), dtype=np.float32), "c_ones8_f")
    c_l1_f32 = nc.inline_tensor(np.triu(np.ones((128, 128), dtype=np.float32)), "c_l1_f32")
    c_u_bf = nc.inline_tensor(np.triu(np.ones((128, 128))).astype(BF), "c_u_bf")
    c_irep_bf = nc.inline_tensor(np.tile(np.eye(128), (1, 16)).astype(BF), "c_irep_bf")
    e8 = np.zeros((8, 128))
    for hh in range(8):
        e8[hh, hh * 16:(hh + 1) * 16] = 1
    c_e8_bf = nc.inline_tensor(e8.astype(BF), "c_e8_bf")
    e127 = np.zeros((128, 1), dtype=np.float32)
    e127[127, 0] = 1.0
    c_e127_f = nc.inline_tensor(e127, "c_e127_f")


    SBA = nc.alloc_sbuf_tensor
    slab1 = SBA("slab1", [128, 65536], BF16)
    slab2 = SBA("slab2", [128, 16384], BF16)
    slab3 = SBA("slab3", [128, 8448], BF16)
    wtmp0 = SBA("wtmp0", [128, 2048], BF16)
    wtmp1 = SBA("wtmp1", [128, 2048], BF16)
    wbuf0 = SBA("wbuf0", [128, 4096], BF16)
    r_irep = SBA("r_irep", [128, 2048], BF16)
    r_u = SBA("r_u", [128, 128], BF16)
    r_idbf = SBA("r_idbf", [128, 128], BF16)
    r_idf = SBA("r_idf", [128, 128], F32)
    r_e8 = SBA("r_e8", [8, 128], BF16)
    r_ones1b = SBA("r_ones1b", [1, 128], BF16)
    r_ones1f = SBA("r_ones1f", [1, 128], F32)
    r_onescolb = SBA("r_onescolb", [128, 1], BF16)
    r_ones8b = SBA("r_ones8b", [8, 1], BF16)
    r_ones8f = SBA("r_ones8f", [8, 1], F32)
    r_brcol = SBA("r_brcol", [8, 1], F32)
    r_e127 = SBA("r_e127", [128, 1], F32)
    r_l1 = SBA("r_l1", [128, 128], F32)
    r_convw = SBA("r_convw", [128, 320], F32)
    r_convb = SBA("r_convb", [128, 80], F32)
    r_normwT = SBA("r_normwT", [128, 32], F32)
    r_gnwT = SBA("r_gnwT", [128, 64], F32)
    r_abc = SBA("r_abc", [128, 128], F32)
    r_biasbc = SBA("r_biasbc", [128, 128], F32)
    r_dpbc = SBA("r_dpbc", [128, 128], BF16)
    f32s = SBA("f32s", [128, 1024], F32)
    f32b = SBA("f32b", [128, 256], F32)
    rem = nc.sbuf_bytes_remaining
    assert rem >= 0, rem

    wbuf1 = slab2[:, 10368:14464]
    ps = nc.alloc_psum_tensor("ps", [128, 4096], F32)

    def psbf(c0, n):
        return ps[:, c0:c0 + n // 2].bitcast(BF16)

    p = Rec(nc)
    dodbg = {k: (dbg_t[k] if k in dbg_t else None) for k in
             ("proj", "h", "x1", "yt", "ygn", "r2", "x2")}

    for dst, src in ((r_irep, c_irep_bf), (r_u, c_u_bf), (r_idbf, c_id_bf),
                     (r_idf, c_id_f32), (r_e8, c_e8_bf), (r_ones1b, c_ones1_bf),
                     (r_ones1f, c_ones1_f), (r_onescolb, c_onescol_bf),
                     (r_ones8b, c_ones8_bf), (r_ones8f, c_ones8_f), (r_e127, c_e127_f), (r_l1, c_l1_f32)):
        p.dma("gpsimd", dst[:, :], src.ap()[:, :], inc="cld")
    for i in range(8):
        src = convw.ap().rearrange("(c p) k -> p c k", p=128)[:, i * 10:(i + 1) * 10, :]
        p.dma("gpsimd", v3(r_convw[:, i * 40:(i + 1) * 40], 10, 4), src, inc="cld")
    p.dma("gpsimd", f32s[0:1, 0:128], dtb.ap()[:, :], inc="cld")
    p.dma("gpsimd", f32s[0:1, 128:256], alog.ap()[:, :], inc="cld")
    p.dma("gpsimd", f32s[0:1, 256:384], dpr.ap()[:, :], inc="cld")
    p.dma("gpsimd", f32s[0:1, 384:392], brow.ap()[:, :], inc="cld")
    p.dma("gpsimd", f32s[0:32, 392:520], normw.ap()[:, :], inc="cld")
    p.dma("gpsimd", f32s[0:64, 520:648], gnw.ap()[:, :], inc="cld")
    p.dma("gpsimd", f32s[0:80, 648:776], convb.ap()[:, :], inc="cld")
    p.dma("gpsimd", a_bfd.ap()[:, :], ffa.ap()[:, :], inc="ffnbf")
    for hh in range(NG):
        p.dma("gpsimd", b_bfd.ap()[:, hh * 16:(hh + 1) * 16], ffb.ap()[hh], inc="ffnbf")
    p.dma("gpsimd", wr_bfd.ap()[:, :], wrr.ap()[:, :], inc="ffnbf")
    CLD = p.val("cld")
    FFNBF = p.val("ffnbf")

    p.op("scalar", lambda s: s.activation(f32s[0:1, 776:904], f32s[0:1, 128:256], AF.Exp),
         waits=[("cld", CLD)], inc="p0a")
    p.op("vector", lambda v: v.tensor_scalar_mul(f32s[0:1, 776:904], f32s[0:1, 776:904], -1.0),
         waits=[("p0a", p.val("p0a"))], inc="p0a")
    P0A = p.val("p0a")

    def small_tr(src_ap, dst_ap, ncol):
        p.op("tensor", lambda t: t.transpose(ps[:, 0:ncol], src_ap, r_idf[0:ncol, 0:ncol]),
             waits=[("cld", CLD), ("p0ev", p.val("p0ev"))], inc="p0tr")
        p.op("vector", lambda v: v.tensor_copy(dst_ap, ps[:, 0:ncol]),
             waits=[("p0tr", p.val("p0tr"))], inc="p0ev")

    small_tr(f32s[0:32, 392:520], r_normwT[:, :], 32)
    small_tr(f32s[0:64, 520:648], r_gnwT[:, :], 64)
    small_tr(f32s[0:80, 648:776], r_convb[:, :], 80)

    def row_bcast(row_ap, dst_ap, n, extra=()):
        p.op("tensor", lambda t: t.matmul(ps[:, 512:512 + n], r_ones1f[0:1, 0:128],
                                          row_ap, start=True, stop=True),
             waits=[("cld", CLD), ("p0ev", p.val("p0ev"))] + list(extra), inc="p0tr")
        p.op("vector", lambda v: v.tensor_copy(dst_ap, ps[:, 512:512 + n]),
             waits=[("p0tr", p.val("p0tr"))], inc="p0ev")

    row_bcast(f32s[0:1, 0:128], r_biasbc[:, :], 128)
    row_bcast(f32s[0:1, 776:904], r_abc[:, :], 128, extra=[("p0a", P0A)])
    row_bcast(f32s[0:1, 256:384], r_dpbc[:, :], 128)
    p.op("tensor", lambda t: t.transpose(ps[0:8, 0:1], f32s[0:1, 384:392], r_idf[0:1, 0:1]),
         waits=[("cld", CLD), ("p0ev", p.val("p0ev"))], inc="p0tr")
    p.op("vector", lambda v: v.tensor_copy(r_brcol[:, :], ps[0:8, 0:1]),
         waits=[("p0tr", p.val("p0tr"))], inc="p0ev")

    at_t = slab3[:, 0:4096]
    bt_t = slab3[:, 4096:8192]
    wrt_t = slab3[:, 8192:8448]

    def build_ffn_tiles(tag):
        p.dma("sync", wtmp0[:, 0:2048], a_bfd.ap()[:, 0:2048], waits=[("ffnbf", FFNBF)], inc=tag + "ld")
        p.dma("sync", wtmp1[:, 0:2048], a_bfd.ap()[:, 2048:4096], waits=[("ffnbf", FFNBF)], inc=tag + "ld")
        p.dma("sync", v3(wbuf1[:, 0:4096], 32, 128),
              b_bfd.ap().rearrange("(c p) q -> p c q", p=128), waits=[("ffnbf", FFNBF)], inc=tag + "ld")
        ldv = p.val(tag + "ld")
        for kc in range(32):
            src = (wtmp0 if kc < 16 else wtmp1)[:, (kc % 16) * 128:(kc % 16) * 128 + 128]
            bank = 1024 + (kc % 2) * 512
            p.op("tensor", lambda t, src=src, bank=bank: t.transpose(psbf(bank, 256)[:, 0:128], src, r_idbf[:, :]),
                 waits=[(tag + "ld", ldv), (tag + "ev", max(0, p.val(tag + "ev") - 1))], inc=tag + "tr")
            p.op("vector", lambda v, kc=kc, bank=bank: v.tensor_copy(
                at_t[:, kc * 128:(kc + 1) * 128], psbf(bank, 256)[:, 0:128]),
                waits=[(tag + "tr", p.val(tag + "tr"))], inc=tag + "ev")
        for oc in range(32):
            src = wbuf1[:, oc * 128:(oc + 1) * 128]
            bank = 1024 + (oc % 2) * 512
            p.op("tensor", lambda t, src=src, bank=bank: t.transpose(psbf(bank, 256)[:, 0:128], src, r_idbf[:, :]),
                 waits=[(tag + "ld", ldv), (tag + "ev", max(0, p.val(tag + "ev") - 1))], inc=tag + "tr")
            p.op("vector", lambda v, oc=oc, bank=bank: v.tensor_copy(
                bt_t[:, oc * 128:(oc + 1) * 128], psbf(bank, 256)[:, 0:128]),
                waits=[(tag + "tr", p.val(tag + "tr"))], inc=tag + "ev")
        p.dma("sync", wtmp0[0:8, 0:2048], wr_bfd.ap()[:, 0:2048],
              waits=[(tag + "ev", p.val(tag + "ev"))], inc=tag + "ld")
        p.dma("sync", wtmp1[0:8, 0:2048], wr_bfd.ap()[:, 2048:4096],
              waits=[(tag + "ev", p.val(tag + "ev"))], inc=tag + "ld")
        ldv = p.val(tag + "ld")
        for kc in range(32):
            src = (wtmp0 if kc < 16 else wtmp1)[0:8, (kc % 16) * 128:(kc % 16) * 128 + 128]
            bank = 1024 + (kc % 2) * 512
            p.op("tensor", lambda t, src=src, bank=bank: t.transpose(psbf(bank, 16)[:, 0:8], src, r_idbf[0:8, 0:8]),
                 waits=[(tag + "ld", ldv), (tag + "ev", max(0, p.val(tag + "ev") - 1))], inc=tag + "tr")
            p.op("vector", lambda v, kc=kc, bank=bank: v.tensor_copy(
                wrt_t[:, kc * 8:(kc + 1) * 8], psbf(bank, 16)[:, 0:8]),
                waits=[(tag + "tr", p.val(tag + "tr"))], inc=tag + "ev")
        return p.val(tag + "ev")

    FFNW1 = build_ffn_tiles("fw1")

    wp_spans = [(i * 2048, 2048) for i in range(9)] + [(18432, 128)]
    span_order = [4, 5, 6, 7, 8, 9, 0, 1, 2, 3]
    wp_cast_at = {}
    w2_cast_at = {}
    _cast_jobs = [("wp", si) for si in span_order] + [("w2", 0), ("w2", 1)]

    def emit_cast():
        if not _cast_jobs:
            return
        kind, si = _cast_jobs.pop(0)
        if kind == "wp":
            o0, olen = wp_spans[si]
            p.dma("gpsimd", wp_bf.ap()[o0:o0 + olen, :], wp.ap()[o0:o0 + olen, :], inc="wpc")
            wp_cast_at[si] = p.val("wpc")
        else:
            m0 = si * 2048
            p.dma("gpsimd", w2_bf.ap()[m0:m0 + 2048, :], w2.ap()[m0:m0 + 2048, :], inc="w2c")
            w2_cast_at[si] = p.val("w2c")
    emit_cast()
    emit_cast()
    emit_cast()

    blk_ready = {}
    w2blk_ready = {}
    _wtp_cnt = [0]
    _build_jobs = []
    for si in span_order:
        o0, olen = wp_spans[si]
        for kc in range(NKC):
            _build_jobs.append(("wp", si, kc))
    for si in (0, 1):
        for kc in range(NIC):
            _build_jobs.append(("w2", si, kc))

    def emit_build_step():
        if not _build_jobs:
            return
        kind, si, kc = _build_jobs.pop(0)
        buf_full = wtmp0 if _wtp_cnt[0] % 2 == 0 else wtmp1
        _wtp_cnt[0] += 1
        if kind == "wp":
            while si not in wp_cast_at:
                emit_cast()
            o0, olen = wp_spans[si]
            nob_span = olen // 128
            buf = buf_full[:, 0:olen]
            p.dma("sync", buf, wp_bf.ap()[o0:o0 + olen, kc * 128:(kc + 1) * 128],
                  waits=[("wpc", wp_cast_at[si]), ("wblk", p.val("wblk")),
                         ("w2blk", p.val("w2blk")), ("fw1ev", FFNW1)],
                  inc="wtp", transpose=True)
            tpv = p.val("wtp")
            dst = wpt_blk.ap()[o0 // 128:o0 // 128 + nob_span, :, kc, :].rearrange("b d o -> d b o")
            p.dma("sync", dst, v3(buf, nob_span, 128), waits=[("wtp", tpv)], inc="wblk")
            if kc == NKC - 1:
                for ob in range(o0 // 128, o0 // 128 + nob_span):
                    blk_ready[ob] = p.val("wblk")
        else:
            while si not in w2_cast_at:
                emit_cast()
            m0 = si * 2048
            buf = buf_full[:, 0:2048]
            p.dma("sync", buf, w2_bf.ap()[m0:m0 + 2048, kc * 128:(kc + 1) * 128],
                  waits=[("w2c", w2_cast_at[si]), ("w2gn", max(0, p.val("w2gn") - 1)),
                         ("wblk", p.val("wblk")), ("w2blk", p.val("w2blk"))],
                  inc="wtp", transpose=True)
            tpv = p.val("wtp")
            p.op("vector", lambda v, buf=buf, kc=kc: v.tensor_scalar_mul(buf, buf, r_gnwT[:, kc:kc + 1]),
                 waits=[("wtp", tpv), ("p0ev", p.val("p0ev"))], inc="w2gn")
            gv = p.val("w2gn")
            dst = w2t_blk.ap()[m0 // 128:m0 // 128 + 16, :, kc, :].rearrange("b d o -> d b o")
            p.dma("sync", dst, v3(buf, 16, 128), waits=[("w2gn", gv)], inc="w2blk")
            if kc == NIC - 1:
                for mb in range(m0 // 128, m0 // 128 + 16):
                    w2blk_ready[mb] = p.val("w2blk")

    for _ in range(5 * NKC):
        emit_build_step()

    h_t = slab1[:, :]
    TT = 256
    NT1 = S // TT
    t_x = slab2[:, 0:8192]
    t_ln = slab2[:, 8192:16384]

    R_SUM, R_SQ, R_M, R_RS, R_T1, R_S2 = 0, 256, 512, 768, 904, 904

    def ln_stats(tag, src_tile, xv, guard):
        p.op("scalar", lambda s: s.activation(t_ln[:, :], src_tile, AF.Square),
             waits=[(tag + "xld", xv)] + list(guard), inc=tag + "sq")
        sqv = p.val(tag + "sq")
        for kc in range(NKC):
            p.op("tensor", lambda t, kc=kc: t.matmul(
                ps[0:1, 2048:2048 + TT], r_onescolb[:, :], src_tile[:, kc * TT:(kc + 1) * TT],
                start=(kc == 0), stop=(kc == NKC - 1)),
                waits=([(tag + "xld", xv)] + list(guard)) if kc == 0 else (),
                inc=None if kc < NKC - 1 else tag + "st")
        for kc in range(NKC):
            p.op("tensor", lambda t, kc=kc: t.matmul(
                ps[0:1, 2560:2560 + TT], r_onescolb[:, :], t_ln[:, kc * TT:(kc + 1) * TT],
                start=(kc == 0), stop=(kc == NKC - 1)),
                waits=[(tag + "sq", sqv)] if kc == 0 else (),
                inc=None if kc < NKC - 1 else tag + "st")
        stv = p.val(tag + "st")
        r = f32s[0:1, :]
        p.op("vector", lambda v: v.tensor_scalar_mul(r[:, R_M:R_M + TT], ps[0:1, 2048:2048 + TT], 1.0 / D),
             waits=[(tag + "st", stv)], inc=tag + "r")
        p.op("vector", lambda v: v.tensor_scalar_mul(r[:, R_SQ:R_SQ + TT], ps[0:1, 2560:2560 + TT], 1.0 / D),
             inc=tag + "r")
        p.op("vector", lambda v: v.tensor_tensor(r[:, R_SUM:R_SUM + TT], r[:, R_M:R_M + TT], r[:, R_M:R_M + TT], ALU.mult),
             waits=[(tag + "r", p.val(tag + "r"))], inc=tag + "r")
        p.op("vector", lambda v: v.tensor_tensor(r[:, R_SQ:R_SQ + TT], r[:, R_SQ:R_SQ + TT], r[:, R_SUM:R_SUM + TT], ALU.subtract),
             waits=[(tag + "r", p.val(tag + "r"))], inc=tag + "r")
        p.op("vector", lambda v: v.tensor_scalar_add(r[:, R_RS:R_RS + TT], r[:, R_SQ:R_SQ + TT], EPS),
             waits=[(tag + "r", p.val(tag + "r"))], inc=tag + "r")
        p.op("scalar", lambda s: s.activation(r[:, R_SM:R_SM + TT], r[:, R_RS:R_RS + TT], AF.Ln),
             waits=[(tag + "r", p.val(tag + "r"))], inc=tag + "r")
        p.op("scalar", lambda s: s.activation(r[:, R_RS:R_RS + TT], r[:, R_SM:R_SM + TT], AF.Exp, scale=-0.5),
             inc=tag + "r")
        return p.val(tag + "r")

    def bc_row(row_ap, dst_c, n, waits, tag):
        p.op("tensor", lambda t: t.matmul(ps[:, dst_c:dst_c + n], r_ones1f[0:1, 0:1],
                                          row_ap, start=True, stop=True),
             waits=waits, inc=tag)
        return p.val(tag)

    h_t = slab1[:, :]
    TT = 256
    NT1 = S // TT
    t_x = slab2[:, 0:8192]
    t_ln = slab2[:, 8192:16384]
    R_SM, R_SQ, R_M, R_RS = 0, 256, 512, 768
    P0EV = p.val("p0ev")
    P0TR = p.val("p0tr")

    def ln_stats(tag, src_tile, xwaits):
        p.op("scalar", lambda s: s.activation(t_ln[:, :], src_tile, AF.Square),
             waits=list(xwaits) + [(tag + "r", p.val(tag + "r"))], inc=tag + "sq")
        sqv = p.val(tag + "sq")
        for kc in range(NKC):
            p.op("tensor", lambda t, kc=kc: t.matmul(
                ps[0:1, 2048:2048 + TT], r_onescolb[:, :], src_tile[:, kc * TT:(kc + 1) * TT],
                start=(kc == 0), stop=(kc == NKC - 1)),
                waits=(list(xwaits) + [(tag + "r", p.val(tag + "r"))]) if kc == 0 else (),
                inc=None if kc < NKC - 1 else tag + "st")
        for kc in range(NKC):
            p.op("tensor", lambda t, kc=kc: t.matmul(
                ps[0:1, 2560:2560 + TT], r_onescolb[:, :], t_ln[:, kc * TT:(kc + 1) * TT],
                start=(kc == 0), stop=(kc == NKC - 1)),
                waits=[(tag + "sq", sqv)] if kc == 0 else (),
                inc=None if kc < NKC - 1 else tag + "st")
        stv = p.val(tag + "st")
        r = f32s[0:1, :]
        p.op("vector", lambda v: v.tensor_scalar_mul(r[:, R_M:R_M + TT], ps[0:1, 2048:2048 + TT], 1.0 / D),
             waits=[(tag + "st", stv)], inc=tag + "r")
        p.op("vector", lambda v: v.tensor_scalar_mul(r[:, R_SQ:R_SQ + TT], ps[0:1, 2560:2560 + TT], 1.0 / D),
             inc=tag + "r")
        p.op("vector", lambda v: v.tensor_tensor(r[:, R_SM:R_SM + TT], r[:, R_M:R_M + TT], r[:, R_M:R_M + TT], ALU.mult),
             waits=[(tag + "r", p.val(tag + "r"))], inc=tag + "r")
        p.op("vector", lambda v: v.tensor_tensor(r[:, R_SQ:R_SQ + TT], r[:, R_SQ:R_SQ + TT], r[:, R_SM:R_SM + TT], ALU.subtract),
             waits=[(tag + "r", p.val(tag + "r"))], inc=tag + "r")
        p.op("vector", lambda v: v.tensor_scalar_add(r[:, R_RS:R_RS + TT], r[:, R_SQ:R_SQ + TT], EPS),
             waits=[(tag + "r", p.val(tag + "r"))], inc=tag + "r")
        p.op("scalar", lambda s: s.activation(r[:, R_SM:R_SM + TT], r[:, R_RS:R_RS + TT], AF.Ln),
             waits=[(tag + "r", p.val(tag + "r"))], inc=tag + "r")
        p.op("scalar", lambda s: s.activation(r[:, R_RS:R_RS + TT], r[:, R_SM:R_SM + TT], AF.Exp, scale=-0.5),
             inc=tag + "r")
        return p.val(tag + "r")

    SMF = slab2[:, 14464:15488].bitcast(F32)
    SMB = slab2

    def ffn_tile(tag, ti, lnv, x_tile, out_writer, extra_first=()):
        for kc in range(NKC):
            p.op("tensor", lambda t, kc=kc: t.matmul(
                ps[:, 0:TT], at_t[:, kc * 128:(kc + 1) * 128], t_ln[:, kc * TT:(kc + 1) * TT],
                start=(kc == 0), stop=(kc == NKC - 1)),
                waits=([(tag + "ln", lnv), (tag + "loev", p.val(tag + "loev"))] + list(extra_first)) if kc == 0 else (),
                inc=None if kc < NKC - 1 else tag + "lo")
        for kc in range(NKC):
            p.op("tensor", lambda t, kc=kc: t.matmul(
                ps[0:8, 512:512 + TT], wrt_t[:, kc * 8:(kc + 1) * 8], t_ln[:, kc * TT:(kc + 1) * TT],
                start=(kc == 0), stop=(kc == NKC - 1)),
                waits=[(tag + "ls", p.val(tag + "ls"))] if kc == 0 else (),
                inc=None if kc < NKC - 1 else tag + "lo")
        lov = p.val(tag + "lo")
        p.op("scalar", lambda s: s.activation(SMF[0:8, 0:TT], ps[0:8, 512:512 + TT], AF.Copy),
             waits=[(tag + "lo", lov), (tag + "sb", p.val(tag + "sb"))], inc=tag + "sm")
        p.op("vector", lambda v: v.tensor_scalar_add(SMF[0:8, 0:TT], SMF[0:8, 0:TT], r_brcol[:, :]),
             waits=[(tag + "sm", p.val(tag + "sm"))], inc=tag + "sm")
        p.op("tensor", lambda t: t.matmul(ps[0:1, 2304:2304 + TT], r_ones8f[:, :], SMF[0:8, 0:TT],
                                          start=True, stop=True),
             waits=[(tag + "sm", p.val(tag + "sm"))], inc=tag + "smm")
        p.op("vector", lambda v: v.tensor_copy(f32s[0:1, R_SM:R_SM + TT], ps[0:1, 2304:2304 + TT]),
             waits=[(tag + "smm", p.val(tag + "smm"))], inc=tag + "sm")
        p.op("tensor", lambda t: t.matmul(ps[0:8, 2304:2304 + TT], r_ones1f[0:1, 0:8], f32s[0:1, R_SM:R_SM + TT],
                                          start=True, stop=True),
             waits=[(tag + "sm", p.val(tag + "sm"))], inc=tag + "smm")
        p.op("vector", lambda v: v.scalar_tensor_tensor(
            SMF[0:8, 256:256 + TT], ps[0:8, 2304:2304 + TT], -0.125, SMF[0:8, 0:TT],
            ALU.mult, ALU.add),
            waits=[(tag + "smm", p.val(tag + "smm"))], inc=tag + "sm")
        p.op("scalar", lambda s: s.activation(SMF[0:8, 0:TT], SMF[0:8, 256:256 + TT], AF.Exp),
             waits=[(tag + "sm", p.val(tag + "sm"))], inc=tag + "sm")
        p.op("tensor", lambda t: t.matmul(ps[0:1, 2304:2304 + TT], r_ones8f[:, :], SMF[0:8, 0:TT],
                                          start=True, stop=True),
             waits=[(tag + "sm", p.val(tag + "sm"))], inc=tag + "smm")
        p.op("vector", lambda v: v.tensor_copy(f32s[0:1, R_SM:R_SM + TT], ps[0:1, 2304:2304 + TT]),
             waits=[(tag + "smm", p.val(tag + "smm"))], inc=tag + "sm")
        p.op("vector", lambda v: v.reciprocal(f32s[0:1, R_SM:R_SM + TT], f32s[0:1, R_SM:R_SM + TT]),
             inc=tag + "sm")
        p.op("tensor", lambda t: t.matmul(ps[0:8, 2304:2304 + TT], r_ones1f[0:1, 0:8], f32s[0:1, R_SM:R_SM + TT],
                                          start=True, stop=True),
             waits=[(tag + "sm", p.val(tag + "sm"))], inc=tag + "smm")
        sc_bf = SMB[0:8, 14976:14976 + TT]
        p.op("vector", lambda v: v.scalar_tensor_tensor(
            sc_bf, ps[0:8, 2304:2304 + TT], 8.0, SMF[0:8, 0:TT], ALU.mult, ALU.mult),
            waits=[(tag + "smm", p.val(tag + "smm"))], inc=tag + "sm")
        smv = p.val(tag + "sm")
        p.op("tensor", lambda t: t.matmul(ps[:, 512:512 + TT], r_e8[:, :], sc_bf, start=True, stop=True),
             waits=[(tag + "sm", smv)], inc=tag + "sb")
        sbv = p.val(tag + "sb")
        low_sb = t_ln[:, 0:TT]
        low_s = t_ln[:, TT:2 * TT]
        p.op("scalar", lambda s: s.activation(low_sb, ps[:, 0:TT], AF.Copy),
             waits=[(tag + "lo", lov)], inc=tag + "loev")
        p.op("vector", lambda v: v.tensor_tensor(low_s, low_sb, ps[:, 512:512 + TT], ALU.mult),
             waits=[(tag + "sb", sbv), (tag + "loev", p.val(tag + "loev"))], inc=tag + "ls")
        lsv = p.val(tag + "ls")
        for kc in range(NKC):
            bank = 1024 + (kc % 2) * 512
            p.op("tensor", lambda t, kc=kc, bank=bank: t.matmul(
                ps[:, bank:bank + TT], bt_t[:, kc * 128:(kc + 1) * 128], low_s,
                start=True, stop=True),
                waits=[(tag + "ls", lsv)] + ([(tag + "oev", p.val(tag + "oev") - 1)] if kc >= 2 else []),
                inc=tag + "o")
            out_writer(kc, ps[:, bank:bank + TT], [(tag + "o", p.val(tag + "o"))])
        return p.val(tag + "oev")

    hdump = []
    for ti in range(NT1):
        emit_cast()
        if ti >= 4:
            emit_cast()
        for _ in range(4):
            emit_build_step()
        xw = [("p2ar2", p.val("p2ar2"))]
        if ti == 0:
            xw += [("p0ev", P0EV), ("fw1ev", FFNW1)]
        for pc in range(4):
            p.dma("gpsimd",
                  v3(t_x[:, pc * 2048:(pc + 1) * 2048], 8, TT),
                  xT.ap().rearrange("(kc q) t -> q kc t", q=128)[:, pc * 8:(pc + 1) * 8, ti * TT:(ti + 1) * TT],
                  waits=xw, inc="f1xld")
        xv = p.val("f1xld")
        rowv = ln_stats("f1", t_x[:, :], [("f1xld", xv), ("p2ast2", p.val("p2ast2"))])
        p.op("tensor", lambda t: t.matmul(ps[:, 3072:3072 + TT], r_ones1f[0:1, 0:128],
                                          f32s[0:1, R_M:R_M + TT], start=True, stop=True),
             waits=[("f1r", rowv), ("f1ln", p.val("f1ln")), ("p2aln2", p.val("p2aln2"))], inc="f1b")
        p.op("tensor", lambda t: t.matmul(ps[:, 3328:3328 + TT], r_ones1f[0:1, 0:128],
                                          f32s[0:1, R_RS:R_RS + TT], start=True, stop=True),
             inc="f1b")
        bv = p.val("f1b")
        p.op("vector", lambda v: v.tensor_tensor(
            v3(t_ln[:, :], 32, TT), v3(t_x[:, :], 32, TT),
            rowb(ps[:, 3072:3072 + TT], 32, TT), ALU.subtract),
            waits=[("f1b", bv), ("f1sq", p.val("f1sq"))], inc="f1ln")
        p.op("vector", lambda v: v.tensor_tensor(
            v3(t_ln[:, :], 32, TT), v3(t_ln[:, :], 32, TT),
            rowb(ps[:, 3328:3328 + TT], 32, TT), ALU.mult),
            waits=[("f1ln", p.val("f1ln"))], inc="f1ln")
        lnv = p.val("f1ln")

        def x1_writer(kc, ps_ap, waits):
            p.op("vector", lambda v, kc=kc, ps_ap=ps_ap: v.tensor_tensor(
                t_x[:, kc * TT:(kc + 1) * TT], ps_ap, t_x[:, kc * TT:(kc + 1) * TT], ALU.add),
                waits=waits, inc="f1oev")

        f1ov = ffn_tile("f1", ti, lnv, t_x, x1_writer)
        if dodbg["x1"]:
            for pc in range(4):
                p.dma("sync", dodbg["x1"].ap().rearrange("(kc q) t -> q kc t", q=128)[:, pc * 8:(pc + 1) * 8, ti * TT:(ti + 1) * TT],
                      v3(t_x[:, pc * 2048:(pc + 1) * 2048], 8, TT),
                      waits=[("f1oev", f1ov)], inc="dbgst")
        rowv = ln_stats("p2a", t_x[:, :], [("f1oev", f1ov)])
        r = f32s[0:1, :]
        SCRQ = f32b[0:1, 0:TT]
        p.op("vector", lambda v: v.tensor_scalar_add(SCRQ, r[:, R_SQ:R_SQ + TT], EPS),
             waits=[("p2ar", rowv), ("p3dt", p.val("p3dt"))], inc="p2ar")
        p.op("vector", lambda v: v.reciprocal(r[:, R_SM:R_SM + TT], SCRQ),
             inc="p2ar")
        p.op("vector", lambda v: v.tensor_tensor(SCRQ, r[:, R_SM:R_SM + TT], r[:, R_SQ:R_SQ + TT], ALU.mult),
             waits=[("p2ar", p.val("p2ar"))], inc="p2ar")
        p.op("vector", lambda v: v.tensor_scalar_add(r[:, R_SM:R_SM + TT], SCRQ, EPS),
             waits=[("p2ar", p.val("p2ar"))], inc="p2ar")
        p.op("scalar", lambda s: s.activation(SCRQ, r[:, R_SM:R_SM + TT], AF.Ln),
             waits=[("p2ar", p.val("p2ar"))], inc="p2ar")
        p.op("scalar", lambda s: s.activation(r[:, R_SM:R_SM + TT], SCRQ, AF.Exp, scale=-0.5),
             inc="p2ar")
        p.op("vector", lambda v: v.tensor_tensor(r[:, R_SM:R_SM + TT], r[:, R_SM:R_SM + TT], r[:, R_RS:R_RS + TT], ALU.mult),
             waits=[("p2ar", p.val("p2ar"))], inc="p2ar")
        rowv = p.val("p2ar")
        p.op("tensor", lambda t: t.matmul(ps[:, 3072:3072 + TT], r_ones1f[0:1, 0:128],
                                          r[:, R_M:R_M + TT], start=True, stop=True),
             waits=[("p2ar", rowv), ("p2au", p.val("p2au")), ("p2aln2", p.val("p2aln2"))],
             inc="p2ab")
        p.op("tensor", lambda t: t.matmul(ps[:, 3328:3328 + TT], r_ones1f[0:1, 0:128],
                                          r[:, R_RS:R_RS + TT], start=True, stop=True),
             inc="p2ab")
        p.op("tensor", lambda t: t.matmul(ps[:, 3584:3584 + TT], r_ones1f[0:1, 0:128],
                                          r[:, R_SM:R_SM + TT], start=True, stop=True),
             waits=[("p2ah", p.val("p2ah"))], inc="p2ab")
        bv = p.val("p2ab")
        p.op("vector", lambda v: v.tensor_tensor(
            v3(t_ln[:, :], 32, TT), v3(t_x[:, :], 32, TT),
            rowb(ps[:, 3072:3072 + TT], 32, TT), ALU.subtract),
            waits=[("p2ab", bv), ("p2asq", p.val("p2asq"))], inc="p2au")
        uv = p.val("p2au")
        hview = v3(h_t, 32, 2048)[:, :, ti * TT:(ti + 1) * TT]
        p.op("vector", lambda v, hview=hview: v.tensor_tensor(
            hview, v3(t_ln[:, :], 32, TT), rowb(ps[:, 3584:3584 + TT], 32, TT), ALU.mult),
            waits=[("p2au", uv)], inc="p2ah")
        for kc in range(NKC):
            p.op("vector", lambda v, kc=kc, ti=ti: v.tensor_scalar_mul(
                h_t[:, kc * 2048 + ti * TT: kc * 2048 + (ti + 1) * TT],
                h_t[:, kc * 2048 + ti * TT: kc * 2048 + (ti + 1) * TT],
                r_normwT[:, kc:kc + 1]),
                waits=[("p2ah", p.val("p2ah"))] if kc == 0 else (), inc="p2ahn")
        hnv = p.val("p2ahn")
        p.op("vector", lambda v: v.tensor_tensor(
            v3(t_ln[:, :], 32, TT), v3(t_ln[:, :], 32, TT),
            rowb(ps[:, 3328:3328 + TT], 32, TT), ALU.mult),
            waits=[("p2ah", p.val("p2ah"))], inc="p2aln2")
        p.op("vector", lambda v: v.tensor_tensor(
            v3(t_ln[:, :], 32, TT), v3(t_ln[:, :], 32, TT), v3(t_x[:, :], 32, TT), ALU.add),
            waits=[("p2aln2", p.val("p2aln2"))], inc="p2ar2")
        r2v = p.val("p2ar2")
        for pc in range(4):
            p.dma("sync", r2_d.ap().rearrange("(kc q) t -> q kc t", q=128)[:, pc * 8:(pc + 1) * 8, ti * TT:(ti + 1) * TT],
                  v3(t_ln[:, pc * 2048:(pc + 1) * 2048], 8, TT),
                  waits=[("p2ar2", r2v)], inc="p2ast2")
        if dodbg["r2"]:
            for pc in range(4):
                p.dma("sync", dodbg["r2"].ap().rearrange("(kc q) t -> q kc t", q=128)[:, pc * 8:(pc + 1) * 8, ti * TT:(ti + 1) * TT],
                      v3(t_ln[:, pc * 2048:(pc + 1) * 2048], 8, TT),
                      waits=[("p2ar2", r2v)], inc="dbgst")
    HREADY = p.val("p2ahn")
    if dodbg["h"]:
        p.dma("sync", dodbg["h"].ap().rearrange("(kc q) t -> q kc t", q=128),
              v3(h_t[:, :], 32, 2048), waits=[("p2ahn", HREADY)], inc="dbgst")

    if "p2b" in PHASES:
        ob_order = list(range(OB_HBC0, NOB)) + list(range(0, OB_HBC0))
        t_pre0 = slab2[:, 0:2051]
        t_pre1 = slab2[:, 2051:4102]
        t_acc = slab2[:, 4102:6150]
        t_st0 = slab2[:, 6150:8198]
        t_st1 = slab2[:, 8198:10246]
        wbufs = (wbuf0, wbuf1)
        p2b_store_done = {}
        p.op("vector", lambda v: v.memset(t_pre0[:, 0:3], 0.0), waits=[("p2ast2", p.val("p2ast2"))], inc="cpad")
        p.op("vector", lambda v: v.memset(t_pre1[:, 0:3], 0.0), inc="cpad")
        CPAD = p.val("cpad")
        for i, ob in enumerate(ob_order):
            par = i % 2
            wb = wbufs[par]
            is_hbc = OB_HBC0 <= ob < OB_HBC0 + CONVD // 128
            is_dt = ob == NOB - 1
            wlsem = "p2bwlA" if par == 0 else "p2bwlB"
            for _ in range(2):
                emit_build_step()
            p.dma("sync", v3(wb, 32, 128), wpt_blk.ap()[ob],
                  waits=[("wblk", blk_ready[ob]), ("p2bmm", max(0, p.val("p2bmm") - 1))],
                  inc=wlsem)
            wlv = (wlsem, p.val(wlsem))
            for k in range(NKC):
                for t4 in range(4):
                    w8 = []
                    if k == 0 and t4 == 0:
                        w8 = [wlv, ("p2ahn", HREADY),
                              ("p2bev", max(0, p.val("p2bev") - 4))]
                    p.op("tensor", lambda t, k=k, t4=t4, par=par, wb=wb: t.matmul(
                        ps[:, par * 2048 + t4 * 512: par * 2048 + t4 * 512 + 512],
                        wb[:, k * 128:(k + 1) * 128],
                        h_t[:, k * 2048 + t4 * 512: k * 2048 + (t4 + 1) * 512],
                        start=(k == 0), stop=(k == NKC - 1)),
                        waits=w8,
                        inc=("p2bmm" if (k == NKC - 1 and t4 == 3) else None))
            mmv = p.val("p2bmm")
            dest = (t_pre0 if par == 0 else t_pre1) if is_hbc else (t_st0 if par == 0 else t_st1)
            off = 3 if is_hbc else 0
            guard = ("p2bcv", p.val("p2bcv")) if is_hbc else ("p2bst", max(0, p.val("p2bst") - 16))
            for t4 in range(4):
                p.op("scalar", lambda s, t4=t4, par=par, dest=dest, off=off: s.activation(
                    dest[:, off + t4 * 512: off + (t4 + 1) * 512],
                    ps[:, par * 2048 + t4 * 512: par * 2048 + t4 * 512 + 512], AF.Copy),
                    waits=[("p2bmm", mmv), guard, ("cpad", CPAD)] if t4 == 0 else [("p2bmm", mmv)],
                    inc="p2bev")
            evv = p.val("p2bev")
            if is_hbc:
                cb = ob - OB_HBC0
                pre = t_pre0 if par == 0 else t_pre1
                st = t_st0 if par == 0 else t_st1
                p.op("vector", lambda v, pre=pre, cb=cb: v.tensor_scalar_mul(
                    t_acc[:, 0:2048], pre[:, 0:2048], r_convw[:, 4 * cb:4 * cb + 1]),
                    waits=[("p2bev", evv), ("p2bsl", p.val("p2bsl"))], inc="p2bcv")
                for tap in range(1, 4):
                    p.op("vector", lambda v, pre=pre, cb=cb, tap=tap: v.scalar_tensor_tensor(
                        t_acc[:, 0:2048], pre[:, tap:2048 + tap], r_convw[:, 4 * cb + tap:4 * cb + tap + 1],
                        t_acc[:, 0:2048], ALU.mult, ALU.add),
                        waits=[("p2bcv", p.val("p2bcv"))], inc="p2bcv")
                cvv = p.val("p2bcv")
                p.op("scalar", lambda s, st=st, cb=cb: s.activation(
                    st[:, 0:2048], t_acc[:, 0:2048], AF.Silu, bias=r_convb[:, cb:cb + 1]),
                    waits=[("p2bcv", cvv), ("p2bst", max(0, p.val("p2bst") - 16))], inc="p2bsl")
                slv = p.val("p2bsl")
                p.dma("gpsimd", proj_d.ap()[ob * 128:(ob + 1) * 128, :], st[:, 0:2048],
                      waits=[("p2bsl", slv)], inc="p2bst")
            else:
                st = t_st0 if par == 0 else t_st1
                p.dma("gpsimd", proj_d.ap()[ob * 128:(ob + 1) * 128, :], st[:, 0:2048],
                      waits=[("p2bev", evv)], inc="p2bst")
            p2b_store_done[ob] = p.val("p2bst")
        PST_HBC = p2b_store_done[NOB - 1]
        PST_ALL = p.val("p2bst")
        if dodbg["proj"]:
            p.dma("sync", dodbg["proj"].ap()[:, :], proj_d.ap()[:, :],
                  waits=[("p2bst", PST_ALL)], inc="dbgst")

    if "p3" in PHASES:
        P2BMM = p.val("p2bmm")
        tileT = (slab1[:, 0:10368], slab1[:, 10368:20736])
        pstate = slab1[:, 20736:28928]
        hc_t = slab1[:, 28928:37120]
        hd2_t = slab1[:, 37120:45312]
        ytc_t = slab1[:, 45312:53504]
        ytmpA = slab1[:, 53504:54528]
        ytmpB = slab1[:, 54528:55552]
        ptmp_t = slab1[:, 55552:56576]
        ghat_t = slab1[:, 56576:57600]
        mhat_t = slab2[:, 0:16384]
        d0sb = (slab3[:, 0:2048], slab3[:, 2048:4096])
        cgf_t = slab3[:, 4096:5120]
        row_t = slab3[:, 5120:7168]
        sm_dtsp = slab3[:, 7168:7296]
        sm_w2 = slab3[:, 7296:7424]
        sm_decay = slab3[:, 7424:7552]
        sm_eacF = slab3[:, 7552:7680]
        sm_lam = slab3[:, 7680:7808]
        sm_eacs = slab3[:, 7808:7936]
        bgf_t = slab3[:, 7936:8192]

        p.op("vector", lambda v: v.memset(pstate[:, :], 0.0),
             waits=[("p2bmm", P2BMM)], inc="p3pupd", inc_n=8)
        for c in range(NCH):
            tl = tileT[c % 2]
            tlw = [("p2bst", PST_HBC), ("p2bmm", P2BMM),
                   ("p3ev2", max(0, (c - 1) * 8)), ("p3mmS", max(0, (c - 1) * 8)),
                   ("p3hdp", max(0, c - 1)),
                   ("p3dt", max(0, p.val("p3dt") - 7))]
            for piece in range(6):
                r0 = 8192 + piece * 1728
                p.dma("sync", tl[:, piece * 1728:(piece + 1) * 1728],
                      proj_d.ap()[r0:r0 + 1728, c * 128:(c + 1) * 128],
                      waits=tlw, inc="p3tl", transpose=True)
            tlv = p.val("p3tl")
            dtT = tl[:, 10240:10368]
            s1 = f32b[:, 0:128]
            scr = f32b[:, 128:256]
            acs_sb = f32s[:, 256:384]
            mx = f32s[:, 128:256]
            p.op("vector", lambda v, dtT=dtT: v.tensor_tensor(s1, dtT, r_biasbc[:, :], ALU.add),
                 waits=[("p3tl", tlv), ("p3acW", p.val("p3acW"))], inc="p3dt")
            p.op("scalar", lambda s: s.activation(scr, s1, AF.Abs),
                 waits=[("p3dt", p.val("p3dt")), ("p3mmT", p.val("p3mmT")),
                        ("p3mmA", p.val("p3mmA"))], inc="p3dta")
            p.op("scalar", lambda s: s.activation(scr, scr, AF.Exp, scale=-1.0), inc="p3dta")
            p.op("scalar", lambda s: s.activation(scr, scr, AF.Ln, bias=1.0), inc="p3dta")
            p.op("vector", lambda v: v.tensor_scalar_max(mx, s1, 0.0),
                 waits=[("p3dt", p.val("p3dt"))], inc="p3dt")
            p.op("vector", lambda v: v.tensor_tensor(s1, scr, mx, ALU.add),
                 waits=[("p3dta", p.val("p3dta")), ("p3dt", p.val("p3dt"))], inc="p3dt")
            p.op("vector", lambda v: v.tensor_copy(sm_dtsp[:, :], s1),
                 waits=[("p3dt", p.val("p3dt")), ("p3hc", p.val("p3hc")),
                        ("p3w2", p.val("p3w2"))], inc="p3dt")
            p.op("vector", lambda v: v.tensor_tensor(scr, s1, r_abc[:, :], ALU.mult),
                 waits=[("p3dt", p.val("p3dt"))], inc="p3acW")
            acv = p.val("p3acW")
            p.op("tensor", lambda t: t.matmul(ps[:, 2048:2176], r_l1[:, :], scr, start=True, stop=True),
                 waits=[("p3acW", acv), ("p3acs", p.val("p3acs")),
                        ("p3bc", p.val("p3bc"))] +
                       ([("p2bev", p.val("p2bev"))] if c == 0 else []),
                 inc="p3mmA")
            mmav = p.val("p3mmA")
            p.op("scalar", lambda s: s.activation(sm_eacs[:, :], ps[:, 2048:2176], AF.Exp),
                 waits=[("p3mmA", mmav), ("p3ev1", p.val("p3ev1"))], inc="p3acs")
            p.op("scalar", lambda s: s.activation(acs_sb, ps[:, 2048:2176], AF.Copy),
                 waits=[("p3mmA", mmav), ("p3mmL", p.val("p3mmL"))], inc="p3acs")
            acsv = p.val("p3acs")
            p.op("tensor", lambda t: t.matmul(ps[0:1, 3584:3712], r_e127[:, :], acs_sb, start=True, stop=True),
                 waits=[("p3acs", acsv), ("p3lam", p.val("p3lam"))], inc="p3mmL")
            p.op("scalar", lambda s: s.activation(f32s[0:1, 0:128], ps[0:1, 3584:3712], AF.Exp),
                 waits=[("p3mmL", p.val("p3mmL"))], inc="p3lam")
            p.op("tensor", lambda t: t.matmul(ps[:, 3584:3712], r_ones1f[0:1, 0:128], f32s[0:1, 0:128],
                                              start=True, stop=True),
                 waits=[("p3lam", p.val("p3lam"))], inc="p3mmL")
            p.op("vector", lambda v: v.tensor_copy(sm_lam[:, :], ps[:, 3584:3712]),
                 waits=[("p3mmL", p.val("p3mmL")), ("p3pt", p.val("p3pt"))], inc="p3lam")
            lamv = p.val("p3lam")
            p.op("tensor", lambda t: t.transpose(ps[:, 2560:2688], scr, r_idf[:, :]),
                 waits=[("p3acW", acv), ("p3eac", p.val("p3eac")),
                        ("p3bc", p.val("p3bc"))], inc="p3mmT")
            p.op("scalar", lambda s: s.activation(sm_eacF[:, :], ps[:, 2560:2688], AF.Exp),
                 waits=[("p3mmT", p.val("p3mmT")), ("p3row", p.val("p3row"))], inc="p3eac")
            p.op("vector", lambda v: v.memset(sm_eacF[:, 0:1], 0.0),
                 waits=[("p3eac", p.val("p3eac"))], inc="p3eac")
            eacv = p.val("p3eac")
            for g in range(NG):
                bank = 2304 if g % 2 == 0 else 2816
                p.op("tensor", lambda t, g=g, bank=bank, tl=tl: t.transpose(
                    psbf(bank, 256)[:, 0:128], tl[:, 8192 + g * 128:8192 + (g + 1) * 128], r_idbf[:, :]),
                    waits=[("p3tl", tlv), ("p3bc", p.val("p3bc")),
                           ("p3acs", acsv), ("p3eac", eacv)], inc="p3mmT")
                p.op("vector", lambda v, g=g, bank=bank: v.tensor_copy(
                    bgf_t[:, (g % 2) * 128:(g % 2) * 128 + 128], psbf(bank, 256)[:, 0:128]),
                    waits=[("p3mmT", p.val("p3mmT"))], inc="p3bc")
                p.op("tensor", lambda t, g=g, bank=bank, tl=tl: t.transpose(
                    psbf(bank + 128, 256)[:, 0:128], tl[:, 9216 + g * 128:9216 + (g + 1) * 128], r_idbf[:, :]),
                    waits=[("p3bc", p.val("p3bc"))], inc="p3mmT")
                p.op("vector", lambda v, g=g, bank=bank: v.tensor_copy(
                    cgf_t[:, g * 128:(g + 1) * 128], psbf(bank + 128, 256)[:, 0:128]),
                    waits=[("p3mmT", p.val("p3mmT"))], inc="p3bc")
                p.op("tensor", lambda t, g=g: t.matmul(
                    ps[:, 3072:3200], bgf_t[:, (g % 2) * 128:(g % 2) * 128 + 128],
                    cgf_t[:, g * 128:(g + 1) * 128], start=True, stop=True),
                    waits=[("p3bc", p.val("p3bc")), ("p3gh", p.val("p3gh"))], inc="p3mmG")
                p.op("vector", lambda v, g=g: v.tensor_tensor(
                    ghat_t[:, g * 128:(g + 1) * 128], ps[:, 3072:3200], r_u[:, :], ALU.mult),
                    waits=[("p3mmG", p.val("p3mmG")), ("p3mh", max(0, c))], inc="p3gh")
            ghv = p.val("p3gh")
            p.op("gpsimd", lambda g, tl=tl: g.tensor_tensor(
                v3(hc_t, NH, PH), v3(tl[:, 0:8192], NH, PH), colb(sm_dtsp[:, :], NH, PH), ALU.mult),
                waits=[("p3dt", p.val("p3dt")), ("p3mmYd", p.val("p3mmYd"))], inc="p3hc")
            for e in range(8):
                p.dma("sync", row_t[0:1, :].rearrange("q (a b) -> q a b", a=16),
                      sm_eacF[e * 16:(e + 1) * 16, :],
                      waits=[("p3eac", eacv), ("p3mmD", p.val("p3mmD"))], inc="p3row")
                rv = p.val("p3row")
                for q in range(4):
                    w8 = [("p3row", rv)]
                    if e == 0 and q == 0:
                        w8 += [("p3pupd", p.val("p3pupd")), ("p3ev2", p.val("p3ev2"))]
                    if q == 0:
                        w8 += [("p3d0", p.val("p3d0"))]
                    p.op("tensor", lambda t, q=q: t.matmul(
                        ps[:, q * 512:(q + 1) * 512], r_ones1b[:, :], row_t[0:1, q * 512:(q + 1) * 512],
                        start=True, stop=True),
                        waits=w8, inc=None if q < 3 else "p3mmD")
                dmv = p.val("p3mmD")
                p.op("scalar", lambda s, e=e: s.activation(d0sb[e % 2][:, :], ps[:, 0:2048], AF.Copy),
                     waits=[("p3mmD", dmv), ("p3scan", max(0, p.val("p3scan") - 1))], inc="p3d0")
                d0v = p.val("p3d0")
                p.op("vector", lambda v, e=e: v.tensor_tensor_scan(
                    mhat_t[:, e * 2048:(e + 1) * 2048], d0sb[e % 2][:, :], r_irep[:, :],
                    0.0, ALU.mult, ALU.add),
                    waits=[("p3d0", d0v), ("p3mmYd", p.val("p3mmYd"))], inc="p3scan")
                scv = p.val("p3scan")
            p.op("vector", lambda v: v.tensor_copy(
                sm_decay[:, :], v3(mhat_t[:, :], 128, 128)[:, :, 127]),
                waits=[("p3scan", p.val("p3scan"))], inc="p3dec")
            p.op("gpsimd", lambda g: g.tensor_tensor(
                mhat_t[:, :].rearrange("q (g r i) -> q g r i", g=8, i=128),
                mhat_t[:, :].rearrange("q (g r i) -> q g r i", g=8, i=128),
                ghat_t[:, :].rearrange("q (g i) -> q g i", g=8).unsqueeze(2).broadcast_to([128, 8, 16, 128]),
                ALU.mult),
                waits=[("p3scan", p.val("p3scan")), ("p3dec", p.val("p3dec")),
                       ("p3gh", ghv)], inc="p3mh")
            p.op("vector", lambda v: v.tensor_tensor(sm_w2[:, :], sm_dtsp[:, :], sm_decay[:, :], ALU.mult),
                 waits=[("p3dec", p.val("p3dec")), ("p3hd", p.val("p3hd"))], inc="p3w2")
            p.op("gpsimd", lambda g, tl=tl: g.tensor_tensor(
                v3(hd2_t, NH, PH), v3(tl[:, 0:8192], NH, PH), colb(sm_w2[:, :], NH, PH), ALU.mult),
                waits=[("p3w2", p.val("p3w2")), ("p3mmS", p.val("p3mmS"))], inc="p3hd")
            hdv = p.val("p3hd")
            p.op("gpsimd", lambda g, tl=tl: g.tensor_tensor(
                v3(tl[:, 0:8192], NH, PH), v3(tl[:, 0:8192], NH, PH),
                colb(r_dpbc[:, :], NH, PH), ALU.mult),
                waits=[("p3hd", hdv), ("p3hc", p.val("p3hc")),
                       ("p3mmYd", p.val("p3mmYd"))], inc="p3hdp")
            hdpv = p.val("p3hdp")
            for e in range(8):
                ytslice = ytc_t[:, e * 1024:(e + 1) * 1024]
                for q in range(2):
                    w8 = [("p3d0", p.val("p3d0"))] if (e == 0 and q == 0) else []
                    if q == 0:
                        w8 += [("p3bc", p.val("p3bc")), ("p3pupd", p.val("p3pupd"))]
                    p.op("tensor", lambda t, e=e, q=q: t.matmul(
                        ps[:, q * 512:(q + 1) * 512], cgf_t[:, e * 128:(e + 1) * 128],
                        pstate[:, e * 1024 + q * 512: e * 1024 + (q + 1) * 512],
                        start=True, stop=True),
                        waits=w8, inc=None if q == 0 else "p3mmY")
                ymv = p.val("p3mmY")
                p.op("vector", lambda v, e=e: v.tensor_tensor(
                    v3(ytmpA, 16, 64), v3(ps[:, 0:1024], 16, 64),
                    colb(sm_eacs[:, e * 16:(e + 1) * 16], 16, 64), ALU.mult),
                    waits=[("p3mmY", ymv), ("p3acs", acsv), ("p3yad", max(0, p.val("p3yad") - 1))],
                    inc="p3ev1")
                e1v = p.val("p3ev1")
                for hr in range(16):
                    w8 = []
                    if hr == 0:
                        w8 = [("p3mh", p.val("p3mh")), ("p3hc", p.val("p3hc")),
                              ("p3ev2", p.val("p3ev2"))]
                    p.op("tensor", lambda t, e=e, hr=hr: t.matmul(
                        ps[:, 1024 + hr * 64:1024 + (hr + 1) * 64],
                        mhat_t[:, (e * 16 + hr) * 128:(e * 16 + hr + 1) * 128],
                        hc_t[:, (e * 16 + hr) * 64:(e * 16 + hr + 1) * 64],
                        start=(hr % 8 == 0), stop=False),
                        waits=w8, inc=None)
                for q in range(2):
                    p.op("tensor", lambda t, e=e, q=q, tl=tl: t.matmul(
                        ps[:, 1024 + q * 512:1024 + (q + 1) * 512], r_idbf[:, :],
                        tl[:, e * 1024 + q * 512:e * 1024 + (q + 1) * 512],
                        start=False, stop=True),
                        waits=[("p3hdp", hdpv)] if q == 0 else (),
                        inc=None if q == 0 else "p3mmYd")
                ydv = p.val("p3mmYd")
                p.op("vector", lambda v, e=e, ytslice=ytslice: v.tensor_tensor(
                    ytslice, ps[:, 1024:2048], ytmpA, ALU.add),
                    waits=[("p3mmYd", ydv), ("p3ev1", e1v), ("p3yst", p.val("p3yst"))],
                    inc="p3ev2")
                for q in range(2):
                    w8 = [("p3ev1", e1v), ("p3hd", hdv),
                          ("p3pupd", p.val("p3pupd"))] if q == 0 else []
                    p.op("tensor", lambda t, e=e, q=q, tl=tl: t.matmul(
                        ps[:, q * 512:(q + 1) * 512], tl[:, 8192 + e * 128:8192 + (e + 1) * 128],
                        hd2_t[:, e * 1024 + q * 512: e * 1024 + (q + 1) * 512],
                        start=True, stop=True),
                        waits=w8, inc=None if q == 0 else "p3mmS")
                smv = p.val("p3mmS")
                p.op("vector", lambda v, e=e: v.tensor_tensor(
                    v3(ptmp_t, 16, 64), v3(pstate[:, e * 1024:(e + 1) * 1024], 16, 64),
                    colb(sm_lam[:, e * 16:(e + 1) * 16], 16, 64), ALU.mult),
                    waits=[("p3lam", lamv), ("p3mmY", ymv)], inc="p3pt")
                ptv = p.val("p3pt")
                p.op("vector", lambda v, e=e: v.tensor_tensor(
                    pstate[:, e * 1024:(e + 1) * 1024], ptmp_t[:, 0:1024],
                    ps[:, 0:1024], ALU.add),
                    waits=[("p3mmS", smv), ("p3pt", ptv)], inc="p3pupd")
            p.dma("sync", yt_d.ap()[c * 128:(c + 1) * 128, :], ytc_t[:, :],
                  waits=[("p3ev2", p.val("p3ev2"))], inc="p3yst")
            if c == 0 and "p3c0" in dbg_t:
                dbgw = [("p3ev2", p.val("p3ev2")), ("p3pupd", p.val("p3pupd")),
                        ("p3mh", p.val("p3mh")), ("p3hd", p.val("p3hd"))]
                dd = dbg_t["p3c0"].ap()
                p.dma("sync", dd[:, 0:16384], mhat_t[:, :], waits=dbgw, inc="dbgst")
                p.dma("sync", dd[:, 16384:24576], hc_t[:, :], waits=dbgw, inc="dbgst")
                p.dma("sync", dd[:, 24576:24704], sm_dtsp[:, :], waits=dbgw, inc="dbgst")
                p.dma("sync", dd[:, 24704:24832], sm_eacF[:, :], waits=dbgw, inc="dbgst")
                p.dma("sync", dd[:, 24832:24960], sm_decay[:, :], waits=dbgw, inc="dbgst")
                p.dma("sync", dd[:, 24960:25088], sm_lam[:, :], waits=dbgw, inc="dbgst")
                p.dma("sync", dd[:, 25088:25216], sm_eacs[:, :], waits=dbgw, inc="dbgst")
                p.dma("sync", dd[:, 25216:26240], ghat_t[:, :], waits=dbgw, inc="dbgst")
                p.dma("sync", dd[:, 26240:34432], pstate[:, :], waits=dbgw, inc="dbgst")
                p.dma("sync", dbg_t["p3f"].ap()[:, 0:256], f32b[:, 0:256], waits=dbgw, inc="dbgst")
                p.barrier_wait("vector", [("dbgst", p.val("dbgst"))])
        P3YST = p.val("p3yst")
        if dodbg["yt"]:
            p.dma("sync", dodbg["yt"].ap()[:, :], yt_d.ap()[:, :],
                  waits=[("p3yst", P3YST)], inc="dbgst")

    if "p35" in PHASES:
        ytin = (slab1[:, 0:8192], slab1[:, 8192:16384])
        gt = (slab1[:, 16384:24576], slab1[:, 24576:32768])
        ygn_b = (slab1[:, 32768:40960], slab1[:, 40960:49152])
        yg_t = slab1[:, 49152:57344]
        P3DONE = [("p3yst", P3YST), ("p3pupd", p.val("p3pupd")),
                  ("p3mmS", p.val("p3mmS")), ("p3ev2", p.val("p3ev2")),
                  ("p3mh", p.val("p3mh")), ("p3hd", p.val("p3hd")),
                  ("p3hdp", p.val("p3hdp")), ("p3hc", p.val("p3hc"))]
        for c in range(NCH):
            par = c % 2
            ldw = P3DONE + [("p35yg", max(0, p.val("p35yg") - 1)), ("p2bst", PST_ALL)]
            p.dma("sync", ytin[par][:, :], yt_d.ap()[c * 128:(c + 1) * 128, :],
                  waits=ldw, inc="p35ld")
            for piece in range(4):
                p.dma("sync", gt[par][:, piece * 2048:(piece + 1) * 2048],
                      proj_d.ap()[piece * 2048:(piece + 1) * 2048, c * 128:(c + 1) * 128],
                      waits=ldw + [("p35sg", max(0, p.val("p35sg") - 1))],
                      inc="p35ld", transpose=True)
            ldv = p.val("p35ld")
            p.op("scalar", lambda s, par=par: s.activation(gt[par][:, :], gt[par][:, :], AF.Silu),
                 waits=[("p35ld", ldv)], inc="p35sg")
            sgv = p.val("p35sg")
            p.op("vector", lambda v, par=par: v.tensor_tensor(yg_t[:, :], ytin[par][:, :], gt[par][:, :], ALU.mult),
                 waits=[("p35sg", sgv), ("p35ld", ldv), ("p35gn", p.val("p35gn"))], inc="p35yg")
            ygv = p.val("p35yg")
            p.op("vector", lambda v, par=par: v.tensor_tensor(ytin[par][:, :], yg_t[:, :], yg_t[:, :], ALU.mult),
                 waits=[("p35yg", ygv)], inc="p35sq")
            sqv = p.val("p35sq")
            for gidx in range(8):
                p.op("vector", lambda v, par=par, gidx=gidx: v.reduce_sum(
                    f32b[:, gidx:gidx + 1], ytin[par][:, gidx * 1024:(gidx + 1) * 1024], axis=AX),
                    waits=[("p35sq", sqv)] if gidx == 0 else (), inc="p35rs")
            rsv = p.val("p35rs")
            p.op("vector", lambda v: v.tensor_scalar(f32b[:, 8:16], f32b[:, 0:8], 1.0 / 1024.0, EPS, ALU.mult, ALU.add),
                 waits=[("p35rs", rsv)], inc="p35r")
            p.op("scalar", lambda s: s.activation(f32b[:, 8:16], f32b[:, 8:16], AF.Ln),
                 waits=[("p35r", p.val("p35r"))], inc="p35r")
            p.op("scalar", lambda s: s.activation(f32b[:, 8:16], f32b[:, 8:16], AF.Exp, scale=-0.5),
                 inc="p35r")
            p.op("vector", lambda v: v.tensor_copy(f32b[:, 16:20].bitcast(BF16)[:, 0:8], f32b[:, 8:16]),
                 waits=[("p35r", p.val("p35r"))], inc="p35r")
            rv8 = p.val("p35r")
            rstd_bf = f32b[:, 16:20].bitcast(BF16)[:, 0:8]
            p.op("vector", lambda v, par=par: v.tensor_tensor(
                v3(ygn_b[par][:, :], 8, 1024), v3(yg_t[:, :], 8, 1024),
                colb(rstd_bf, 8, 1024), ALU.mult),
                waits=[("p35r", rv8), ("p35st", max(0, p.val("p35st") - 16))], inc="p35gn")
            gnv = p.val("p35gn")
            p.dma("sync", ygn_d.ap()[c * 128:(c + 1) * 128, :], ygn_b[par][:, :],
                  waits=[("p35gn", gnv)], inc="p35st")
        P35ST = p.val("p35st")
        if dodbg["ygn"]:
            p.dma("sync", dodbg["ygn"].ap()[:, :], ygn_d.ap()[:, :],
                  waits=[("p35st", P35ST)], inc="dbgst")

    FFNW2 = build_ffn_tiles("fw2")

    if "p4" in PHASES:
        ygnF = slab1[:, 0:32768]
        w2b = (slab1[:, 32768:40960], slab1[:, 40960:49152])
        r2t = (slab1[:, 49152:49664], slab1[:, 49664:50176])
        x2t = (slab1[:, 50176:50688], slab1[:, 50688:51200])
        for t4 in range(4):
            for kc in range(NIC):
                w8y = [("p35st", P35ST), ("p35gn", p.val("p35gn"))]
                if kc == 0 and t4 > 0:
                    w8y.append(("p4ev", p.val("p4ev")))
                p.dma("sync", ygnF[:, kc * 512:(kc + 1) * 512],
                      ygn_d.ap()[t4 * 512:(t4 + 1) * 512, kc * 128:(kc + 1) * 128],
                      waits=w8y, inc="p4yl", transpose=True)
            ylv = p.val("p4yl")
            for mb in range(NMB):
                par = mb % 2
                p.dma("gpsimd", v3(w2b[par], NIC, 128), w2t_blk.ap()[mb],
                      waits=[("w2blk", w2blk_ready[mb]), ("p4mm", max(0, p.val("p4mm") - 1)),
                             ("p35st", P35ST), ("p35gn", p.val("p35gn"))],
                      inc="p4wl")
                wlv = p.val("p4wl")
                p.dma("gpsimd", r2t[par], r2_d.ap()[mb * 128:(mb + 1) * 128, t4 * 512:(t4 + 1) * 512],
                      waits=[("p2ast2", p.val("p2ast2")), ("p4ev", max(0, p.val("p4ev") - 1)),
                             ("p35st", P35ST), ("p35gn", p.val("p35gn"))],
                      inc="p4rl")
                rlv = p.val("p4rl")
                for kc in range(NIC):
                    w8 = []
                    if kc == 0:
                        w8 = [("p4yl", ylv), ("p4wl", wlv),
                              ("p4ev", max(0, p.val("p4ev") - 1))]
                        if t4 == 0 and mb == 0:
                            w8 += [("p3pupd", p.val("p3pupd")),
                                   ("p3ev2", p.val("p3ev2")),
                                   ("p3ev1", p.val("p3ev1"))]
                    p.op("tensor", lambda t, kc=kc, par=par: t.matmul(
                        ps[:, par * 512:(par + 1) * 512], w2b[par][:, kc * 128:(kc + 1) * 128],
                        ygnF[:, kc * 512:(kc + 1) * 512],
                        start=(kc == 0), stop=(kc == NIC - 1)),
                        waits=w8, inc=None if kc < NIC - 1 else "p4mm")
                mmv = p.val("p4mm")
                p.op("vector", lambda v, par=par: v.tensor_tensor(
                    x2t[par], ps[:, par * 512:(par + 1) * 512], r2t[par], ALU.add),
                    waits=[("p4mm", mmv), ("p4rl", rlv), ("p4st", max(0, p.val("p4st") - 16))],
                    inc="p4ev")
                evv = p.val("p4ev")
                p.dma("sync", x2_d.ap()[mb * 128:(mb + 1) * 128, t4 * 512:(t4 + 1) * 512],
                      x2t[par], waits=[("p4ev", evv)], inc="p4st")
        P4ST = p.val("p4st")
        if dodbg["x2"]:
            p.dma("sync", dodbg["x2"].ap()[:, :], x2_d.ap()[:, :],
                  waits=[("p4st", P4ST)], inc="dbgst")

    if "p5" in PHASES:
        for ti in range(NT1):
            xw = [("p4st", P4ST), ("fw2ev", FFNW2), ("f5oev", p.val("f5oev")),
                  ("p3mmYd", p.val("p3mmYd")), ("p35gn", p.val("p35gn"))]
            for pc in range(4):
                p.dma("sync",
                      v3(t_x[:, pc * 2048:(pc + 1) * 2048], 8, TT),
                      x2_d.ap().rearrange("(kc q) t -> q kc t", q=128)[:, pc * 8:(pc + 1) * 8, ti * TT:(ti + 1) * TT],
                      waits=xw, inc="f5xld")
            xv = p.val("f5xld")
            rowv = ln_stats("f5", t_x[:, :], [("f5xld", xv), ("f5ost", max(0, p.val("f5ost") - 32))])
            p.op("tensor", lambda t: t.matmul(ps[:, 3072:3072 + TT], r_ones1f[0:1, 0:128],
                                              f32s[0:1, R_M:R_M + TT], start=True, stop=True),
                 waits=[("f5r", rowv), ("f5ln", p.val("f5ln"))], inc="f5b")
            p.op("tensor", lambda t: t.matmul(ps[:, 3328:3328 + TT], r_ones1f[0:1, 0:128],
                                              f32s[0:1, R_RS:R_RS + TT], start=True, stop=True),
                 inc="f5b")
            bv = p.val("f5b")
            p.op("vector", lambda v: v.tensor_tensor(
                v3(t_ln[:, :], 32, TT), v3(t_x[:, :], 32, TT),
                rowb(ps[:, 3072:3072 + TT], 32, TT), ALU.subtract),
                waits=[("f5b", bv), ("f5sq", p.val("f5sq"))], inc="f5ln")
            p.op("vector", lambda v: v.tensor_tensor(
                v3(t_ln[:, :], 32, TT), v3(t_ln[:, :], 32, TT),
                rowb(ps[:, 3328:3328 + TT], 32, TT), ALU.mult),
                waits=[("f5ln", p.val("f5ln"))], inc="f5ln")
            lnv = p.val("f5ln")

            def out_writer(kc, ps_ap, waits, ti=ti):
                fpar = kc % 2
                fb = wbuf1.bitcast(F32)[:, fpar * 256:fpar * 256 + TT]
                p.op("vector", lambda v, kc=kc, ps_ap=ps_ap, fb=fb: v.tensor_tensor(
                    fb, ps_ap, t_x[:, kc * TT:(kc + 1) * TT], ALU.add),
                    waits=list(waits) + [("f5ost", max(0, p.val("f5ost") - 16))],
                    inc="f5oev")
                p.dma("sync", out_d.ap().rearrange("(kc q) t -> q kc t", q=128)[:, kc, ti * TT:(ti + 1) * TT],
                      fb, waits=[("f5oev", p.val("f5oev"))], inc="f5ost")

            ffn_tile("f5", ti, lnv, t_x, out_writer)

    fin = [("f5ost", p.val("f5ost"))] if p.val("f5ost") else [("p2ast2", p.val("p2ast2"))]
    if p.val("dbgst"):
        fin.append(("dbgst", p.val("dbgst")))
    p.barrier_wait("sync", fin)

    with nc.Block() as block:
        @block.tensor
        def _(e):
            for f in p.ev["tensor"]:
                f(e)

        @block.vector
        def _(e):
            for f in p.ev["vector"]:
                f(e)

        @block.scalar
        def _(e):
            for f in p.ev["scalar"]:
                f(e)

        @block.gpsimd
        def _(e):
            for f in p.ev["gpsimd"]:
                f(e)

        @block.sync
        def _(e):
            for f in p.ev["sync"]:
                f(e)

    return nc


def _prep_inputs(inp):
    w = {}
    w["wp"] = np.ascontiguousarray(inp["L0_attn_in_proj_w"])
    w["w2"] = np.ascontiguousarray(inp["L0_attn_out_proj_w"])
    w["ffa"] = np.ascontiguousarray(inp["L0_ffn_A"].reshape(128, D))
    w["ffb"] = np.ascontiguousarray(inp["L0_ffn_B"])
    w["wrr"] = np.ascontiguousarray(inp["L0_ffn_Wr"])
    w["brow"] = np.ascontiguousarray(inp["L0_ffn_br"].reshape(1, NG))
    w["normw"] = np.ascontiguousarray(inp["L0_attn_norm_w"].reshape(NKC, 128))
    w["convw"] = np.ascontiguousarray(inp["L0_attn_conv_w"])
    w["convb"] = np.ascontiguousarray(inp["L0_attn_conv_b"].reshape(CONVD // 128, 128))
    w["dtb"] = np.ascontiguousarray(inp["L0_attn_dt_bias"].reshape(1, NH))
    w["alog"] = np.ascontiguousarray(inp["L0_attn_A_log"].reshape(1, NH))
    w["dpr"] = np.ascontiguousarray(inp["L0_attn_Dp"].reshape(1, NH))
    w["gnw"] = np.ascontiguousarray(inp["L0_attn_gnorm_w"].reshape(NIC, 128))
    return w


def kernel(**inp):
    w = _prep_inputs(inp)
    x = inp["L0_attn_x"]
    nc = build_program()
    in_maps = []
    for b in range(8):
        m = dict(w)
        m["xT"] = np.ascontiguousarray(x[b].T)
        in_maps.append(m)
    res = run_bass_kernel_spmd(nc, in_maps, core_ids=list(range(8)))
    out = np.stack([r["out"].T for r in res.results], axis=0)
    return np.ascontiguousarray(out.astype(np.float32))


if __name__ == "__main__":
    nc = build_program()
    n = nc.n_instructions()
    print("instructions:", n)


# version 03 (final)
# speedup vs baseline: 1.6340x; 1.0456x over previous
import numpy as np
import ml_dtypes
import concourse.bass as bass
import concourse.mybir as mybir
from concourse.bass_utils import run_bass_kernel_spmd

F32 = mybir.dt.float32
BF16 = mybir.dt.bfloat16
AF = mybir.ActivationFunctionType
ALU = mybir.AluOpType
AX = mybir.AxisListType.X
BF = ml_dtypes.bfloat16

S, D, DI, CONVD, PROJ = 2048, 4096, 8192, 10240, 18560
NH, PH, NG, NN, CH = 128, 64, 8, 128, 128
NCH, REP = S // CH, NH // NG
EPS = 1e-5
NOB, NKC, NIC = PROJ // 128, D // 128, DI // 128
NMB = D // 128
OB_HBC0 = DI // 128
ENGS = ("tensor", "vector", "scalar", "gpsimd", "sync")


class Rec:

    def __init__(self, nc):
        self.nc = nc
        self.ev = {e: [] for e in ENGS}
        self.cnt = {}
        self.handles = {}

    ALIAS = {}

    @classmethod
    def canon(cls, name):
        if name == "f5xld":
            return name
        for pre, sub in (("f5", "f1"), ("fw2", "fw1")):
            if name.startswith(pre):
                return sub + name[len(pre):]
        return name

    def _h(self, name):
        name = self.canon(name)
        if name not in self.cnt:
            self.cnt[name] = 0
            self.handles[name] = self.nc.alloc_semaphore(name)
        return self.handles[name]

    def val(self, name):
        return self.cnt.get(self.canon(name), 0)

    def op(self, eng, fn, waits=(), inc=None, inc_n=1):
        wl = []
        for w in waits:
            name, v = w if isinstance(w, tuple) else (w, self.val(w))
            if v > 0:
                wl.append((self._h(name), v))
        ih = None
        if inc is not None:
            inc = self.canon(inc)
            self._h(inc)
            self.cnt[inc] += inc_n
            ih = (self.handles[inc], inc_n)

        def emit(e, fn=fn, wl=wl, ih=ih):
            for h, v in wl:
                e.wait_ge(h, v)
            ins = fn(e)
            if ih is not None:
                ins.then_inc(ih[0], ih[1])

        self.ev[eng].append(emit)
        return self.cnt[inc] if inc else None

    STRICT = {"wpc", "w2c", "wtp", "wblk", "w2blk", "p2bst",
              "p4wl", "p4rl", "p4st", "f1ost"}

    def dma(self, eng, out, in_, waits=(), inc=None, **kw):
        waits = list(waits)
        if inc is not None and self.canon(inc) in self.STRICT:
            waits.append((self.canon(inc), self.val(inc)))
        return self.op(eng, lambda e: e.dma_start(out, in_, **kw), waits, inc, 16)

    def barrier_wait(self, eng, waits):
        wl = [(w if isinstance(w, tuple) else (w, self.val(w))) for w in waits]
        wl = [(self._h(n), v) for n, v in wl if v > 0]

        def emit(e, wl=wl):
            for h, v in wl:
                e.wait_ge(h, v)
        self.ev[eng].append(emit)


def v3(ap2d, k, r):
    return ap2d.rearrange("p (k r) -> p k r", r=r)


def colb(col_ap, k, r):
    return col_ap.broadcast_to(list(col_ap.shape) + [r])


def rowb(row_ap, k, r):
    p0, rr = row_ap.shape
    return row_ap.unsqueeze(1).broadcast_to([p0, k, rr])


class Carve:

    def __init__(self, t, total):
        self.t = t
        self.total = total
        self.off = 0

    def take(self, n):
        assert self.off + n <= self.total, (self.off, n, self.total)
        ap = self.t[:, self.off:self.off + n]
        self.off += n
        return ap

    def reset(self, off=0):
        self.off = off


def build_program(dbg=None, phases=("p2b", "p3", "p35", "p4", "p5"), detect_races=True):
    PHASES = set(phases)
    dbg = dbg or {}
    nc = bass.Bass("TRN2", target_bir_lowering=False,
                   detect_race_conditions=detect_races)

    xT = nc.dram_tensor("xT", [D, S], F32, kind="ExternalInput")
    wp = nc.dram_tensor("wp", [PROJ, D], F32, kind="ExternalInput")
    w2 = nc.dram_tensor("w2", [D, DI], F32, kind="ExternalInput")
    ffa = nc.dram_tensor("ffa", [128, D], F32, kind="ExternalInput")
    ffb = nc.dram_tensor("ffb", [NG, D, 16], F32, kind="ExternalInput")
    wrr = nc.dram_tensor("wrr", [NG, D], F32, kind="ExternalInput")
    brow = nc.dram_tensor("brow", [1, NG], F32, kind="ExternalInput")
    normw = nc.dram_tensor("normw", [NKC, 128], F32, kind="ExternalInput")
    convw = nc.dram_tensor("convw", [CONVD, 4], F32, kind="ExternalInput")
    convb = nc.dram_tensor("convb", [CONVD // 128, 128], F32, kind="ExternalInput")
    dtb = nc.dram_tensor("dtb", [1, NH], F32, kind="ExternalInput")
    alog = nc.dram_tensor("alog", [1, NH], F32, kind="ExternalInput")
    dpr = nc.dram_tensor("dpr", [1, NH], F32, kind="ExternalInput")
    gnw = nc.dram_tensor("gnw", [NIC, 128], F32, kind="ExternalInput")
    out_d = nc.dram_tensor("out", [D, S], F32, kind="ExternalOutput")

    dbg_t = {}
    for k, shape, dt_ in (("proj", [PROJ, S], BF16), ("h", [D, S], BF16),
                          ("x1", [D, S], BF16), ("yt", [S, DI], BF16),
                          ("ygn", [S, DI], BF16), ("r2", [D, S], BF16),
                          ("x2", [D, S], BF16)):
        if dbg.get(k):
            dbg_t[k] = nc.dram_tensor("dbg_" + k, shape, dt_, kind="ExternalOutput")
    if dbg.get("p3c0"):
        dbg_t["p3c0"] = nc.dram_tensor("dbg_p3c0", [128, 36096], BF16, kind="ExternalOutput")
        dbg_t["p3f"] = nc.dram_tensor("dbg_p3f", [128, 512], F32, kind="ExternalOutput")

    wp_bf = nc.dram_tensor("wp_bf", [PROJ, D], BF16)
    w2_bf = nc.dram_tensor("w2_bf", [D, DI], BF16)
    wpt_blk = nc.dram_tensor("wpt_blk", [NOB, 128, NKC, 128], BF16)
    w2t_blk = nc.dram_tensor("w2t_blk", [NMB, 128, NIC, 128], BF16)
    a_bfd = nc.dram_tensor("a_bfd", [128, D], BF16)
    b_bfd = nc.dram_tensor("b_bfd", [D, 128], BF16)
    wr_bfd = nc.dram_tensor("wr_bfd", [NG, D], BF16)
    proj_d = nc.dram_tensor("proj_d", [PROJ, S], BF16)
    r2_d = nc.dram_tensor("r2_d", [D, S], BF16)
    yt_d = nc.dram_tensor("yt_d", [S, DI], BF16)
    ygn_d = nc.dram_tensor("ygn_d", [S, DI], BF16)
    x2_d = nc.dram_tensor("x2_d", [D, S], BF16)

    c_id_bf = nc.inline_tensor(np.eye(128, dtype=BF), "c_id_bf")
    c_id_f32 = nc.inline_tensor(np.eye(128, dtype=np.float32), "c_id_f32")
    c_ones1_bf = nc.inline_tensor(np.ones((1, 128), dtype=BF), "c_ones1_bf")
    c_ones1_f = nc.inline_tensor(np.ones((1, 128), dtype=np.float32), "c_ones1_f")
    c_onescol_bf = nc.inline_tensor(np.ones((128, 1), dtype=BF), "c_onescol_bf")
    c_ones8_bf = nc.inline_tensor(np.ones((8, 1), dtype=BF), "c_ones8_bf")
    c_ones8_f = nc.inline_tensor(np.ones((8, 1), dtype=np.float32), "c_ones8_f")
    c_l1_f32 = nc.inline_tensor(np.triu(np.ones((128, 128), dtype=np.float32)), "c_l1_f32")
    c_u_bf = nc.inline_tensor(np.triu(np.ones((128, 128))).astype(BF), "c_u_bf")
    c_irep_bf = nc.inline_tensor(np.tile(np.eye(128), (1, 16)).astype(BF), "c_irep_bf")
    e8 = np.zeros((8, 128))
    for hh in range(8):
        e8[hh, hh * 16:(hh + 1) * 16] = 1
    c_e8_bf = nc.inline_tensor(e8.astype(BF), "c_e8_bf")
    e127 = np.zeros((128, 1), dtype=np.float32)
    e127[127, 0] = 1.0
    c_e127_f = nc.inline_tensor(e127, "c_e127_f")


    SBA = nc.alloc_sbuf_tensor
    slab1 = SBA("slab1", [128, 65536], BF16)
    slab2 = SBA("slab2", [128, 16384], BF16)
    slab3 = SBA("slab3", [128, 8448], BF16)
    wtmp0 = SBA("wtmp0", [128, 2048], BF16)
    wtmp1 = SBA("wtmp1", [128, 2048], BF16)
    wbuf0 = SBA("wbuf0", [128, 4096], BF16)
    r_irep = SBA("r_irep", [128, 2048], BF16)
    r_u = SBA("r_u", [128, 128], BF16)
    r_idbf = SBA("r_idbf", [128, 128], BF16)
    r_idf = SBA("r_idf", [128, 128], F32)
    r_e8 = SBA("r_e8", [8, 128], BF16)
    r_ones1b = SBA("r_ones1b", [1, 128], BF16)
    r_ones1f = SBA("r_ones1f", [1, 128], F32)
    r_onescolb = SBA("r_onescolb", [128, 1], BF16)
    r_ones8b = SBA("r_ones8b", [8, 1], BF16)
    r_ones8f = SBA("r_ones8f", [8, 1], F32)
    r_brcol = SBA("r_brcol", [8, 1], F32)
    r_e127 = SBA("r_e127", [128, 1], F32)
    r_l1 = SBA("r_l1", [128, 128], F32)
    r_convw = SBA("r_convw", [128, 320], F32)
    r_convb = SBA("r_convb", [128, 80], F32)
    r_normwT = SBA("r_normwT", [128, 32], F32)
    r_gnwT = SBA("r_gnwT", [128, 64], F32)
    r_abc = SBA("r_abc", [128, 128], F32)
    r_biasbc = SBA("r_biasbc", [128, 128], F32)
    r_dpbc = SBA("r_dpbc", [128, 128], BF16)
    f32s = SBA("f32s", [128, 1024], F32)
    f32b = SBA("f32b", [128, 256], F32)
    rem = nc.sbuf_bytes_remaining
    assert rem >= 0, rem

    wbuf1 = slab2[:, 10368:14464]
    ps = nc.alloc_psum_tensor("ps", [128, 4096], F32)

    def psbf(c0, n):
        return ps[:, c0:c0 + n // 2].bitcast(BF16)

    p = Rec(nc)
    dodbg = {k: (dbg_t[k] if k in dbg_t else None) for k in
             ("proj", "h", "x1", "yt", "ygn", "r2", "x2")}

    for dst, src in ((r_irep, c_irep_bf), (r_u, c_u_bf), (r_idbf, c_id_bf),
                     (r_idf, c_id_f32), (r_e8, c_e8_bf), (r_ones1b, c_ones1_bf),
                     (r_ones1f, c_ones1_f), (r_onescolb, c_onescol_bf),
                     (r_ones8b, c_ones8_bf), (r_ones8f, c_ones8_f), (r_e127, c_e127_f), (r_l1, c_l1_f32)):
        p.dma("gpsimd", dst[:, :], src.ap()[:, :], inc="cld")
    for i in range(8):
        src = convw.ap().rearrange("(c p) k -> p c k", p=128)[:, i * 10:(i + 1) * 10, :]
        p.dma("gpsimd", v3(r_convw[:, i * 40:(i + 1) * 40], 10, 4), src, inc="cld")
    p.dma("gpsimd", f32s[0:1, 0:128], dtb.ap()[:, :], inc="cld")
    p.dma("gpsimd", f32s[0:1, 128:256], alog.ap()[:, :], inc="cld")
    p.dma("gpsimd", f32s[0:1, 256:384], dpr.ap()[:, :], inc="cld")
    p.dma("gpsimd", f32s[0:1, 384:392], brow.ap()[:, :], inc="cld")
    p.dma("gpsimd", f32s[0:32, 392:520], normw.ap()[:, :], inc="cld")
    p.dma("gpsimd", f32s[0:64, 520:648], gnw.ap()[:, :], inc="cld")
    p.dma("gpsimd", f32s[0:80, 648:776], convb.ap()[:, :], inc="cld")
    p.dma("gpsimd", a_bfd.ap()[:, :], ffa.ap()[:, :], inc="ffnbf")
    for hh in range(NG):
        p.dma("gpsimd", b_bfd.ap()[:, hh * 16:(hh + 1) * 16], ffb.ap()[hh], inc="ffnbf")
    p.dma("gpsimd", wr_bfd.ap()[:, :], wrr.ap()[:, :], inc="ffnbf")
    CLD = p.val("cld")
    FFNBF = p.val("ffnbf")

    p.op("scalar", lambda s: s.activation(f32s[0:1, 776:904], f32s[0:1, 128:256], AF.Exp),
         waits=[("cld", CLD)], inc="p0a")
    p.op("vector", lambda v: v.tensor_scalar_mul(f32s[0:1, 776:904], f32s[0:1, 776:904], -1.0),
         waits=[("p0a", p.val("p0a"))], inc="p0a")
    P0A = p.val("p0a")

    def small_tr(src_ap, dst_ap, ncol):
        p.op("tensor", lambda t: t.transpose(ps[:, 0:ncol], src_ap, r_idf[0:ncol, 0:ncol]),
             waits=[("cld", CLD), ("p0ev", p.val("p0ev"))], inc="p0tr")
        p.op("vector", lambda v: v.tensor_copy(dst_ap, ps[:, 0:ncol]),
             waits=[("p0tr", p.val("p0tr"))], inc="p0ev")

    small_tr(f32s[0:32, 392:520], r_normwT[:, :], 32)
    small_tr(f32s[0:64, 520:648], r_gnwT[:, :], 64)
    small_tr(f32s[0:80, 648:776], r_convb[:, :], 80)

    def row_bcast(row_ap, dst_ap, n, extra=()):
        p.op("tensor", lambda t: t.matmul(ps[:, 512:512 + n], r_ones1f[0:1, 0:128],
                                          row_ap, start=True, stop=True),
             waits=[("cld", CLD), ("p0ev", p.val("p0ev"))] + list(extra), inc="p0tr")
        p.op("vector", lambda v: v.tensor_copy(dst_ap, ps[:, 512:512 + n]),
             waits=[("p0tr", p.val("p0tr"))], inc="p0ev")

    row_bcast(f32s[0:1, 0:128], r_biasbc[:, :], 128)
    row_bcast(f32s[0:1, 776:904], r_abc[:, :], 128, extra=[("p0a", P0A)])
    row_bcast(f32s[0:1, 256:384], r_dpbc[:, :], 128)
    p.op("tensor", lambda t: t.transpose(ps[0:8, 0:1], f32s[0:1, 384:392], r_idf[0:1, 0:1]),
         waits=[("cld", CLD), ("p0ev", p.val("p0ev"))], inc="p0tr")
    p.op("vector", lambda v: v.tensor_copy(r_brcol[:, :], ps[0:8, 0:1]),
         waits=[("p0tr", p.val("p0tr"))], inc="p0ev")

    at_t = slab3[:, 0:4096]
    bt_t = slab3[:, 4096:8192]
    wrt_t = slab3[:, 8192:8448]

    def build_ffn_tiles(tag):
        p.dma("sync", wtmp0[:, 0:2048], a_bfd.ap()[:, 0:2048], waits=[("ffnbf", FFNBF)], inc=tag + "ld")
        p.dma("sync", wtmp1[:, 0:2048], a_bfd.ap()[:, 2048:4096], waits=[("ffnbf", FFNBF)], inc=tag + "ld")
        p.dma("sync", v3(wbuf1[:, 0:4096], 32, 128),
              b_bfd.ap().rearrange("(c p) q -> p c q", p=128), waits=[("ffnbf", FFNBF)], inc=tag + "ld")
        ldv = p.val(tag + "ld")
        for kc in range(32):
            src = (wtmp0 if kc < 16 else wtmp1)[:, (kc % 16) * 128:(kc % 16) * 128 + 128]
            bank = 1024 + (kc % 2) * 512
            p.op("tensor", lambda t, src=src, bank=bank: t.transpose(psbf(bank, 256)[:, 0:128], src, r_idbf[:, :]),
                 waits=[(tag + "ld", ldv), (tag + "ev", max(0, p.val(tag + "ev") - 1))], inc=tag + "tr")
            p.op("vector", lambda v, kc=kc, bank=bank: v.tensor_copy(
                at_t[:, kc * 128:(kc + 1) * 128], psbf(bank, 256)[:, 0:128]),
                waits=[(tag + "tr", p.val(tag + "tr"))], inc=tag + "ev")
        for oc in range(32):
            src = wbuf1[:, oc * 128:(oc + 1) * 128]
            bank = 1024 + (oc % 2) * 512
            p.op("tensor", lambda t, src=src, bank=bank: t.transpose(psbf(bank, 256)[:, 0:128], src, r_idbf[:, :]),
                 waits=[(tag + "ld", ldv), (tag + "ev", max(0, p.val(tag + "ev") - 1))], inc=tag + "tr")
            p.op("vector", lambda v, oc=oc, bank=bank: v.tensor_copy(
                bt_t[:, oc * 128:(oc + 1) * 128], psbf(bank, 256)[:, 0:128]),
                waits=[(tag + "tr", p.val(tag + "tr"))], inc=tag + "ev")
        p.dma("sync", wtmp0[0:8, 0:2048], wr_bfd.ap()[:, 0:2048],
              waits=[(tag + "ev", p.val(tag + "ev"))], inc=tag + "ld")
        p.dma("sync", wtmp1[0:8, 0:2048], wr_bfd.ap()[:, 2048:4096],
              waits=[(tag + "ev", p.val(tag + "ev"))], inc=tag + "ld")
        ldv = p.val(tag + "ld")
        for kc in range(32):
            src = (wtmp0 if kc < 16 else wtmp1)[0:8, (kc % 16) * 128:(kc % 16) * 128 + 128]
            bank = 1024 + (kc % 2) * 512
            p.op("tensor", lambda t, src=src, bank=bank: t.transpose(psbf(bank, 16)[:, 0:8], src, r_idbf[0:8, 0:8]),
                 waits=[(tag + "ld", ldv), (tag + "ev", max(0, p.val(tag + "ev") - 1))], inc=tag + "tr")
            p.op("vector", lambda v, kc=kc, bank=bank: v.tensor_copy(
                wrt_t[:, kc * 8:(kc + 1) * 8], psbf(bank, 16)[:, 0:8]),
                waits=[(tag + "tr", p.val(tag + "tr"))], inc=tag + "ev")
        return p.val(tag + "ev")

    FFNW1 = build_ffn_tiles("fw1")

    wp_spans = [(i * 2048, 2048) for i in range(9)] + [(18432, 128)]
    span_order = [4, 5, 6, 7, 8, 9, 0, 1, 2, 3]
    wp_cast_at = {}
    w2_cast_at = {}
    _cast_jobs = [("wp", si) for si in span_order] + [("w2", 0), ("w2", 1)]

    def emit_cast():
        if not _cast_jobs:
            return
        kind, si = _cast_jobs.pop(0)
        if kind == "wp":
            o0, olen = wp_spans[si]
            p.dma("gpsimd", wp_bf.ap()[o0:o0 + olen, :], wp.ap()[o0:o0 + olen, :], inc="wpc")
            wp_cast_at[si] = p.val("wpc")
        else:
            m0 = si * 2048
            p.dma("gpsimd", w2_bf.ap()[m0:m0 + 2048, :], w2.ap()[m0:m0 + 2048, :], inc="w2c")
            w2_cast_at[si] = p.val("w2c")
    emit_cast()
    emit_cast()
    emit_cast()

    blk_ready = {}
    w2blk_ready = {}
    _wtp_cnt = [0]
    _build_jobs = []
    for si in span_order:
        o0, olen = wp_spans[si]
        for kc in range(NKC):
            _build_jobs.append(("wp", si, kc))
    for si in (0, 1):
        for kc in range(NIC):
            _build_jobs.append(("w2", si, kc))

    def emit_build_step():
        if not _build_jobs:
            return
        kind, si, kc = _build_jobs.pop(0)
        buf_full = wtmp0 if _wtp_cnt[0] % 2 == 0 else wtmp1
        _wtp_cnt[0] += 1
        if kind == "wp":
            while si not in wp_cast_at:
                emit_cast()
            o0, olen = wp_spans[si]
            nob_span = olen // 128
            buf = buf_full[:, 0:olen]
            p.dma("sync", buf, wp_bf.ap()[o0:o0 + olen, kc * 128:(kc + 1) * 128],
                  waits=[("wpc", wp_cast_at[si]), ("wblk", p.val("wblk")),
                         ("w2blk", p.val("w2blk")), ("fw1ev", FFNW1)],
                  inc="wtp", transpose=True)
            tpv = p.val("wtp")
            dst = wpt_blk.ap()[o0 // 128:o0 // 128 + nob_span, :, kc, :].rearrange("b d o -> d b o")
            p.dma("sync", dst, v3(buf, nob_span, 128), waits=[("wtp", tpv)], inc="wblk")
            if kc == NKC - 1:
                for ob in range(o0 // 128, o0 // 128 + nob_span):
                    blk_ready[ob] = p.val("wblk")
        else:
            while si not in w2_cast_at:
                emit_cast()
            m0 = si * 2048
            buf = buf_full[:, 0:2048]
            p.dma("sync", buf, w2_bf.ap()[m0:m0 + 2048, kc * 128:(kc + 1) * 128],
                  waits=[("w2c", w2_cast_at[si]), ("w2gn", max(0, p.val("w2gn") - 1)),
                         ("wblk", p.val("wblk")), ("w2blk", p.val("w2blk"))],
                  inc="wtp", transpose=True)
            tpv = p.val("wtp")
            p.op("vector", lambda v, buf=buf, kc=kc: v.tensor_scalar_mul(buf, buf, r_gnwT[:, kc:kc + 1]),
                 waits=[("wtp", tpv), ("p0ev", p.val("p0ev"))], inc="w2gn")
            gv = p.val("w2gn")
            dst = w2t_blk.ap()[m0 // 128:m0 // 128 + 16, :, kc, :].rearrange("b d o -> d b o")
            p.dma("sync", dst, v3(buf, 16, 128), waits=[("w2gn", gv)], inc="w2blk")
            if kc == NIC - 1:
                for mb in range(m0 // 128, m0 // 128 + 16):
                    w2blk_ready[mb] = p.val("w2blk")

    for _ in range(2 * NKC):
        emit_build_step()

    h_t = slab1[:, :]
    TT = 256
    NT1 = S // TT
    t_x = slab2[:, 0:8192]
    t_ln = slab2[:, 8192:16384]

    R_SUM, R_SQ, R_M, R_RS, R_T1, R_S2 = 0, 256, 512, 768, 904, 904

    def ln_stats(tag, src_tile, xv, guard):
        p.op("scalar", lambda s: s.activation(t_ln[:, :], src_tile, AF.Square),
             waits=[(tag + "xld", xv)] + list(guard), inc=tag + "sq")
        sqv = p.val(tag + "sq")
        for kc in range(NKC):
            p.op("tensor", lambda t, kc=kc: t.matmul(
                ps[0:1, 2048:2048 + TT], r_onescolb[:, :], src_tile[:, kc * TT:(kc + 1) * TT],
                start=(kc == 0), stop=(kc == NKC - 1)),
                waits=([(tag + "xld", xv)] + list(guard)) if kc == 0 else (),
                inc=None if kc < NKC - 1 else tag + "st")
        for kc in range(NKC):
            p.op("tensor", lambda t, kc=kc: t.matmul(
                ps[0:1, 2560:2560 + TT], r_onescolb[:, :], t_ln[:, kc * TT:(kc + 1) * TT],
                start=(kc == 0), stop=(kc == NKC - 1)),
                waits=[(tag + "sq", sqv)] if kc == 0 else (),
                inc=None if kc < NKC - 1 else tag + "st")
        stv = p.val(tag + "st")
        r = f32s[0:1, :]
        p.op("vector", lambda v: v.tensor_scalar_mul(r[:, R_M:R_M + TT], ps[0:1, 2048:2048 + TT], 1.0 / D),
             waits=[(tag + "st", stv)], inc=tag + "r")
        p.op("vector", lambda v: v.tensor_scalar_mul(r[:, R_SQ:R_SQ + TT], ps[0:1, 2560:2560 + TT], 1.0 / D),
             inc=tag + "r")
        p.op("vector", lambda v: v.tensor_tensor(r[:, R_SUM:R_SUM + TT], r[:, R_M:R_M + TT], r[:, R_M:R_M + TT], ALU.mult),
             waits=[(tag + "r", p.val(tag + "r"))], inc=tag + "r")
        p.op("vector", lambda v: v.tensor_tensor(r[:, R_SQ:R_SQ + TT], r[:, R_SQ:R_SQ + TT], r[:, R_SUM:R_SUM + TT], ALU.subtract),
             waits=[(tag + "r", p.val(tag + "r"))], inc=tag + "r")
        p.op("vector", lambda v: v.tensor_scalar_add(r[:, R_RS:R_RS + TT], r[:, R_SQ:R_SQ + TT], EPS),
             waits=[(tag + "r", p.val(tag + "r"))], inc=tag + "r")
        p.op("scalar", lambda s: s.activation(r[:, R_SM:R_SM + TT], r[:, R_RS:R_RS + TT], AF.Ln),
             waits=[(tag + "r", p.val(tag + "r"))], inc=tag + "r")
        p.op("scalar", lambda s: s.activation(r[:, R_RS:R_RS + TT], r[:, R_SM:R_SM + TT], AF.Exp, scale=-0.5),
             inc=tag + "r")
        return p.val(tag + "r")

    def bc_row(row_ap, dst_c, n, waits, tag):
        p.op("tensor", lambda t: t.matmul(ps[:, dst_c:dst_c + n], r_ones1f[0:1, 0:1],
                                          row_ap, start=True, stop=True),
             waits=waits, inc=tag)
        return p.val(tag)

    h_t = slab1[:, :]
    TT = 256
    NT1 = S // TT
    t_x = slab2[:, 0:8192]
    t_ln = slab2[:, 8192:16384]
    R_SM, R_SQ, R_M, R_RS = 0, 256, 512, 768
    P0EV = p.val("p0ev")
    P0TR = p.val("p0tr")

    def ln_stats(tag, src_tile, xwaits):
        p.op("scalar", lambda s: s.activation(t_ln[:, :], src_tile, AF.Square),
             waits=list(xwaits) + [(tag + "r", p.val(tag + "r"))], inc=tag + "sq")
        sqv = p.val(tag + "sq")
        for kc in range(NKC):
            p.op("tensor", lambda t, kc=kc: t.matmul(
                ps[0:1, 2048:2048 + TT], r_onescolb[:, :], src_tile[:, kc * TT:(kc + 1) * TT],
                start=(kc == 0), stop=(kc == NKC - 1)),
                waits=(list(xwaits) + [(tag + "r", p.val(tag + "r"))]) if kc == 0 else (),
                inc=None if kc < NKC - 1 else tag + "st")
        for kc in range(NKC):
            p.op("tensor", lambda t, kc=kc: t.matmul(
                ps[0:1, 2560:2560 + TT], r_onescolb[:, :], t_ln[:, kc * TT:(kc + 1) * TT],
                start=(kc == 0), stop=(kc == NKC - 1)),
                waits=[(tag + "sq", sqv)] if kc == 0 else (),
                inc=None if kc < NKC - 1 else tag + "st")
        stv = p.val(tag + "st")
        r = f32s[0:1, :]
        p.op("vector", lambda v: v.tensor_scalar_mul(r[:, R_M:R_M + TT], ps[0:1, 2048:2048 + TT], 1.0 / D),
             waits=[(tag + "st", stv)], inc=tag + "r")
        p.op("vector", lambda v: v.tensor_scalar_mul(r[:, R_SQ:R_SQ + TT], ps[0:1, 2560:2560 + TT], 1.0 / D),
             inc=tag + "r")
        p.op("vector", lambda v: v.tensor_tensor(r[:, R_SM:R_SM + TT], r[:, R_M:R_M + TT], r[:, R_M:R_M + TT], ALU.mult),
             waits=[(tag + "r", p.val(tag + "r"))], inc=tag + "r")
        p.op("vector", lambda v: v.tensor_tensor(r[:, R_SQ:R_SQ + TT], r[:, R_SQ:R_SQ + TT], r[:, R_SM:R_SM + TT], ALU.subtract),
             waits=[(tag + "r", p.val(tag + "r"))], inc=tag + "r")
        p.op("vector", lambda v: v.tensor_scalar_add(r[:, R_RS:R_RS + TT], r[:, R_SQ:R_SQ + TT], EPS),
             waits=[(tag + "r", p.val(tag + "r"))], inc=tag + "r")
        p.op("scalar", lambda s: s.activation(r[:, R_SM:R_SM + TT], r[:, R_RS:R_RS + TT], AF.Ln),
             waits=[(tag + "r", p.val(tag + "r"))], inc=tag + "r")
        p.op("scalar", lambda s: s.activation(r[:, R_RS:R_RS + TT], r[:, R_SM:R_SM + TT], AF.Exp, scale=-0.5),
             inc=tag + "r")
        return p.val(tag + "r")

    SMF = slab2[:, 14464:15488].bitcast(F32)
    SMB = slab2

    def ffn_tile(tag, ti, lnv, x_tile, out_writer, extra_first=()):
        for kc in range(NKC):
            p.op("tensor", lambda t, kc=kc: t.matmul(
                ps[:, 0:TT], at_t[:, kc * 128:(kc + 1) * 128], t_ln[:, kc * TT:(kc + 1) * TT],
                start=(kc == 0), stop=(kc == NKC - 1)),
                waits=([(tag + "ln", lnv), (tag + "loev", p.val(tag + "loev"))] + list(extra_first)) if kc == 0 else (),
                inc=None if kc < NKC - 1 else tag + "lo")
        for kc in range(NKC):
            p.op("tensor", lambda t, kc=kc: t.matmul(
                ps[0:8, 512:512 + TT], wrt_t[:, kc * 8:(kc + 1) * 8], t_ln[:, kc * TT:(kc + 1) * TT],
                start=(kc == 0), stop=(kc == NKC - 1)),
                waits=[(tag + "ls", p.val(tag + "ls"))] if kc == 0 else (),
                inc=None if kc < NKC - 1 else tag + "lo")
        lov = p.val(tag + "lo")
        p.op("scalar", lambda s: s.activation(SMF[0:8, 0:TT], ps[0:8, 512:512 + TT], AF.Copy),
             waits=[(tag + "lo", lov), (tag + "sb", p.val(tag + "sb"))], inc=tag + "sm")
        p.op("vector", lambda v: v.tensor_scalar_add(SMF[0:8, 0:TT], SMF[0:8, 0:TT], r_brcol[:, :]),
             waits=[(tag + "sm", p.val(tag + "sm"))], inc=tag + "sm")
        p.op("tensor", lambda t: t.matmul(ps[0:1, 2304:2304 + TT], r_ones8f[:, :], SMF[0:8, 0:TT],
                                          start=True, stop=True),
             waits=[(tag + "sm", p.val(tag + "sm"))], inc=tag + "smm")
        p.op("vector", lambda v: v.tensor_copy(f32s[0:1, R_SM:R_SM + TT], ps[0:1, 2304:2304 + TT]),
             waits=[(tag + "smm", p.val(tag + "smm"))], inc=tag + "sm")
        p.op("tensor", lambda t: t.matmul(ps[0:8, 2304:2304 + TT], r_ones1f[0:1, 0:8], f32s[0:1, R_SM:R_SM + TT],
                                          start=True, stop=True),
             waits=[(tag + "sm", p.val(tag + "sm"))], inc=tag + "smm")
        p.op("vector", lambda v: v.scalar_tensor_tensor(
            SMF[0:8, 256:256 + TT], ps[0:8, 2304:2304 + TT], -0.125, SMF[0:8, 0:TT],
            ALU.mult, ALU.add),
            waits=[(tag + "smm", p.val(tag + "smm"))], inc=tag + "sm")
        p.op("scalar", lambda s: s.activation(SMF[0:8, 0:TT], SMF[0:8, 256:256 + TT], AF.Exp),
             waits=[(tag + "sm", p.val(tag + "sm"))], inc=tag + "sm")
        p.op("tensor", lambda t: t.matmul(ps[0:1, 2304:2304 + TT], r_ones8f[:, :], SMF[0:8, 0:TT],
                                          start=True, stop=True),
             waits=[(tag + "sm", p.val(tag + "sm"))], inc=tag + "smm")
        p.op("vector", lambda v: v.tensor_copy(f32s[0:1, R_SM:R_SM + TT], ps[0:1, 2304:2304 + TT]),
             waits=[(tag + "smm", p.val(tag + "smm"))], inc=tag + "sm")
        p.op("vector", lambda v: v.reciprocal(f32s[0:1, R_SM:R_SM + TT], f32s[0:1, R_SM:R_SM + TT]),
             inc=tag + "sm")
        p.op("tensor", lambda t: t.matmul(ps[0:8, 2304:2304 + TT], r_ones1f[0:1, 0:8], f32s[0:1, R_SM:R_SM + TT],
                                          start=True, stop=True),
             waits=[(tag + "sm", p.val(tag + "sm"))], inc=tag + "smm")
        sc_bf = SMB[0:8, 14976:14976 + TT]
        p.op("vector", lambda v: v.scalar_tensor_tensor(
            sc_bf, ps[0:8, 2304:2304 + TT], 8.0, SMF[0:8, 0:TT], ALU.mult, ALU.mult),
            waits=[(tag + "smm", p.val(tag + "smm"))], inc=tag + "sm")
        smv = p.val(tag + "sm")
        p.op("tensor", lambda t: t.matmul(ps[:, 512:512 + TT], r_e8[:, :], sc_bf, start=True, stop=True),
             waits=[(tag + "sm", smv)], inc=tag + "sb")
        sbv = p.val(tag + "sb")
        low_sb = t_ln[:, 0:TT]
        low_s = t_ln[:, TT:2 * TT]
        p.op("scalar", lambda s: s.activation(low_sb, ps[:, 0:TT], AF.Copy),
             waits=[(tag + "lo", lov)], inc=tag + "loev")
        p.op("vector", lambda v: v.tensor_tensor(low_s, low_sb, ps[:, 512:512 + TT], ALU.mult),
             waits=[(tag + "sb", sbv), (tag + "loev", p.val(tag + "loev"))], inc=tag + "ls")
        lsv = p.val(tag + "ls")
        for kc in range(NKC):
            bank = 1024 + (kc % 2) * 512
            p.op("tensor", lambda t, kc=kc, bank=bank: t.matmul(
                ps[:, bank:bank + TT], bt_t[:, kc * 128:(kc + 1) * 128], low_s,
                start=True, stop=True),
                waits=[(tag + "ls", lsv)] + ([(tag + "oev", p.val(tag + "oev") - 1)] if kc >= 2 else []),
                inc=tag + "o")
            out_writer(kc, ps[:, bank:bank + TT], [(tag + "o", p.val(tag + "o"))])
        return p.val(tag + "oev")

    hdump = []
    for ti in range(NT1):
        emit_cast()
        if ti >= 4:
            emit_cast()
        for _ in range(4):
            emit_build_step()
        xw = [("p2ar2", p.val("p2ar2"))]
        if ti == 0:
            xw += [("p0ev", P0EV), ("fw1ev", FFNW1)]
        for pc in range(4):
            p.dma("gpsimd",
                  v3(t_x[:, pc * 2048:(pc + 1) * 2048], 8, TT),
                  xT.ap().rearrange("(kc q) t -> q kc t", q=128)[:, pc * 8:(pc + 1) * 8, ti * TT:(ti + 1) * TT],
                  waits=xw, inc="f1xld")
        xv = p.val("f1xld")
        rowv = ln_stats("f1", t_x[:, :], [("f1xld", xv), ("p2ast2", p.val("p2ast2"))])
        p.op("tensor", lambda t: t.matmul(ps[:, 3072:3072 + TT], r_ones1f[0:1, 0:128],
                                          f32s[0:1, R_M:R_M + TT], start=True, stop=True),
             waits=[("f1r", rowv), ("f1ln", p.val("f1ln")), ("p2aln2", p.val("p2aln2"))], inc="f1b")
        p.op("tensor", lambda t: t.matmul(ps[:, 3328:3328 + TT], r_ones1f[0:1, 0:128],
                                          f32s[0:1, R_RS:R_RS + TT], start=True, stop=True),
             inc="f1b")
        bv = p.val("f1b")
        p.op("vector", lambda v: v.tensor_tensor(
            v3(t_ln[:, :], 32, TT), v3(t_x[:, :], 32, TT),
            rowb(ps[:, 3072:3072 + TT], 32, TT), ALU.subtract),
            waits=[("f1b", bv), ("f1sq", p.val("f1sq"))], inc="f1ln")
        p.op("vector", lambda v: v.tensor_tensor(
            v3(t_ln[:, :], 32, TT), v3(t_ln[:, :], 32, TT),
            rowb(ps[:, 3328:3328 + TT], 32, TT), ALU.mult),
            waits=[("f1ln", p.val("f1ln"))], inc="f1ln")
        lnv = p.val("f1ln")

        def x1_writer(kc, ps_ap, waits):
            p.op("vector", lambda v, kc=kc, ps_ap=ps_ap: v.tensor_tensor(
                t_x[:, kc * TT:(kc + 1) * TT], ps_ap, t_x[:, kc * TT:(kc + 1) * TT], ALU.add),
                waits=waits, inc="f1oev")

        f1ov = ffn_tile("f1", ti, lnv, t_x, x1_writer)
        if dodbg["x1"]:
            for pc in range(4):
                p.dma("sync", dodbg["x1"].ap().rearrange("(kc q) t -> q kc t", q=128)[:, pc * 8:(pc + 1) * 8, ti * TT:(ti + 1) * TT],
                      v3(t_x[:, pc * 2048:(pc + 1) * 2048], 8, TT),
                      waits=[("f1oev", f1ov)], inc="dbgst")
        rowv = ln_stats("p2a", t_x[:, :], [("f1oev", f1ov)])
        r = f32s[0:1, :]
        SCRQ = f32b[0:1, 0:TT]
        p.op("vector", lambda v: v.tensor_scalar_add(SCRQ, r[:, R_SQ:R_SQ + TT], EPS),
             waits=[("p2ar", rowv), ("p3dt", p.val("p3dt"))], inc="p2ar")
        p.op("vector", lambda v: v.reciprocal(r[:, R_SM:R_SM + TT], SCRQ),
             inc="p2ar")
        p.op("vector", lambda v: v.tensor_tensor(SCRQ, r[:, R_SM:R_SM + TT], r[:, R_SQ:R_SQ + TT], ALU.mult),
             waits=[("p2ar", p.val("p2ar"))], inc="p2ar")
        p.op("vector", lambda v: v.tensor_scalar_add(r[:, R_SM:R_SM + TT], SCRQ, EPS),
             waits=[("p2ar", p.val("p2ar"))], inc="p2ar")
        p.op("scalar", lambda s: s.activation(SCRQ, r[:, R_SM:R_SM + TT], AF.Ln),
             waits=[("p2ar", p.val("p2ar"))], inc="p2ar")
        p.op("scalar", lambda s: s.activation(r[:, R_SM:R_SM + TT], SCRQ, AF.Exp, scale=-0.5),
             inc="p2ar")
        p.op("vector", lambda v: v.tensor_tensor(r[:, R_SM:R_SM + TT], r[:, R_SM:R_SM + TT], r[:, R_RS:R_RS + TT], ALU.mult),
             waits=[("p2ar", p.val("p2ar"))], inc="p2ar")
        rowv = p.val("p2ar")
        p.op("tensor", lambda t: t.matmul(ps[:, 3072:3072 + TT], r_ones1f[0:1, 0:128],
                                          r[:, R_M:R_M + TT], start=True, stop=True),
             waits=[("p2ar", rowv), ("p2au", p.val("p2au")), ("p2aln2", p.val("p2aln2"))],
             inc="p2ab")
        p.op("tensor", lambda t: t.matmul(ps[:, 3328:3328 + TT], r_ones1f[0:1, 0:128],
                                          r[:, R_RS:R_RS + TT], start=True, stop=True),
             inc="p2ab")
        p.op("tensor", lambda t: t.matmul(ps[:, 3584:3584 + TT], r_ones1f[0:1, 0:128],
                                          r[:, R_SM:R_SM + TT], start=True, stop=True),
             waits=[("p2ah", p.val("p2ah"))], inc="p2ab")
        bv = p.val("p2ab")
        p.op("vector", lambda v: v.tensor_tensor(
            v3(t_ln[:, :], 32, TT), v3(t_x[:, :], 32, TT),
            rowb(ps[:, 3072:3072 + TT], 32, TT), ALU.subtract),
            waits=[("p2ab", bv), ("p2asq", p.val("p2asq"))], inc="p2au")
        uv = p.val("p2au")
        hview = v3(h_t, 32, 2048)[:, :, ti * TT:(ti + 1) * TT]
        p.op("vector", lambda v, hview=hview: v.tensor_tensor(
            hview, v3(t_ln[:, :], 32, TT), rowb(ps[:, 3584:3584 + TT], 32, TT), ALU.mult),
            waits=[("p2au", uv)], inc="p2ah")
        for kc in range(NKC):
            p.op("vector", lambda v, kc=kc, ti=ti: v.tensor_scalar_mul(
                h_t[:, kc * 2048 + ti * TT: kc * 2048 + (ti + 1) * TT],
                h_t[:, kc * 2048 + ti * TT: kc * 2048 + (ti + 1) * TT],
                r_normwT[:, kc:kc + 1]),
                waits=[("p2ah", p.val("p2ah"))] if kc == 0 else (), inc="p2ahn")
        hnv = p.val("p2ahn")
        p.op("vector", lambda v: v.tensor_tensor(
            v3(t_ln[:, :], 32, TT), v3(t_ln[:, :], 32, TT),
            rowb(ps[:, 3328:3328 + TT], 32, TT), ALU.mult),
            waits=[("p2ah", p.val("p2ah"))], inc="p2aln2")
        p.op("vector", lambda v: v.tensor_tensor(
            v3(t_ln[:, :], 32, TT), v3(t_ln[:, :], 32, TT), v3(t_x[:, :], 32, TT), ALU.add),
            waits=[("p2aln2", p.val("p2aln2"))], inc="p2ar2")
        r2v = p.val("p2ar2")
        for pc in range(4):
            p.dma("sync", r2_d.ap().rearrange("(kc q) t -> q kc t", q=128)[:, pc * 8:(pc + 1) * 8, ti * TT:(ti + 1) * TT],
                  v3(t_ln[:, pc * 2048:(pc + 1) * 2048], 8, TT),
                  waits=[("p2ar2", r2v)], inc="p2ast2")
        if dodbg["r2"]:
            for pc in range(4):
                p.dma("sync", dodbg["r2"].ap().rearrange("(kc q) t -> q kc t", q=128)[:, pc * 8:(pc + 1) * 8, ti * TT:(ti + 1) * TT],
                      v3(t_ln[:, pc * 2048:(pc + 1) * 2048], 8, TT),
                      waits=[("p2ar2", r2v)], inc="dbgst")
    HREADY = p.val("p2ahn")
    if dodbg["h"]:
        p.dma("sync", dodbg["h"].ap().rearrange("(kc q) t -> q kc t", q=128),
              v3(h_t[:, :], 32, 2048), waits=[("p2ahn", HREADY)], inc="dbgst")

    if "p2b" in PHASES:
        ob_order = list(range(OB_HBC0, NOB)) + list(range(0, OB_HBC0))
        t_pre0 = slab2[:, 0:2051]
        t_pre1 = slab2[:, 2051:4102]
        t_acc = slab2[:, 4102:6150]
        t_st0 = slab2[:, 6150:8198]
        t_st1 = slab2[:, 8198:10246]
        wbufs = (wbuf0, wbuf1)
        p2b_store_done = {}
        p.op("vector", lambda v: v.memset(t_pre0[:, 0:3], 0.0), waits=[("p2ast2", p.val("p2ast2"))], inc="cpad")
        p.op("vector", lambda v: v.memset(t_pre1[:, 0:3], 0.0), inc="cpad")
        CPAD = p.val("cpad")
        for i, ob in enumerate(ob_order):
            par = i % 2
            wb = wbufs[par]
            is_hbc = OB_HBC0 <= ob < OB_HBC0 + CONVD // 128
            is_dt = ob == NOB - 1
            wlsem = "p2bwlA" if par == 0 else "p2bwlB"
            for _ in range(2):
                emit_build_step()
            p.dma("sync", v3(wb, 32, 128), wpt_blk.ap()[ob],
                  waits=[("wblk", blk_ready[ob]), ("p2bmm", max(0, p.val("p2bmm") - 1))],
                  inc=wlsem)
            wlv = (wlsem, p.val(wlsem))
            for k in range(NKC):
                for t4 in range(4):
                    w8 = []
                    if k == 0 and t4 == 0:
                        w8 = [wlv, ("p2ahn", HREADY),
                              ("p2bev", max(0, p.val("p2bev") - 4))]
                    p.op("tensor", lambda t, k=k, t4=t4, par=par, wb=wb: t.matmul(
                        ps[:, par * 2048 + t4 * 512: par * 2048 + t4 * 512 + 512],
                        wb[:, k * 128:(k + 1) * 128],
                        h_t[:, k * 2048 + t4 * 512: k * 2048 + (t4 + 1) * 512],
                        start=(k == 0), stop=(k == NKC - 1)),
                        waits=w8,
                        inc=("p2bmm" if (k == NKC - 1 and t4 == 3) else None))
            mmv = p.val("p2bmm")
            dest = (t_pre0 if par == 0 else t_pre1) if is_hbc else (t_st0 if par == 0 else t_st1)
            off = 3 if is_hbc else 0
            guard = ("p2bcv", p.val("p2bcv")) if is_hbc else ("p2bst", max(0, p.val("p2bst") - 16))
            for t4 in range(4):
                p.op("scalar", lambda s, t4=t4, par=par, dest=dest, off=off: s.activation(
                    dest[:, off + t4 * 512: off + (t4 + 1) * 512],
                    ps[:, par * 2048 + t4 * 512: par * 2048 + t4 * 512 + 512], AF.Copy),
                    waits=[("p2bmm", mmv), guard, ("cpad", CPAD)] if t4 == 0 else [("p2bmm", mmv)],
                    inc="p2bev")
            evv = p.val("p2bev")
            if is_hbc:
                cb = ob - OB_HBC0
                pre = t_pre0 if par == 0 else t_pre1
                st = t_st0 if par == 0 else t_st1
                p.op("vector", lambda v, pre=pre, cb=cb: v.tensor_scalar_mul(
                    t_acc[:, 0:2048], pre[:, 0:2048], r_convw[:, 4 * cb:4 * cb + 1]),
                    waits=[("p2bev", evv), ("p2bsl", p.val("p2bsl"))], inc="p2bcv")
                for tap in range(1, 4):
                    p.op("vector", lambda v, pre=pre, cb=cb, tap=tap: v.scalar_tensor_tensor(
                        t_acc[:, 0:2048], pre[:, tap:2048 + tap], r_convw[:, 4 * cb + tap:4 * cb + tap + 1],
                        t_acc[:, 0:2048], ALU.mult, ALU.add),
                        waits=[("p2bcv", p.val("p2bcv"))], inc="p2bcv")
                cvv = p.val("p2bcv")
                p.op("scalar", lambda s, st=st, cb=cb: s.activation(
                    st[:, 0:2048], t_acc[:, 0:2048], AF.Silu, bias=r_convb[:, cb:cb + 1]),
                    waits=[("p2bcv", cvv), ("p2bst", max(0, p.val("p2bst") - 16))], inc="p2bsl")
                slv = p.val("p2bsl")
                p.dma("gpsimd", proj_d.ap()[ob * 128:(ob + 1) * 128, :], st[:, 0:2048],
                      waits=[("p2bsl", slv)], inc="p2bst")
            else:
                st = t_st0 if par == 0 else t_st1
                p.dma("gpsimd", proj_d.ap()[ob * 128:(ob + 1) * 128, :], st[:, 0:2048],
                      waits=[("p2bev", evv)], inc="p2bst")
            p2b_store_done[ob] = p.val("p2bst")
        while _build_jobs:
            emit_build_step()
        PST_HBC = p2b_store_done[NOB - 1]
        PST_ALL = p.val("p2bst")
        if dodbg["proj"]:
            p.dma("sync", dodbg["proj"].ap()[:, :], proj_d.ap()[:, :],
                  waits=[("p2bst", PST_ALL)], inc="dbgst")

    if "p3" in PHASES:
        P2BMM = p.val("p2bmm")
        tileT = (slab1[:, 0:10368], slab1[:, 10368:20736])
        pstate = slab1[:, 20736:28928]
        hc_t = slab1[:, 28928:37120]
        hd2_t = slab1[:, 37120:45312]
        ytc_t = slab1[:, 45312:53504]
        ytmpA = slab1[:, 53504:54528]
        ytmpB = slab1[:, 54528:55552]
        ptmp_t = slab1[:, 55552:56576]
        ghat_t = slab1[:, 56576:57600]
        mhat_t = slab2[:, 0:16384]
        d0sb = (slab3[:, 0:2048], slab3[:, 2048:4096])
        cgf_t = slab3[:, 4096:5120]
        row_t = slab3[:, 5120:7168]
        sm_dtsp = slab3[:, 7168:7296]
        sm_w2 = slab3[:, 7296:7424]
        sm_decay = slab3[:, 7424:7552]
        sm_eacF = slab3[:, 7552:7680]
        sm_lam = slab3[:, 7680:7808]
        sm_eacs = slab3[:, 7808:7936]
        bgf_t = slab3[:, 7936:8192]

        p.op("vector", lambda v: v.memset(pstate[:, :], 0.0),
             waits=[("p2bmm", P2BMM)], inc="p3pupd", inc_n=8)
        for c in range(NCH):
            tl = tileT[c % 2]
            tlw = [("p2bst", PST_HBC), ("p2bmm", P2BMM),
                   ("p3ev2", max(0, (c - 1) * 8)), ("p3mmS", max(0, (c - 1) * 8)),
                   ("p3hdp", max(0, c - 1)),
                   ("p3dt", max(0, p.val("p3dt") - 7))]
            for piece in range(6):
                r0 = 8192 + piece * 1728
                p.dma("sync", tl[:, piece * 1728:(piece + 1) * 1728],
                      proj_d.ap()[r0:r0 + 1728, c * 128:(c + 1) * 128],
                      waits=tlw, inc="p3tl", transpose=True)
            tlv = p.val("p3tl")
            dtT = tl[:, 10240:10368]
            s1 = f32b[:, 0:128]
            scr = f32b[:, 128:256]
            acs_sb = f32s[:, 256:384]
            mx = f32s[:, 128:256]
            p.op("vector", lambda v, dtT=dtT: v.tensor_tensor(s1, dtT, r_biasbc[:, :], ALU.add),
                 waits=[("p3tl", tlv), ("p3acW", p.val("p3acW"))], inc="p3dt")
            p.op("scalar", lambda s: s.activation(scr, s1, AF.Abs),
                 waits=[("p3dt", p.val("p3dt")), ("p3mmT", p.val("p3mmT")),
                        ("p3mmA", p.val("p3mmA"))], inc="p3dta")
            p.op("scalar", lambda s: s.activation(scr, scr, AF.Exp, scale=-1.0), inc="p3dta")
            p.op("scalar", lambda s: s.activation(scr, scr, AF.Ln, bias=1.0), inc="p3dta")
            p.op("vector", lambda v: v.tensor_scalar_max(mx, s1, 0.0),
                 waits=[("p3dt", p.val("p3dt"))], inc="p3dt")
            p.op("vector", lambda v: v.tensor_tensor(s1, scr, mx, ALU.add),
                 waits=[("p3dta", p.val("p3dta")), ("p3dt", p.val("p3dt"))], inc="p3dt")
            p.op("vector", lambda v: v.tensor_copy(sm_dtsp[:, :], s1),
                 waits=[("p3dt", p.val("p3dt")), ("p3hc", p.val("p3hc")),
                        ("p3w2", p.val("p3w2"))], inc="p3dt")
            p.op("vector", lambda v: v.tensor_tensor(scr, s1, r_abc[:, :], ALU.mult),
                 waits=[("p3dt", p.val("p3dt"))], inc="p3acW")
            acv = p.val("p3acW")
            p.op("tensor", lambda t: t.matmul(ps[:, 3072:3200], r_l1[:, :], scr, start=True, stop=True),
                 waits=[("p3acW", acv), ("p3acs", p.val("p3acs")),
                        ("p3eac", p.val("p3eac")),
                        ("p3bc", p.val("p3bc"))] +
                       ([("p2bev", p.val("p2bev"))] if c == 0 else []),
                 inc="p3mmA")
            mmav = p.val("p3mmA")
            p.op("scalar", lambda s: s.activation(sm_eacs[:, :], ps[:, 3072:3200], AF.Exp),
                 waits=[("p3mmA", mmav), ("p3ev1", p.val("p3ev1"))], inc="p3acs")
            p.op("scalar", lambda s: s.activation(acs_sb, ps[:, 3072:3200], AF.Copy),
                 waits=[("p3mmA", mmav), ("p3mmL", p.val("p3mmL"))], inc="p3acs")
            acsv = p.val("p3acs")
            p.op("tensor", lambda t: t.matmul(ps[0:1, 3584:3712], r_e127[:, :], acs_sb, start=True, stop=True),
                 waits=[("p3acs", acsv), ("p3lam", p.val("p3lam")),
                        ("p3gh", p.val("p3gh"))], inc="p3mmL")
            p.op("scalar", lambda s: s.activation(f32s[0:1, 0:128], ps[0:1, 3584:3712], AF.Exp),
                 waits=[("p3mmL", p.val("p3mmL"))], inc="p3lam")
            p.op("tensor", lambda t: t.matmul(ps[:, 3584:3712], r_ones1f[0:1, 0:128], f32s[0:1, 0:128],
                                              start=True, stop=True),
                 waits=[("p3lam", p.val("p3lam"))], inc="p3mmL")
            p.op("vector", lambda v: v.tensor_copy(sm_lam[:, :], ps[:, 3584:3712]),
                 waits=[("p3mmL", p.val("p3mmL")), ("p3pt", p.val("p3pt"))], inc="p3lam")
            lamv = p.val("p3lam")
            p.op("tensor", lambda t: t.transpose(ps[:, 3200:3328], scr, r_idf[:, :]),
                 waits=[("p3acW", acv), ("p3eac", p.val("p3eac")),
                        ("p3acs", acsv),
                        ("p3bc", p.val("p3bc"))], inc="p3mmT")
            p.op("scalar", lambda s: s.activation(sm_eacF[:, :], ps[:, 3200:3328], AF.Exp),
                 waits=[("p3mmT", p.val("p3mmT")), ("p3row", p.val("p3row"))], inc="p3eac")
            p.op("vector", lambda v: v.memset(sm_eacF[:, 0:1], 0.0),
                 waits=[("p3eac", p.val("p3eac"))], inc="p3eac")
            eacv = p.val("p3eac")
            for g in range(NG):
                bank = 3328 if g % 2 == 0 else 3840
                p.op("tensor", lambda t, g=g, bank=bank, tl=tl: t.transpose(
                    psbf(bank, 256)[:, 0:128], tl[:, 8192 + g * 128:8192 + (g + 1) * 128], r_idbf[:, :]),
                    waits=[("p3tl", tlv), ("p3bc", p.val("p3bc")),
                           ("p3gh", p.val("p3gh")), ("p3lam", p.val("p3lam")),
                           ("p3acs", acsv), ("p3eac", eacv)], inc="p3mmT")
                p.op("vector", lambda v, g=g, bank=bank: v.tensor_copy(
                    bgf_t[:, (g % 2) * 128:(g % 2) * 128 + 128], psbf(bank, 256)[:, 0:128]),
                    waits=[("p3mmT", p.val("p3mmT"))], inc="p3bc")
                p.op("tensor", lambda t, g=g, bank=bank, tl=tl: t.transpose(
                    psbf(bank + 128, 256)[:, 0:128], tl[:, 9216 + g * 128:9216 + (g + 1) * 128], r_idbf[:, :]),
                    waits=[("p3bc", p.val("p3bc"))], inc="p3mmT")
                p.op("vector", lambda v, g=g, bank=bank: v.tensor_copy(
                    cgf_t[:, g * 128:(g + 1) * 128], psbf(bank + 128, 256)[:, 0:128]),
                    waits=[("p3mmT", p.val("p3mmT"))], inc="p3bc")
                p.op("tensor", lambda t, g=g: t.matmul(
                    ps[:, 3712:3840], bgf_t[:, (g % 2) * 128:(g % 2) * 128 + 128],
                    cgf_t[:, g * 128:(g + 1) * 128], start=True, stop=True),
                    waits=[("p3bc", p.val("p3bc")), ("p3gh", p.val("p3gh")),
                           ("p3lam", p.val("p3lam"))], inc="p3mmG")
                p.op("vector", lambda v, g=g: v.tensor_tensor(
                    ghat_t[:, g * 128:(g + 1) * 128], ps[:, 3712:3840], r_u[:, :], ALU.mult),
                    waits=[("p3mmG", p.val("p3mmG")), ("p3mh", max(0, c))], inc="p3gh")
            ghv = p.val("p3gh")
            p.op("gpsimd", lambda g, tl=tl: g.tensor_tensor(
                v3(hc_t, NH, PH), v3(tl[:, 0:8192], NH, PH), colb(sm_dtsp[:, :], NH, PH), ALU.mult),
                waits=[("p3dt", p.val("p3dt")), ("p3mmYd", p.val("p3mmYd"))], inc="p3hc")
            for e in range(8):
                p.dma("sync", row_t[0:1, :].rearrange("q (a b) -> q a b", a=16),
                      sm_eacF[e * 16:(e + 1) * 16, :],
                      waits=[("p3eac", eacv), ("p3mmD", p.val("p3mmD"))], inc="p3row")
                rv = p.val("p3row")
                for hh in range(2):
                    for q in range(2):
                        w8 = [("p3row", rv), ("p3d0", p.val("p3d0"))] if q == 0 else []
                        p.op("tensor", lambda t, hh=hh, q=q: t.matmul(
                            ps[:, 2048 + q * 512:2048 + (q + 1) * 512], r_ones1b[:, :],
                            row_t[0:1, hh * 1024 + q * 512:hh * 1024 + (q + 1) * 512],
                            start=True, stop=True),
                            waits=w8, inc=None if q == 0 else "p3mmD")
                    dmv = p.val("p3mmD")
                    p.op("scalar", lambda s, e=e, hh=hh: s.activation(
                        d0sb[e % 2][:, hh * 1024:(hh + 1) * 1024], ps[:, 2048:3072], AF.Copy),
                        waits=[("p3mmD", dmv), ("p3scan", max(0, p.val("p3scan") - 1))], inc="p3d0")
                d0v = p.val("p3d0")
                p.op("vector", lambda v, e=e: v.tensor_tensor_scan(
                    mhat_t[:, e * 2048:(e + 1) * 2048], d0sb[e % 2][:, :], r_irep[:, :],
                    0.0, ALU.mult, ALU.add),
                    waits=[("p3d0", d0v), ("p3mmYd", max(0, 8 * (c - 1) + e + 1))], inc="p3scan")
                scv = p.val("p3scan")
            p.op("vector", lambda v: v.tensor_copy(
                sm_decay[:, :], v3(mhat_t[:, :], 128, 128)[:, :, 127]),
                waits=[("p3scan", p.val("p3scan"))], inc="p3dec")
            p.op("gpsimd", lambda g: g.tensor_tensor(
                mhat_t[:, :].rearrange("q (g r i) -> q g r i", g=8, i=128),
                mhat_t[:, :].rearrange("q (g r i) -> q g r i", g=8, i=128),
                ghat_t[:, :].rearrange("q (g i) -> q g i", g=8).unsqueeze(2).broadcast_to([128, 8, 16, 128]),
                ALU.mult),
                waits=[("p3scan", p.val("p3scan")), ("p3dec", p.val("p3dec")),
                       ("p3gh", ghv)], inc="p3mh")
            p.op("vector", lambda v: v.tensor_tensor(sm_w2[:, :], sm_dtsp[:, :], sm_decay[:, :], ALU.mult),
                 waits=[("p3dec", p.val("p3dec")), ("p3hd", p.val("p3hd"))], inc="p3w2")
            p.op("gpsimd", lambda g, tl=tl: g.tensor_tensor(
                v3(hd2_t, NH, PH), v3(tl[:, 0:8192], NH, PH), colb(sm_w2[:, :], NH, PH), ALU.mult),
                waits=[("p3w2", p.val("p3w2")), ("p3mmS", p.val("p3mmS"))], inc="p3hd")
            hdv = p.val("p3hd")
            p.op("gpsimd", lambda g, tl=tl: g.tensor_tensor(
                v3(tl[:, 0:8192], NH, PH), v3(tl[:, 0:8192], NH, PH),
                colb(r_dpbc[:, :], NH, PH), ALU.mult),
                waits=[("p3hd", hdv), ("p3hc", p.val("p3hc")),
                       ("p3mmYd", p.val("p3mmYd"))], inc="p3hdp")
            hdpv = p.val("p3hdp")
            for e in range(8):
                ytslice = ytc_t[:, e * 1024:(e + 1) * 1024]
                for q in range(2):
                    w8 = []
                    if q == 0:
                        w8 += [("p3bc", p.val("p3bc")), ("p3pupd", p.val("p3pupd"))]
                    p.op("tensor", lambda t, e=e, q=q: t.matmul(
                        ps[:, q * 512:(q + 1) * 512], cgf_t[:, e * 128:(e + 1) * 128],
                        pstate[:, e * 1024 + q * 512: e * 1024 + (q + 1) * 512],
                        start=True, stop=True),
                        waits=w8, inc=None if q == 0 else "p3mmY")
                ymv = p.val("p3mmY")
                p.op("vector", lambda v, e=e: v.tensor_tensor(
                    v3(ytmpA, 16, 64), v3(ps[:, 0:1024], 16, 64),
                    colb(sm_eacs[:, e * 16:(e + 1) * 16], 16, 64), ALU.mult),
                    waits=[("p3mmY", ymv), ("p3acs", acsv), ("p3yad", max(0, p.val("p3yad") - 1))],
                    inc="p3ev1")
                e1v = p.val("p3ev1")
                for hr in range(16):
                    w8 = []
                    if hr == 0:
                        w8 = [("p3mh", p.val("p3mh")), ("p3hc", p.val("p3hc")),
                              ("p3ev2", p.val("p3ev2"))]
                    p.op("tensor", lambda t, e=e, hr=hr: t.matmul(
                        ps[:, 1024 + hr * 64:1024 + (hr + 1) * 64],
                        mhat_t[:, (e * 16 + hr) * 128:(e * 16 + hr + 1) * 128],
                        hc_t[:, (e * 16 + hr) * 64:(e * 16 + hr + 1) * 64],
                        start=(hr % 8 == 0), stop=False),
                        waits=w8, inc=None)
                for q in range(2):
                    p.op("tensor", lambda t, e=e, q=q, tl=tl: t.matmul(
                        ps[:, 1024 + q * 512:1024 + (q + 1) * 512], r_idbf[:, :],
                        tl[:, e * 1024 + q * 512:e * 1024 + (q + 1) * 512],
                        start=False, stop=True),
                        waits=[("p3hdp", hdpv)] if q == 0 else (),
                        inc=None if q == 0 else "p3mmYd")
                ydv = p.val("p3mmYd")
                p.op("vector", lambda v, e=e, ytslice=ytslice: v.tensor_tensor(
                    ytslice, ps[:, 1024:2048], ytmpA, ALU.add),
                    waits=[("p3mmYd", ydv), ("p3ev1", e1v), ("p3yst", p.val("p3yst"))],
                    inc="p3ev2")
                for q in range(2):
                    w8 = [("p3ev1", e1v), ("p3hd", hdv),
                          ("p3pupd", p.val("p3pupd"))] if q == 0 else []
                    p.op("tensor", lambda t, e=e, q=q, tl=tl: t.matmul(
                        ps[:, q * 512:(q + 1) * 512], tl[:, 8192 + e * 128:8192 + (e + 1) * 128],
                        hd2_t[:, e * 1024 + q * 512: e * 1024 + (q + 1) * 512],
                        start=True, stop=True),
                        waits=w8, inc=None if q == 0 else "p3mmS")
                smv = p.val("p3mmS")
                p.op("vector", lambda v, e=e: v.tensor_tensor(
                    v3(ptmp_t, 16, 64), v3(pstate[:, e * 1024:(e + 1) * 1024], 16, 64),
                    colb(sm_lam[:, e * 16:(e + 1) * 16], 16, 64), ALU.mult),
                    waits=[("p3lam", lamv), ("p3mmY", ymv)], inc="p3pt")
                ptv = p.val("p3pt")
                p.op("vector", lambda v, e=e: v.tensor_tensor(
                    pstate[:, e * 1024:(e + 1) * 1024], ptmp_t[:, 0:1024],
                    ps[:, 0:1024], ALU.add),
                    waits=[("p3mmS", smv), ("p3pt", ptv)], inc="p3pupd")
            p.dma("sync", yt_d.ap()[c * 128:(c + 1) * 128, :], ytc_t[:, :],
                  waits=[("p3ev2", p.val("p3ev2"))], inc="p3yst")
            if c == 0 and "p3c0" in dbg_t:
                dbgw = [("p3ev2", p.val("p3ev2")), ("p3pupd", p.val("p3pupd")),
                        ("p3mh", p.val("p3mh")), ("p3hd", p.val("p3hd"))]
                dd = dbg_t["p3c0"].ap()
                p.dma("sync", dd[:, 0:16384], mhat_t[:, :], waits=dbgw, inc="dbgst")
                p.dma("sync", dd[:, 16384:24576], hc_t[:, :], waits=dbgw, inc="dbgst")
                p.dma("sync", dd[:, 24576:24704], sm_dtsp[:, :], waits=dbgw, inc="dbgst")
                p.dma("sync", dd[:, 24704:24832], sm_eacF[:, :], waits=dbgw, inc="dbgst")
                p.dma("sync", dd[:, 24832:24960], sm_decay[:, :], waits=dbgw, inc="dbgst")
                p.dma("sync", dd[:, 24960:25088], sm_lam[:, :], waits=dbgw, inc="dbgst")
                p.dma("sync", dd[:, 25088:25216], sm_eacs[:, :], waits=dbgw, inc="dbgst")
                p.dma("sync", dd[:, 25216:26240], ghat_t[:, :], waits=dbgw, inc="dbgst")
                p.dma("sync", dd[:, 26240:34432], pstate[:, :], waits=dbgw, inc="dbgst")
                p.dma("sync", dbg_t["p3f"].ap()[:, 0:256], f32b[:, 0:256], waits=dbgw, inc="dbgst")
                p.barrier_wait("vector", [("dbgst", p.val("dbgst"))])
        P3YST = p.val("p3yst")
        if dodbg["yt"]:
            p.dma("sync", dodbg["yt"].ap()[:, :], yt_d.ap()[:, :],
                  waits=[("p3yst", P3YST)], inc="dbgst")

    if "p35" in PHASES:
        ytin = (slab1[:, 0:8192], slab1[:, 8192:16384])
        gt = (slab1[:, 16384:24576], slab1[:, 24576:32768])
        ygn_b = (slab1[:, 32768:40960], slab1[:, 40960:49152])
        yg_t = slab1[:, 49152:57344]
        P3DONE = [("p3yst", P3YST), ("p3pupd", p.val("p3pupd")),
                  ("p3mmS", p.val("p3mmS")), ("p3ev2", p.val("p3ev2")),
                  ("p3mh", p.val("p3mh")), ("p3hd", p.val("p3hd")),
                  ("p3hdp", p.val("p3hdp")), ("p3hc", p.val("p3hc"))]
        for c in range(NCH):
            par = c % 2
            ldw = P3DONE + [("p35yg", max(0, p.val("p35yg") - 1)), ("p2bst", PST_ALL)]
            p.dma("sync", ytin[par][:, :], yt_d.ap()[c * 128:(c + 1) * 128, :],
                  waits=ldw, inc="p35ld")
            for piece in range(4):
                p.dma("sync", gt[par][:, piece * 2048:(piece + 1) * 2048],
                      proj_d.ap()[piece * 2048:(piece + 1) * 2048, c * 128:(c + 1) * 128],
                      waits=ldw + [("p35sg", max(0, p.val("p35sg") - 1))],
                      inc="p35ld", transpose=True)
            ldv = p.val("p35ld")
            p.op("scalar", lambda s, par=par: s.activation(gt[par][:, :], gt[par][:, :], AF.Silu),
                 waits=[("p35ld", ldv)], inc="p35sg")
            sgv = p.val("p35sg")
            p.op("vector", lambda v, par=par: v.tensor_tensor(yg_t[:, :], ytin[par][:, :], gt[par][:, :], ALU.mult),
                 waits=[("p35sg", sgv), ("p35ld", ldv), ("p35gn", p.val("p35gn"))], inc="p35yg")
            ygv = p.val("p35yg")
            p.op("vector", lambda v, par=par: v.tensor_tensor(ytin[par][:, :], yg_t[:, :], yg_t[:, :], ALU.mult),
                 waits=[("p35yg", ygv)], inc="p35sq")
            sqv = p.val("p35sq")
            for gidx in range(8):
                p.op("vector", lambda v, par=par, gidx=gidx: v.reduce_sum(
                    f32b[:, gidx:gidx + 1], ytin[par][:, gidx * 1024:(gidx + 1) * 1024], axis=AX),
                    waits=[("p35sq", sqv)] if gidx == 0 else (), inc="p35rs")
            rsv = p.val("p35rs")
            p.op("vector", lambda v: v.tensor_scalar(f32b[:, 8:16], f32b[:, 0:8], 1.0 / 1024.0, EPS, ALU.mult, ALU.add),
                 waits=[("p35rs", rsv)], inc="p35r")
            p.op("scalar", lambda s: s.activation(f32b[:, 8:16], f32b[:, 8:16], AF.Ln),
                 waits=[("p35r", p.val("p35r"))], inc="p35r")
            p.op("scalar", lambda s: s.activation(f32b[:, 8:16], f32b[:, 8:16], AF.Exp, scale=-0.5),
                 inc="p35r")
            p.op("vector", lambda v: v.tensor_copy(f32b[:, 16:20].bitcast(BF16)[:, 0:8], f32b[:, 8:16]),
                 waits=[("p35r", p.val("p35r"))], inc="p35r")
            rv8 = p.val("p35r")
            rstd_bf = f32b[:, 16:20].bitcast(BF16)[:, 0:8]
            p.op("vector", lambda v, par=par: v.tensor_tensor(
                v3(ygn_b[par][:, :], 8, 1024), v3(yg_t[:, :], 8, 1024),
                colb(rstd_bf, 8, 1024), ALU.mult),
                waits=[("p35r", rv8), ("p35st", max(0, p.val("p35st") - 16))], inc="p35gn")
            gnv = p.val("p35gn")
            p.dma("sync", ygn_d.ap()[c * 128:(c + 1) * 128, :], ygn_b[par][:, :],
                  waits=[("p35gn", gnv)], inc="p35st")
        P35ST = p.val("p35st")
        if dodbg["ygn"]:
            p.dma("sync", dodbg["ygn"].ap()[:, :], ygn_d.ap()[:, :],
                  waits=[("p35st", P35ST)], inc="dbgst")

    FFNW2 = build_ffn_tiles("fw2")

    if "p4" in PHASES:
        ygnF = slab1[:, 0:32768]
        w2b = (slab1[:, 32768:40960], slab1[:, 40960:49152])
        r2t = (slab1[:, 49152:49664], slab1[:, 49664:50176])
        x2t = (slab1[:, 50176:50688], slab1[:, 50688:51200])
        for t4 in range(4):
            for kc in range(NIC):
                w8y = [("p35st", P35ST), ("p35gn", p.val("p35gn"))]
                if kc == 0 and t4 > 0:
                    w8y.append(("p4ev", p.val("p4ev")))
                p.dma("sync", ygnF[:, kc * 512:(kc + 1) * 512],
                      ygn_d.ap()[t4 * 512:(t4 + 1) * 512, kc * 128:(kc + 1) * 128],
                      waits=w8y, inc="p4yl", transpose=True)
            ylv = p.val("p4yl")
            for mb in range(NMB):
                par = mb % 2
                p.dma("gpsimd", v3(w2b[par], NIC, 128), w2t_blk.ap()[mb],
                      waits=[("w2blk", w2blk_ready[mb]), ("p4mm", max(0, p.val("p4mm") - 1)),
                             ("p35st", P35ST), ("p35gn", p.val("p35gn"))],
                      inc="p4wl")
                wlv = p.val("p4wl")
                p.dma("gpsimd", r2t[par], r2_d.ap()[mb * 128:(mb + 1) * 128, t4 * 512:(t4 + 1) * 512],
                      waits=[("p2ast2", p.val("p2ast2")), ("p4ev", max(0, p.val("p4ev") - 1)),
                             ("p35st", P35ST), ("p35gn", p.val("p35gn"))],
                      inc="p4rl")
                rlv = p.val("p4rl")
                for kc in range(NIC):
                    w8 = []
                    if kc == 0:
                        w8 = [("p4yl", ylv), ("p4wl", wlv),
                              ("p4ev", max(0, p.val("p4ev") - 1))]
                        if t4 == 0 and mb == 0:
                            w8 += [("p3pupd", p.val("p3pupd")),
                                   ("p3ev2", p.val("p3ev2")),
                                   ("p3ev1", p.val("p3ev1"))]
                    p.op("tensor", lambda t, kc=kc, par=par: t.matmul(
                        ps[:, par * 512:(par + 1) * 512], w2b[par][:, kc * 128:(kc + 1) * 128],
                        ygnF[:, kc * 512:(kc + 1) * 512],
                        start=(kc == 0), stop=(kc == NIC - 1)),
                        waits=w8, inc=None if kc < NIC - 1 else "p4mm")
                mmv = p.val("p4mm")
                p.op("vector", lambda v, par=par: v.tensor_tensor(
                    x2t[par], ps[:, par * 512:(par + 1) * 512], r2t[par], ALU.add),
                    waits=[("p4mm", mmv), ("p4rl", rlv), ("p4st", max(0, p.val("p4st") - 16))],
                    inc="p4ev")
                evv = p.val("p4ev")
                p.dma("sync", x2_d.ap()[mb * 128:(mb + 1) * 128, t4 * 512:(t4 + 1) * 512],
                      x2t[par], waits=[("p4ev", evv)], inc="p4st")
        P4ST = p.val("p4st")
        if dodbg["x2"]:
            p.dma("sync", dodbg["x2"].ap()[:, :], x2_d.ap()[:, :],
                  waits=[("p4st", P4ST)], inc="dbgst")

    if "p5" in PHASES:
        for ti in range(NT1):
            xw = [("p4st", P4ST), ("fw2ev", FFNW2), ("f5oev", p.val("f5oev")),
                  ("p3mmYd", p.val("p3mmYd")), ("p35gn", p.val("p35gn"))]
            for pc in range(4):
                p.dma("sync",
                      v3(t_x[:, pc * 2048:(pc + 1) * 2048], 8, TT),
                      x2_d.ap().rearrange("(kc q) t -> q kc t", q=128)[:, pc * 8:(pc + 1) * 8, ti * TT:(ti + 1) * TT],
                      waits=xw, inc="f5xld")
            xv = p.val("f5xld")
            rowv = ln_stats("f5", t_x[:, :], [("f5xld", xv), ("f5ost", max(0, p.val("f5ost") - 32))])
            p.op("tensor", lambda t: t.matmul(ps[:, 3072:3072 + TT], r_ones1f[0:1, 0:128],
                                              f32s[0:1, R_M:R_M + TT], start=True, stop=True),
                 waits=[("f5r", rowv), ("f5ln", p.val("f5ln"))], inc="f5b")
            p.op("tensor", lambda t: t.matmul(ps[:, 3328:3328 + TT], r_ones1f[0:1, 0:128],
                                              f32s[0:1, R_RS:R_RS + TT], start=True, stop=True),
                 inc="f5b")
            bv = p.val("f5b")
            p.op("vector", lambda v: v.tensor_tensor(
                v3(t_ln[:, :], 32, TT), v3(t_x[:, :], 32, TT),
                rowb(ps[:, 3072:3072 + TT], 32, TT), ALU.subtract),
                waits=[("f5b", bv), ("f5sq", p.val("f5sq"))], inc="f5ln")
            p.op("vector", lambda v: v.tensor_tensor(
                v3(t_ln[:, :], 32, TT), v3(t_ln[:, :], 32, TT),
                rowb(ps[:, 3328:3328 + TT], 32, TT), ALU.mult),
                waits=[("f5ln", p.val("f5ln"))], inc="f5ln")
            lnv = p.val("f5ln")

            def out_writer(kc, ps_ap, waits, ti=ti):
                fpar = kc % 2
                fb = wbuf1.bitcast(F32)[:, fpar * 256:fpar * 256 + TT]
                p.op("vector", lambda v, kc=kc, ps_ap=ps_ap, fb=fb: v.tensor_tensor(
                    fb, ps_ap, t_x[:, kc * TT:(kc + 1) * TT], ALU.add),
                    waits=list(waits) + [("f5ost", max(0, p.val("f5ost") - 16))],
                    inc="f5oev")
                p.dma("sync", out_d.ap().rearrange("(kc q) t -> q kc t", q=128)[:, kc, ti * TT:(ti + 1) * TT],
                      fb, waits=[("f5oev", p.val("f5oev"))], inc="f5ost")

            ffn_tile("f5", ti, lnv, t_x, out_writer)

    fin = [("f5ost", p.val("f5ost"))] if p.val("f5ost") else [("p2ast2", p.val("p2ast2"))]
    if p.val("dbgst"):
        fin.append(("dbgst", p.val("dbgst")))
    p.barrier_wait("sync", fin)

    with nc.Block() as block:
        @block.tensor
        def _(e):
            for f in p.ev["tensor"]:
                f(e)

        @block.vector
        def _(e):
            for f in p.ev["vector"]:
                f(e)

        @block.scalar
        def _(e):
            for f in p.ev["scalar"]:
                f(e)

        @block.gpsimd
        def _(e):
            for f in p.ev["gpsimd"]:
                f(e)

        @block.sync
        def _(e):
            for f in p.ev["sync"]:
                f(e)

    return nc


def _prep_inputs(inp):
    w = {}
    w["wp"] = np.ascontiguousarray(inp["L0_attn_in_proj_w"])
    w["w2"] = np.ascontiguousarray(inp["L0_attn_out_proj_w"])
    w["ffa"] = np.ascontiguousarray(inp["L0_ffn_A"].reshape(128, D))
    w["ffb"] = np.ascontiguousarray(inp["L0_ffn_B"])
    w["wrr"] = np.ascontiguousarray(inp["L0_ffn_Wr"])
    w["brow"] = np.ascontiguousarray(inp["L0_ffn_br"].reshape(1, NG))
    w["normw"] = np.ascontiguousarray(inp["L0_attn_norm_w"].reshape(NKC, 128))
    w["convw"] = np.ascontiguousarray(inp["L0_attn_conv_w"])
    w["convb"] = np.ascontiguousarray(inp["L0_attn_conv_b"].reshape(CONVD // 128, 128))
    w["dtb"] = np.ascontiguousarray(inp["L0_attn_dt_bias"].reshape(1, NH))
    w["alog"] = np.ascontiguousarray(inp["L0_attn_A_log"].reshape(1, NH))
    w["dpr"] = np.ascontiguousarray(inp["L0_attn_Dp"].reshape(1, NH))
    w["gnw"] = np.ascontiguousarray(inp["L0_attn_gnorm_w"].reshape(NIC, 128))
    return w


def kernel(**inp):
    w = _prep_inputs(inp)
    x = inp["L0_attn_x"]
    nc = build_program()
    in_maps = []
    for b in range(8):
        m = dict(w)
        m["xT"] = np.ascontiguousarray(x[b].T)
        in_maps.append(m)
    res = run_bass_kernel_spmd(nc, in_maps, core_ids=list(range(8)))
    out = np.stack([r["out"].T for r in res.results], axis=0)
    return np.ascontiguousarray(out.astype(np.float32))


if __name__ == "__main__":
    nc = build_program()
    n = nc.n_instructions()
    print("instructions:", n)
